# version 21
# baseline (speedup 1.0000x reference)
import numpy as np
import ml_dtypes
from collections import defaultdict
import contextlib

import concourse.bass as bass
import concourse.mybir as mybir
from concourse.bass_utils import run_bass_kernel_spmd

DEBUG_DUMPS = False
DEBUG_TAPS2 = False
DEBUG_TAPS = False

F32 = mybir.dt.float32
BF16 = mybir.dt.bfloat16
AF = mybir.ActivationFunctionType
ALU = mybir.AluOpType
AX = mybir.AxisListType

B, S, H, II = 8, 4096, 1024, 2816
WIN = 128
NW = S // WIN
NT = NW // 2
HC = H // 128
ICN = II // 128
MAGIC = 12582912.0
EPS = 1e-5
BIGB = 1024.0
NEG = -1.0e30


class Sched:

    QSEM = {"pe": "spe", "sc": "ssc", "dv": "sdv", "pl": "spl", "sy": "ssy"}
    DSEM = {"sc": "scq", "dv": "svq", "pl": "sgq", "sy": "ssy2"}

    def __init__(self):
        self.items = defaultdict(list)
        self.evt = {}
        self.cnt = defaultdict(int)
        self.reg = {}

    def _add(self, stream, fn, waits, m, phase, mark):
        it = [fn, list(waits), m, phase]
        self.items[stream].append(it)
        if mark is not None:
            self.reg[mark] = it
        return mark

    def op(self, stream, fn, waits=(), mark=None, phase=0):
        semn = self.QSEM[stream]
        m = None
        if mark is not None:
            self.cnt[semn] += 1
            self.evt[mark] = (semn, self.cnt[semn])
            m = (semn, 1)
        return self._add(stream, fn, waits, m, phase, mark)

    def dma(self, stream, fn, waits=(), mark=None, phase=0, sem=None):
        semn = sem if sem is not None else self.DSEM[stream]
        self.cnt[semn] += 16
        if mark is not None:
            self.evt[mark] = (semn, self.cnt[semn])
        return self._add(stream, fn, waits, (semn, 16), phase, mark)

    def cc(self, fn, waits=(), mark=None, phase=0):
        self.cnt["scc"] += 1
        if mark is not None:
            self.evt[mark] = ("scc", self.cnt["scc"])
        self._add("pl", fn, waits, ("scc", 1), phase, mark)

    def push(self, mark, *extra_waits):
        self.reg[mark][1].extend(extra_waits)

    def latest(self, *keys):
        evs = [(self.evt[k], k) for k in keys]
        sems = {e[0][0] for e in evs}
        assert len(sems) == 1, f"latest() on mixed sems: {keys}"
        return max(evs)[1]

    def wait_only(self, stream, waits, phase=0):
        assert stream != "pe"
        self.items[stream].append([None, list(waits), None, phase])

    def emit(self, block, sems, phase):
        streams = {"pe": "tensor", "sc": "scalar", "dv": "vector",
                   "pl": "gpsimd", "sy": "sync"}
        for stream, attr in streams.items():
            its = [it for it in self.items[stream] if it[3] == phase]
            if not its:
                continue

            def run(e, its=its, stream=stream):
                for fn, waits, m, _ in its:
                    if fn is None:
                        for wk in waits:
                            sn, cv = self.evt[wk]
                            e.wait_ge(sems[sn], cv)
                        continue
                    best = {}
                    for wk in waits:
                        sn, cv = self.evt[wk]
                        if cv > best.get(sn, -1):
                            best[sn] = cv
                    resolved = sorted(best.items())
                    if stream == "pe":
                        assert len(resolved) <= 1, f"PE item with waits {waits}"
                    for sn, cv in resolved[1:]:
                        e.wait_ge(sems[sn], cv)
                    ins = fn(e)
                    if resolved:
                        sn, cv = resolved[0]
                        ins._wait_ge(sems[sn], cv)
                    if m is not None:
                        sn, dv_ = m
                        if sn == "scc":
                            ins.then_inc(sems[sn])
                        else:
                            ins.then_inc(sems[sn], dv_)

            getattr(block, attr)(run)


def build_program():
    nc = bass.Bass("TRN2", target_bir_lowering=False, detect_race_conditions=False)

    d_x = nc.dram_tensor("x_tm", [S, H], F32, kind="ExternalInput").ap()
    d_xT = nc.dram_tensor("x_fm", [H, S], F32, kind="ExternalInput").ap()
    d_wqT = nc.dram_tensor("wqT", [H, H], F32, kind="ExternalInput").ap()
    d_wkT = nc.dram_tensor("wkT", [H, H], F32, kind="ExternalInput").ap()
    d_wvT = nc.dram_tensor("wvT", [H, H], F32, kind="ExternalInput").ap()
    d_bq = nc.dram_tensor("bq_pc", [128, HC], F32, kind="ExternalInput").ap()
    d_bk = nc.dram_tensor("bk_pc", [128, HC], F32, kind="ExternalInput").ap()
    d_bv = nc.dram_tensor("bv_row", [1, H], F32, kind="ExternalInput").ap()
    d_gam = nc.dram_tensor("gam_row", [1, H], F32, kind="ExternalInput").ap()
    d_bet = nc.dram_tensor("bet_row", [1, H], F32, kind="ExternalInput").ap()
    d_clip = nc.dram_tensor("clip2", [1, 2], F32, kind="ExternalInput").ap()
    d_wg = nc.dram_tensor("w_gate", [II, H], F32, kind="ExternalInput").ap()
    d_wgT = nc.dram_tensor("w_gateT", [H, II], F32, kind="ExternalInput").ap()
    d_wu = nc.dram_tensor("w_up", [II, H], F32, kind="ExternalInput").ap()
    d_wuT = nc.dram_tensor("w_upT", [H, II], F32, kind="ExternalInput").ap()
    d_wd = nc.dram_tensor("w_down", [H, II], F32, kind="ExternalInput").ap()
    d_wdT = nc.dram_tensor("w_downT", [II, H], F32, kind="ExternalInput").ap()
    d_out = nc.dram_tensor("out", [S, H], F32, kind="ExternalOutput").ap()

    d_x1 = nc.dram_tensor("x1_scr", [S, H], F32, kind="Internal").ap()
    d_p = nc.dram_tensor("p_scr", [II, S], F32, kind="Internal").ap()
    d_iwg = nc.dram_tensor("iwg_scr", [H, II], BF16, kind="Internal").ap()
    d_iwu = nc.dram_tensor("iwu_scr", [H, II], BF16, kind="Internal").ap()
    d_iwd = nc.dram_tensor("iwd_scr", [II, H], BF16, kind="Internal").ap()
    d_rr = {w: nc.dram_tensor(f"rrow_{w}", [1, II if w != "d" else H], F32,
                              kind="Internal").ap() for w in ("g", "u", "d")}
    d_cc = {k: nc.dram_tensor(f"cc_{k}", [1, 2], F32, kind="Internal")
            for k in ("in1", "out1", "in2", "out2")}

    i_eyeb = nc.inline_tensor(np.eye(128, dtype=ml_dtypes.bfloat16), name="eye_bf").ap()
    i_eyef = nc.inline_tensor(np.eye(128, dtype=np.float32), name="eye_f32").ap()
    i_ones = nc.inline_tensor(np.ones((1, 128), dtype=np.float32), name="ones_f32").ap()
    i_onesb = nc.inline_tensor(np.ones((1, 128), dtype=ml_dtypes.bfloat16), name="ones_bf").ap()

    v_x = d_x.rearrange("(w p) h -> p w h", p=128)
    v_xT = d_xT.rearrange("(c p) t -> p c t", p=128)
    v_x1 = d_x1.rearrange("(w p) h -> p w h", p=128)
    v_out = d_out.rearrange("(w p) h -> p w h", p=128)
    v_p = d_p.rearrange("(c p) t -> p c t", p=128)
    v_wg = d_wg.rearrange("(c p) h -> p c h", p=128)
    v_wu = d_wu.rearrange("(c p) h -> p c h", p=128)
    v_wd = d_wd.rearrange("(c p) i -> p c i", p=128)
    v_wgT = d_wgT.rearrange("(c p) i -> p c i", p=128)
    v_wuT = d_wuT.rearrange("(c p) i -> p c i", p=128)
    v_wdT = d_wdT.rearrange("(c p) h -> p c h", p=128)
    v_iwg = d_iwg.rearrange("(c p) i -> p c i", p=128)
    v_iwu = d_iwu.rearrange("(c p) i -> p c i", p=128)
    v_iwd = d_iwd.rearrange("(c p) h -> p c h", p=128)

    sch = Sched()

    ctx = contextlib.ExitStack()
    with ctx:
        ent = ctx.enter_context

        semnames = ["spe", "ssc", "sdv", "spl", "ssy", "ssy2", "scq", "svq", "sgq",
                    "scc", "siwg", "siwu", "siwd"]
        sems = {n: ent(nc.semaphore(n)) for n in semnames}

        pb = [ent(nc.psum_tensor(f"pb{i}", [128, 512], F32)) for i in range(8)]

        gb_bc = ent(nc.sbuf_tensor("gb_bc", [128, 2 * H], BF16))
        bvb = ent(nc.sbuf_tensor("bvb", [1, H], BF16))
        onesb = ent(nc.sbuf_tensor("onesb", [1, 128], BF16))
        ones32 = ent(nc.sbuf_tensor("ones32", [1, 128], F32))
        eyeb = ent(nc.sbuf_tensor("eyeb", [128, 128], BF16))
        eyef = ent(nc.sbuf_tensor("eyef", [128, 128], F32))
        bq_sb = ent(nc.sbuf_tensor("bq_sb", [128, HC], F32))
        bk_sb = ent(nc.sbuf_tensor("bk_sb", [128, HC], F32))
        clip_sb = ent(nc.sbuf_tensor("clip_sb", [1, 2], F32))
        rs_g = ent(nc.sbuf_tensor("rs_g", [128, ICN], F32))
        rs_u = ent(nc.sbuf_tensor("rs_u", [128, ICN], F32))
        rs_d = ent(nc.sbuf_tensor("rs_d", [128, HC], F32))
        sg_all = ent(nc.sbuf_tensor("sg_all", [128, ICN], F32))
        su_all = ent(nc.sbuf_tensor("su_all", [128, ICN], F32))
        mx_all = ent(nc.sbuf_tensor("mx_all", [128, NW], F32))
        mp_all = ent(nc.sbuf_tensor("mp_all", [128, 176], F32))
        srow = ent(nc.sbuf_tensor("srow", [1, 16], F32))
        bc2 = ent(nc.sbuf_tensor("bc2", [128, 4], F32))
        mvec = ent(nc.sbuf_tensor("mvec", [128, 2], F32))
        row_sb = ent(nc.sbuf_tensor("row_sb", [ICN, 128], F32))
        rqbc = ent(nc.sbuf_tensor("rqbc", [128, II], F32))
        sdbc = ent(nc.sbuf_tensor("sdbc", [128, H], F32))
        sml = ent(nc.sbuf_tensor("sml", [128, 24], F32))
        epsv = ent(nc.sbuf_tensor("epsv", [128, 1], F32))

        def sl(par, idx):
            return sml[:, par * 12 + idx: par * 12 + idx + 1]

        ph = 0
        a = contextlib.ExitStack()
        ent2 = a.enter_context
        wq_sb = ent2(nc.sbuf_tensor("wq_sb", [128, HC, H], BF16))
        wk_sb = ent2(nc.sbuf_tensor("wk_sb", [128, HC, H], BF16))
        wv_sb = ent2(nc.sbuf_tensor("wv_sb", [128, HC, H], BF16))
        xw = [ent2(nc.sbuf_tensor(f"xw{i}", [128, 2, H], F32)) for i in range(2)]
        xTf = [ent2(nc.sbuf_tensor("xTf0", [128, HC, 256], F32))] * 2
        xbf = [ent2(nc.sbuf_tensor(f"xbf{i}", [128, HC, 256], BF16)) for i in range(2)]
        qf = ent2(nc.sbuf_tensor("qf", [128, HC, 256], BF16))
        kf = ent2(nc.sbuf_tensor("kf", [128, HC, 256], BF16))
        vsb = [ent2(nc.sbuf_tensor(f"vsb{i}", [128, 2, H], BF16)) for i in range(2)]
        ssb = ent2(nc.sbuf_tensor("ssb", [128, 2, 128], F32))
        wrk = ent2(nc.sbuf_tensor("wrk", [128, 2, 128], F32))
        smb = ent2(nc.sbuf_tensor("smb", [128, 2, 128], F32))
        mskb = ent2(nc.sbuf_tensor("mskb", [128, 2, 128], F32))
        m8 = ent2(nc.sbuf_tensor("m8", [128, 8], F32))
        atn = ent2(nc.sbuf_tensor("atn", [128, 2, 128], BF16))
        atnT = ent2(nc.sbuf_tensor("atnT", [128, 2, 128], BF16))
        osb = ent2(nc.sbuf_tensor("osb", [128, H], F32))
        rsb = ent2(nc.sbuf_tensor("rsb", [128, H], F32))
        x1w = [ent2(nc.sbuf_tensor(f"x1w{i}", [128, H], F32)) for i in range(2)]
        wtl = [ent2(nc.sbuf_tensor(f"wtl{i}", [128, II], F32)) for i in range(5)]
        iw_st = [ent2(nc.sbuf_tensor(f"iwst{i}", [128, II], BF16)) for i in range(2)]

        def dup_sy(fn, mark):
            sch.dma("sy", fn, phase=ph)
            sch.dma("sy", fn, mark=mark, phase=ph)

        def dup_pl(fn, mark, waits=()):
            sch.dma("pl", fn, waits=list(waits), mark=mark + ":a", phase=ph)
            sch.dma("pl", fn, waits=[mark + ":a"], mark=mark, phase=ph)

        dup_sy(lambda e: e.dma_start(out=bq_sb[:], in_=d_bq), "L:bq")
        dup_sy(lambda e: e.dma_start(out=bk_sb[:], in_=d_bk), "L:bk")
        dup_sy(lambda e: e.dma_start(out=clip_sb[:], in_=d_clip), "L:clip")
        dup_sy(lambda e: e.dma_start(out=eyeb[:], in_=i_eyeb), "L:eyeb")
        dup_sy(lambda e: e.dma_start(out=eyef[:], in_=i_eyef), "L:eyef")
        dup_sy(lambda e: e.dma_start(out=ones32[:], in_=i_ones), "L:ones")
        dup_sy(lambda e: e.dma_start(out=onesb[:], in_=i_onesb), "L:onesb")
        dup_pl(lambda e: e.dma_start(out=gb_bc[:, 0:H], in_=d_gam.partition_broadcast(128)),
               "L:gam")
        dup_pl(lambda e: e.dma_start(out=gb_bc[:, H:2 * H], in_=d_bet.partition_broadcast(128)),
               "L:bet")
        dup_pl(lambda e: e.dma_start(out=wq_sb[:], in_=d_wqT.rearrange("(c p) o -> p c o", p=128)),
               "L:wq")
        dup_pl(lambda e: e.dma_start(out=wk_sb[:], in_=d_wkT.rearrange("(c p) o -> p c o", p=128)),
               "L:wk")
        dup_pl(lambda e: e.dma_start(out=wv_sb[:], in_=d_wvT.rearrange("(c p) o -> p c o", p=128)),
               "L:wv")
        dup_pl(lambda e: e.dma_start(out=bvb[:], in_=d_bv), "L:bv")

        for _k in ("L:wq", "L:wk", "L:wv", "L:bv", "L:gam", "L:bet"):
            sch.evt[_k] = ("sgq", sch.cnt["sgq"])

        def x_load(t):
            waits = [f"D:r2:{2 * (t - 2) + 1}"] if t >= 2 else []
            sch.dma("sy", lambda e, t=t: e.dma_start(out=xw[t % 2][:], in_=v_x[:, 2 * t:2 * t + 2, :]),
                    waits=waits, mark=f"L:x:{t}", phase=ph)
            waits2 = [f"S:cast:{t - 1}"] if t >= 1 else []
            sch.dma("sy", lambda e, t=t: e.dma_start(out=xTf[t % 2][:], in_=v_xT[:, :, 256 * t:256 * (t + 1)]),
                    waits=waits2, mark=f"L:xT:{t}", phase=ph)

        x_load(0)
        x_load(1)

        sch.op("sc", lambda e: e.activation(out=srow[0:1, 8:9], in_=clip_sb[0:1, 0:1], func=AF.Sigmoid),
               waits=["L:clip"], mark="S:sigx", phase=ph)
        sch.op("sc", lambda e: e.activation(out=srow[0:1, 9:10], in_=clip_sb[0:1, 1:2], func=AF.Sigmoid),
               mark="S:sigp", phase=ph)
        sch.op("dv", lambda e: e.memset(srow[0:1, 0:8], 0.0),
               waits=["L:eyef", "L:ones"], mark="D:srz", phase=ph)
        sch.op("dv", lambda e: e.memset(epsv[:], EPS), mark="D:eps", phase=ph)
        if DEBUG_TAPS:
            sch.op("dv", lambda e: e.memset(sml[:], 0.0), phase=ph)
            sch.op("dv", lambda e: e.memset(mskb[:], 0.0), phase=ph)
            sch.op("dv", lambda e: e.memset(smb[:], 0.0), phase=ph)
            sch.op("dv", lambda e: e.memset(wrk[:], 0.0), phase=ph)
            sch.op("dv", lambda e: e.memset(atn[:], 0.0), phase=ph)
            sch.op("dv", lambda e: e.memset(atnT[:], 0.0), phase=ph)

        WROT = 5
        wq_entries = []
        wq_state = {"li": 0, "ri": 0, "last_li": 0}

        def rs_stream(name, vsrc, ntile, rs_dst):
            for kk in range(ntile):
                bi = len(wq_entries) % WROT
                wq_entries.append(dict(
                    lf=lambda e, kk=kk, vsrc=vsrc, bi=bi: e.dma_start(
                        out=wtl[bi][:, 0:vsrc.shape[2]], in_=vsrc[:, kk, :]),
                    lm=f"Lw:rs{name}:{kk}",
                    cons=f"P:rs{name}:{kk}",
                    dv=(lambda e, kk=kk, vsrc=vsrc, rs_dst=rs_dst, bi=bi: e.reduce_max(
                        out=rs_dst[:, kk:kk + 1], in_=wtl[bi][:, 0:vsrc.shape[2]],
                        axis=AX.X, apply_absolute_value=True),
                        [f"Lw:rs{name}:{kk}"] + (["D:srz"] if (name == "g" and kk == 0) else []),
                        f"P:rs{name}:{kk}"),
                ))

        rs_stream("g", v_wg, ICN, rs_g)
        rs_stream("u", v_wu, ICN, rs_u)
        rs_stream("d", v_wd, HC, rs_d)

        def wq_drain():
            dv_hi = wq_state["last_li"]
            while wq_state["ri"] < dv_hi:
                ri = wq_state["ri"]
                ent_ = wq_entries[ri]
                if ent_["dv"] is not None:
                    dvs = ent_["dv"]
                    if not isinstance(dvs, list):
                        dvs = [dvs]
                    for fn, ws, mk in dvs:
                        sch.op("dv", fn, waits=list(ws), mark=mk, phase=ph)
                wq_state["ri"] += 1
            n = 0
            while wq_state["li"] < len(wq_entries) and n < 3:
                idx = wq_state["li"]
                ws = []
                if idx >= WROT:
                    ws.append(wq_entries[idx - WROT]["cons"])
                sch.dma("sc", wq_entries[idx]["lf"], waits=ws,
                        mark=wq_entries[idx]["lm"], phase=ph)
                wq_state["li"] += 1
                n += 1
            wq_state["last_li"] = wq_state["li"]

        def quant_stream(name, vsrcT, ntile, ncols, viw):
            for kk in range(ntile):
                bi = len(wq_entries) % WROT
                w0 = [f"Lw:q{name}:{kk}", f"G:bc:{name}"]
                if kk < ntile - 1:
                    w0.append(f"Lw:q{name}:{kk + 1}")
                sch.op("pl", lambda e, bi=bi, ncols=ncols: e.tensor_tensor(
                    out=wtl[bi][:, 0:ncols], in0=wtl[bi][:, 0:ncols],
                    in1=rqbc[:, 0:ncols], op=ALU.mult),
                    waits=w0, mark=f"P:tt{name}:{kk}", phase=ph)
                wts_ts = [f"P:tt{name}:{kk}"]
                if kk >= 2:
                    wts_ts.append(f"G:iw{name}:{kk - 1}")
                dv_ops = [
                    (lambda e, kk=kk, bi=bi, ncols=ncols: e.tensor_scalar(
                        out=iw_st[kk % 2][:, 0:ncols], in0=wtl[bi][:, 0:ncols],
                        scalar1=MAGIC, scalar2=-MAGIC, op0=ALU.add, op1=ALU.add),
                        wts_ts, f"P:q{name}:{kk}"),
                ]
                wq_entries.append(dict(
                    lf=lambda e, kk=kk, vsrcT=vsrcT, ncols=ncols, bi=bi: e.dma_start(
                        out=wtl[bi][:, 0:ncols], in_=vsrcT[:, kk, :]),
                    lm=f"Lw:q{name}:{kk}",
                    cons=f"P:q{name}:{kk}",
                    dv=dv_ops,
                    st=None,
                ))
                stw = [f"P:q{name}:{kk}"]
                if kk > 0:
                    stw.append(f"G:iw{name}:{kk - 1}")
                sch.dma("pl", lambda e, kk=kk, viw=viw, ncols=ncols: e.dma_start(
                    out=viw[:, kk, :], in_=iw_st[kk % 2][:, 0:ncols]),
                    waits=stw, mark=f"G:iw{name}:{kk}", phase=ph, sem=f"siw{name}")

        b5_reader = ["L:eyeb"]

        def rs_flip(name, rs_src, nicol, prev_store, rs_done, bc_extra=()):
            sch.op("pe", lambda e, rs_src=rs_src, nicol=nicol: e.matmul(
                pb[5][0:nicol, 0:128], rs_src[:, 0:nicol], eyef[:], start=True, stop=True),
                waits=[sch.latest(rs_done, *b5_reader)], mark=f"T:fl{name}", phase=ph)
            evw = [f"T:fl{name}"] + ([prev_store] if prev_store else [])
            sch.op("dv", lambda e, nicol=nicol: e.tensor_scalar(
                out=row_sb[0:nicol, :], in0=pb[5][0:nicol, 0:128],
                scalar1=1.0 / 127.0, scalar2=None, op0=ALU.mult),
                waits=evw, mark=f"D:fle:{name}", phase=ph)
            b5_reader[:] = [f"D:fle:{name}"]
            sch.op("dv", lambda e, nicol=nicol: e.reciprocal(
                out=row_sb[0:nicol, :], in_=row_sb[0:nicol, :]),
                mark=f"D:flr:{name}", phase=ph)
            sch.dma("sc", lambda e, name=name, nicol=nicol: e.dma_start(
                out=d_rr[name].rearrange("o (a b) -> (o a) b", b=128)[0:nicol, :],
                in_=row_sb[0:nicol, :]), waits=[f"D:flr:{name}"],
                mark=f"V:rr:{name}", phase=ph)
            ncols = II if name != "d" else H
            sch.dma("pl", lambda e, name=name, ncols=ncols: e.dma_start(
                out=rqbc[:, 0:ncols], in_=d_rr[name].partition_broadcast(128)),
                waits=[f"V:rr:{name}"] + list(bc_extra), mark=f"G:bc:{name}:a", phase=ph)
            sch.dma("pl", lambda e, name=name, ncols=ncols: e.dma_start(
                out=rqbc[:, 0:ncols], in_=d_rr[name].partition_broadcast(128)),
                waits=[f"G:bc:{name}:a"], mark=f"G:bc:{name}", phase=ph)

        pend_sc = []
        for t in range(NT):
            if DEBUG_TAPS and t == 1:
                taps = [
                    ("xbf", xbf[0], BF16, "S:cast:0"),
                    ("qf", qf, BF16, "S:pje:0:7"),
                    ("kf", kf, BF16, "S:pje:0:15"),
                    ("o0", osb, F32, "S:oe:0"),
                    ("r0", rsb, F32, "D:r:0"),
                    ("y0", osb, F32, "D:r2:0"),
                    ("x1w0", x1w[0], F32, "D:x1:0"),
                    ("vsb", vsb[0], BF16, "S:ve:1"),
                    ("ssb", ssb, F32, "D:se:1"),
                    ("mskb", mskb, F32, "D:sm:1"),
                    ("smb", smb, F32, "D:sm:1"),
                    ("wrk", wrk, F32, "S:ex:1"),
                    ("atn", atn, BF16, "D:at:1"),
                    ("atnT", atnT, BF16, "D:ae:1"),
                    ("sml", sml, F32, "D:x1:1"),
                ]
                for nm, tens, dtp, wk in taps:
                    free = 1
                    for d_ in tens.shape[1:]:
                        free *= d_
                    dbg = nc.dram_tensor(f"tap_{nm}", [128, free], dtp,
                                         kind="ExternalOutput").ap()
                    sch.dma("sy", lambda e, tens=tens, dbg=dbg: e.dma_start(
                        out=dbg, in_=tens[:]), waits=[wk], mark=f"TAP:{nm}", phase=ph)

            if t + 2 < NT:
                if DEBUG_TAPS and t == 0:
                    pass
                elif DEBUG_TAPS and t == 1:
                    x_load(2)
                    x_load(3)
                else:
                    x_load(t + 2)
            if t == 6:
                rs_flip("g", rs_g, ICN, None, f"P:rsg:{ICN - 1}")
                quant_stream("g", v_wgT, HC, II, v_iwg)
            if t == 9:
                rs_flip("u", rs_u, ICN, "V:rr:g", f"P:rsu:{ICN - 1}",
                        bc_extra=[f"P:qg:{HC - 1}"])
                quant_stream("u", v_wuT, HC, II, v_iwu)
            if t == 12:
                rs_flip("d", rs_d, HC, "V:rr:u", f"P:rsd:{HC - 1}",
                        bc_extra=[f"P:qu:{HC - 1}"])
                quant_stream("d", v_wdT, ICN, H, v_iwd)
            wq_drain()

            wts = [f"L:xT:{t}"]
            if t == 0:
                wts += ["L:wq", "L:wk", "L:wv", "L:bv", "L:onesb", "L:eyeb"]
            if t >= 2:
                wts.append(f"T:v:{2 * (t - 2) + 1}")
            if DEBUG_TAPS and t == 1:
                wts += ["TAP:sml"]
            sch.op("sc", lambda e, t=t: e.copy(out=xbf[t % 2][:], in_=xTf[t % 2][:]),
                   waits=wts, mark=f"S:cast:{t}", phase=ph)

            for wi, (wsb, bsb, dst) in enumerate(((wq_sb, bq_sb, qf), (wk_sb, bk_sb, kf))):
                for oc in range(HC):
                    g = wi * 8 + oc
                    bank = pb[g % 2]
                    if g == 0:
                        w0 = [f"S:cast:{t}"]
                    elif g == 1:
                        w0 = [f"S:pje:{t - 1}:15"] if t > 0 else []
                    else:
                        w0 = [f"S:pje:{t}:{g - 2}"]
                    for hc in range(HC):
                        sch.op("pe", lambda e, bank=bank, wsb=wsb, oc=oc, hc=hc, t=t: e.matmul(
                            bank[:, 0:256], wsb[:, hc, oc * 128:(oc + 1) * 128],
                            xbf[t % 2][:, hc, :], start=(hc == 0), stop=(hc == HC - 1)),
                            waits=(w0 if hc == 0 else []),
                            mark=(f"T:pj:{t}:{g}" if hc == HC - 1 else None), phase=ph)
                    wE = [f"T:pj:{t}:{g}"]
                    if g == 0:
                        wE.append("L:bq")
                        if t == 0:
                            wE.append("L:bk")
                    sch.op("sc", lambda e, bank=bank, dst=dst, bsb=bsb, oc=oc, t=t: e.activation(
                        out=dst[:, oc, :], in_=bank[:, 0:256], func=AF.Identity,
                        bias=bsb[:, oc:oc + 1], scale=1.0),
                        waits=wE, mark=f"S:pje:{t}:{g}", phase=ph)
            sch.evt[f"T:pj:{t}"] = sch.evt[f"T:pj:{t}:15"]

            for j in range(2):
                if j == 1:
                    wq_drain()
                w = 2 * t + j
                par = w % 2

                wv0 = [f"S:ve:{w - 1}"] if w >= 1 else []
                for half in range(2):
                    for hc in range(HC):
                        sch.op("pe", lambda e, half=half, hc=hc, t=t, j=j: e.matmul(
                            pb[2 + half][:, 0:512],
                            xbf[t % 2][:, hc, j * 128:(j + 1) * 128],
                            wv_sb[:, hc, half * 512:(half + 1) * 512],
                            start=(hc == 0), stop=False),
                            waits=(wv0 if (half == 0 and hc == 0) else []), phase=ph)
                    sch.op("pe", lambda e, half=half: e.matmul(
                        pb[2 + half][:, 0:512], onesb[0:1, :],
                        bvb[0:1, half * 512:(half + 1) * 512], start=False, stop=True),
                        mark=(f"T:v:{w}" if half == 1 else None), phase=ph)
                sch.op("sc", lambda e, t=t, j=j: e.copy(out=vsb[t % 2][:, j, 0:512],
                                                        in_=pb[2][:, 0:512]),
                       waits=[f"T:v:{w}"], phase=ph)
                sch.op("sc", lambda e, t=t, j=j: e.copy(out=vsb[t % 2][:, j, 512:1024],
                                                        in_=pb[3][:, 0:512]),
                       mark=f"S:ve:{w}", phase=ph)

                ws0 = [f"S:pje:{t}:15"] if j == 0 else [f"D:se:{w - 1}"]
                if j == 0 and w >= 1:
                    sch.push(f"S:pje:{t}:15", f"D:se:{w - 1}")
                for oc in range(HC):
                    sch.op("pe", lambda e, oc=oc, j=j: e.matmul(
                        pb[4][:, 0:128], qf[:, oc, j * 128:(j + 1) * 128],
                        kf[:, oc, j * 128:(j + 1) * 128],
                        start=(oc == 0), stop=(oc == HC - 1)),
                        waits=(ws0 if oc == 0 else []),
                        mark=(f"T:s:{w}" if oc == HC - 1 else None), phase=ph)
                wse = [f"T:s:{w}"]
                if w >= 2:
                    wse.append(f"D:sm:{w - 2}")
                sch.op("dv", lambda e, j=j: e.tensor_scalar(
                    out=ssb[:, j, :], in0=pb[4][:, 0:128], scalar1=0.03125,
                    scalar2=None, op0=ALU.mult),
                    waits=wse, mark=f"D:se:{w}", phase=ph)
                wtk0 = [f"S:ex:{w - 2}"] if w >= 2 else []
                sch.op("dv", lambda e, j=j: e.tensor_copy(
                    out=wrk[:, j, :], in_=ssb[:, j, :]),
                    waits=wtk0, mark=f"D:tk:{w}:0", phase=ph)
                tki = 1
                for it in range(4):
                    sch.op("dv", lambda e, j=j: e.max(out=m8[:], in_=wrk[:, j, :]),
                           waits=[f"D:tk:{w}:{tki - 1}"], mark=f"D:tk:{w}:{tki}", phase=ph)
                    tki += 1
                    if it < 3:
                        sch.op("dv", lambda e, j=j: e.match_replace(
                            out=wrk[:, j, :], in_to_replace=m8[:], in_values=wrk[:, j, :],
                            imm_value=NEG),
                            waits=[f"D:tk:{w}:{tki - 1}"], mark=f"D:tk:{w}:{tki}", phase=ph)
                        tki += 1
                sch.op("dv", lambda e, par=par: e.tensor_copy(out=sl(par, 0), in_=m8[:, 7:8]),
                       waits=[f"D:tk:{w}:{tki - 1}"], mark=f"D:tk:{w}:{tki}", phase=ph)
                sch.op("dv", lambda e, j=j, par=par: e.reduce_max(
                    out=sl(par, 1), in_=ssb[:, j, :], axis=AX.X), mark=f"D:rx:{w}", phase=ph)
                sch.op("dv", lambda e, par=par: e.tensor_scalar(
                    out=sl(par, 2), in0=sl(par, 1), scalar1=BIGB, scalar2=-1.0,
                    op0=ALU.add, op1=ALU.mult), waits=[f"D:rx:{w}"], phase=ph)
                sch.op("dv", lambda e, j=j, par=par: e.tensor_scalar(
                    out=mskb[:, j, :], in0=ssb[:, j, :], scalar1=sl(par, 0), scalar2=None,
                    op0=ALU.is_ge), waits=[f"D:tk:{w}:8"], phase=ph)
                sch.op("dv", lambda e, j=j: e.scalar_tensor_tensor(
                    out=smb[:, j, :], in0=ssb[:, j, :], scalar=BIGB, in1=mskb[:, j, :],
                    op0=ALU.add, op1=ALU.mult), mark=f"D:sm:{w}", phase=ph)
                while pend_sc:
                    pfn, pwaits, pmk = pend_sc.pop(0)
                    sch.dma("sc", pfn, waits=pwaits, mark=pmk, phase=ph)
                sch.op("sc", lambda e, j=j, par=par: e.activation(
                    out=wrk[:, j, :], in_=smb[:, j, :], func=AF.Exp, bias=sl(par, 2),
                    scale=1.0, accum_out=sl(par, 3)),
                    waits=[f"D:sm:{w}"], mark=f"S:ex:{w}", phase=ph)
                sch.op("dv", lambda e, par=par: e.reciprocal(out=sl(par, 4), in_=sl(par, 3)),
                       waits=[f"S:ex:{w}"], mark=f"D:rc0:{w}", phase=ph)
                wat = [f"D:rc0:{w}"] + ([f"T:aT:{w - 2}"] if w >= 2 else [])
                sch.op("dv", lambda e, j=j, par=par: e.tensor_scalar(
                    out=atn[:, j, :], in0=wrk[:, j, :], scalar1=sl(par, 4), scalar2=None,
                    op0=ALU.mult), waits=wat, mark=f"D:at:{w}", phase=ph)

                sch.op("pe", lambda e, j=j: e.matmul(
                    pb[5][:, 0:128], atn[:, j, :], eyeb[:], start=True, stop=True),
                    waits=[f"D:at:{w}"], mark=f"T:aT:{w}", phase=ph)
                wae = [f"T:aT:{w}"]
                if w >= 2:
                    wae.append(f"T:o:{w - 2}")
                if w >= 1:
                    wae.append(f"S:oe:{w - 1}")
                sch.op("dv", lambda e, j=j: e.tensor_copy(out=atnT[:, j, :], in_=pb[5][:, 0:128]),
                       waits=wae, mark=f"D:ae:{w}", phase=ph)
                b5_reader[:] = [f"D:ae:{w}"]

                wo0 = [f"D:ae:{w}"]
                for half in range(2):
                    sch.op("pe", lambda e, half=half, t=t, j=j: e.matmul(
                        pb[6 + half][:, 0:512], atnT[:, j, :],
                        vsb[t % 2][:, j, half * 512:(half + 1) * 512], start=True, stop=True),
                        waits=(wo0 if half == 0 else []),
                        mark=(f"T:o:{w}" if half == 1 else None), phase=ph)
                sch.op("sc", lambda e: e.copy(out=osb[:, 0:512], in_=pb[6][:, 0:512]),
                       waits=[f"T:o:{w}"], phase=ph)
                sch.op("sc", lambda e: e.copy(out=osb[:, 512:1024], in_=pb[7][:, 0:512]),
                       mark=f"S:oe:{w}", phase=ph)

                wlk = [f"S:oe:{w}"]
                if DEBUG_TAPS and w == 0:
                    wlk.append("TAP:o0")
                sch.op("dv", lambda e: e.scalar_tensor_tensor(
                    out=osb[:], in0=osb[:], scalar=0.01, in1=osb[:], op0=ALU.mult, op1=ALU.max),
                    waits=wlk, phase=ph)
                wr0 = [f"L:x:{t}"] if j == 0 else []
                sch.op("dv", lambda e, t=t, j=j, par=par: e.scalar_tensor_tensor(
                    out=rsb[:], in0=osb[:], scalar=1.0, in1=xw[t % 2][:, j, :],
                    op0=ALU.mult, op1=ALU.add, accum_out=sl(par, 5)),
                    waits=wr0, mark=f"D:r:{w}", phase=ph)
                wsq1 = [f"D:r:{w}"]
                if DEBUG_TAPS and w == 0:
                    wsq1.append("TAP:r0")
                sch.op("sc", lambda e, par=par: e.activation(
                    out=osb[:], in_=rsb[:], func=AF.Square, accum_out=sl(par, 6)),
                    waits=wsq1, mark=f"S:sq1:{w}", phase=ph)
                sch.op("dv", lambda e, par=par: e.tensor_scalar(
                    out=sl(par, 7), in0=sl(par, 5), scalar1=1.0 / 1024.0, scalar2=None,
                    op0=ALU.mult), waits=[f"S:sq1:{w}"], mark=f"D:mn1:{w}", phase=ph)
                sch.op("dv", lambda e, par=par: e.tensor_scalar(
                    out=sl(par, 8), in0=sl(par, 7), scalar1=sl(par, 7), scalar2=None,
                    op0=ALU.mult), waits=[f"D:mn1:{w}"], mark=f"D:m21:{w}", phase=ph)
                sch.op("dv", lambda e, par=par: e.tensor_scalar(
                    out=sl(par, 8), in0=sl(par, 6), scalar1=1.0 / 1024.0, scalar2=sl(par, 8),
                    op0=ALU.mult, op1=ALU.subtract),
                    waits=[f"D:m21:{w}"], mark=f"D:vr1:{w}", phase=ph)
                sch.op("sc", lambda e, par=par: e.activation(
                    out=sl(par, 9), in_=sl(par, 8), func=AF.Sqrt, bias=epsv[:]),
                    waits=[f"D:vr1:{w}"], mark=f"S:st1:{w}", phase=ph)
                sch.op("dv", lambda e, par=par: e.reciprocal(out=sl(par, 9), in_=sl(par, 9)),
                       waits=[f"S:st1:{w}"], mark=f"D:rc1:{w}", phase=ph)
                sch.op("dv", lambda e, par=par: e.tensor_scalar(
                    out=osb[:], in0=rsb[:], scalar1=sl(par, 7), scalar2=sl(par, 9),
                    op0=ALU.subtract, op1=ALU.mult), waits=[f"D:rc1:{w}"], phase=ph)
                wg0 = ["L:gam", "L:bet"] if w == 0 else []
                sch.op("dv", lambda e: e.tensor_tensor(
                    out=osb[:], in0=osb[:], in1=gb_bc[:, 0:H], op=ALU.mult), waits=wg0, phase=ph)
                sch.op("dv", lambda e: e.tensor_tensor(
                    out=osb[:], in0=osb[:], in1=gb_bc[:, H:2 * H], op=ALU.add), phase=ph)
                sch.op("dv", lambda e, t=t, j=j, par=par: e.scalar_tensor_tensor(
                    out=rsb[:], in0=osb[:], scalar=1.0, in1=xw[t % 2][:, j, :],
                    op0=ALU.mult, op1=ALU.add, accum_out=sl(par, 5)),
                    mark=f"D:r2:{w}", phase=ph)
                wsq2 = [f"D:r2:{w}"]
                if DEBUG_TAPS and w == 0:
                    wsq2.append("TAP:y0")
                sch.op("sc", lambda e, par=par: e.activation(
                    out=osb[:], in_=rsb[:], func=AF.Square, accum_out=sl(par, 6)),
                    waits=wsq2, mark=f"S:sq2:{w}", phase=ph)
                sch.op("dv", lambda e, par=par: e.tensor_scalar(
                    out=sl(par, 7), in0=sl(par, 5), scalar1=1.0 / 1024.0, scalar2=None,
                    op0=ALU.mult), waits=[f"S:sq2:{w}"], mark=f"D:mn2:{w}", phase=ph)
                sch.op("dv", lambda e, par=par: e.tensor_scalar(
                    out=sl(par, 8), in0=sl(par, 7), scalar1=sl(par, 7), scalar2=None,
                    op0=ALU.mult), waits=[f"D:mn2:{w}"], mark=f"D:m22:{w}", phase=ph)
                sch.op("dv", lambda e, par=par: e.tensor_scalar(
                    out=sl(par, 8), in0=sl(par, 6), scalar1=1.0 / 1024.0, scalar2=sl(par, 8),
                    op0=ALU.mult, op1=ALU.subtract),
                    waits=[f"D:m22:{w}"], mark=f"D:vr2:{w}", phase=ph)
                sch.op("sc", lambda e, par=par: e.activation(
                    out=sl(par, 9), in_=sl(par, 8), func=AF.Sqrt, bias=epsv[:]),
                    waits=[f"D:vr2:{w}"], mark=f"S:st2:{w}", phase=ph)
                sch.op("dv", lambda e, par=par: e.reciprocal(out=sl(par, 9), in_=sl(par, 9)),
                       waits=[f"S:st2:{w}"], mark=f"D:rc2:{w}", phase=ph)
                wx1 = [f"D:rc2:{w}"] + ([f"V:x1:{w - 2}"] if w >= 2 else [])
                sch.op("dv", lambda e, par=par: e.tensor_scalar(
                    out=x1w[par][:], in0=rsb[:], scalar1=sl(par, 7), scalar2=sl(par, 9),
                    op0=ALU.subtract, op1=ALU.mult), waits=wx1, phase=ph)
                sch.op("dv", lambda e, w=w, par=par: e.reduce_max(
                    out=mx_all[:, w:w + 1], in_=x1w[par][:], axis=AX.X,
                    apply_absolute_value=True), mark=f"D:x1:{w}", phase=ph)
                pend_sc.append((
                    lambda e, w=w, par=par: e.dma_start(out=v_x1[:, w, :], in_=x1w[par][:]),
                    [f"D:x1:{w}"], f"V:x1:{w}"))

        while pend_sc:
            pfn, pwaits, pmk = pend_sc.pop(0)
            sch.dma("sc", pfn, waits=pwaits, mark=pmk, phase=ph)
        while wq_state["li"] < len(wq_entries) or wq_state["ri"] < len(wq_entries):
            wq_drain()
        assert sch.cnt["siwg"] == 16 * HC and sch.cnt["siwu"] == 16 * HC
        assert sch.cnt["siwd"] == 16 * ICN
        sch.evt["G:iwg"] = ("siwg", 16 * HC)
        sch.evt["G:iwu"] = ("siwu", 16 * HC)
        sch.evt["G:iwd"] = ("siwd", 16 * ICN)

        sch.op("dv", lambda e: e.reduce_max(out=mvec[:, 0:1], in_=mx_all[:], axis=AX.X),
               waits=[f"D:x1:{NW - 1}"], mark="D:mxv", phase=ph)
        sch.op("pe", lambda e: e.matmul(pb[5][0:1, 0:128], mvec[:, 0:1], eyef[:],
                                        start=True, stop=True),
               waits=["D:mxv"], mark="T:mfl", phase=ph)
        sch.op("dv", lambda e: e.tensor_copy(out=row_sb[0:1, :], in_=pb[5][0:1, 0:128]),
               waits=["T:mfl", "V:rr:d"], mark="D:mfe", phase=ph)
        b5_reader[:] = ["D:mfe"]
        sch.op("dv", lambda e: e.reduce_max(out=srow[0:1, 0:1], in_=row_sb[0:1, 0:128], axis=AX.X),
               waits=["D:srz", "D:mfe"], mark="D:gx", phase=ph)
        sch.dma("sy", lambda e: e.dma_start(out=d_cc["in1"].ap(), in_=srow[0:1, 0:2]),
                waits=["D:gx"], mark="G:cc1i", phase=ph)
        sch.cc(lambda e: e.collective_compute(
            "AllReduce", ALU.max, replica_groups=[list(range(8))],
            ins=[d_cc["in1"].ap().opt()], outs=[d_cc["out1"].ap().opt()]),
            waits=["G:cc1i"], mark="C:1", phase=ph)
        sch.dma("sy", lambda e: e.dma_start(out=srow[0:1, 0:2], in_=d_cc["out1"].ap()),
                waits=["C:1"], mark="G:cc1o:a", phase=ph)
        sch.dma("sy", lambda e: e.dma_start(out=srow[0:1, 0:2], in_=d_cc["out1"].ap()),
                waits=["G:cc1o:a"], mark="G:cc1o", phase=ph)
        sch.op("dv", lambda e: e.tensor_scalar(
            out=srow[0:1, 2:3], in0=srow[0:1, 0:1], scalar1=srow[0:1, 8:9], scalar2=None,
            op0=ALU.mult), waits=["G:cc1o", "S:sigx"], mark="D:tx", phase=ph)
        sch.op("dv", lambda e: e.tensor_scalar(
            out=srow[0:1, 3:4], in0=srow[0:1, 2:3], scalar1=1.0 / 127.0, scalar2=1.0 / 127.0,
            op0=ALU.mult, op1=ALU.mult), waits=["D:tx"], mark="D:rqx", phase=ph)
        sch.op("pe", lambda e: e.matmul(pb[5][:, 0:2], ones32[0:1, :], srow[0:1, 2:4],
                                        start=True, stop=True),
               waits=["D:rqx"], mark="T:k1a", phase=ph)
        sch.op("dv", lambda e: e.tensor_copy(out=bc2[:, 0:2], in_=pb[5][:, 0:2]),
               waits=["T:k1a"], mark="D:bce", phase=ph)
        b5_reader[:] = ["D:bca"]
        sch.op("dv", lambda e: e.reciprocal(out=bc2[:, 0:1], in_=bc2[:, 0:1]),
               waits=["D:bce"], mark="D:rca", phase=ph)
        sch.op("dv", lambda e: e.tensor_scalar(
            out=bc2[:, 0:1], in0=bc2[:, 0:1], scalar1=127.0, scalar2=None, op0=ALU.mult),
            waits=["D:rca"], mark="D:bca", phase=ph)
        sch.op("dv", lambda e: e.tensor_scalar(
            out=sg_all[:], in0=rs_g[:], scalar1=bc2[:, 1:2], scalar2=None, op0=ALU.mult),
            waits=["D:bca", f"P:rsg:{ICN - 1}", f"P:rsu:{ICN - 1}"], phase=ph)
        sch.op("dv", lambda e: e.tensor_scalar(
            out=su_all[:], in0=rs_u[:], scalar1=bc2[:, 1:2], scalar2=None, op0=ALU.mult),
            mark="D:sga", phase=ph)

        sch.evt["T:aend"] = sch.evt["T:k1a"]
        sch.evt["S:aend"] = sch.evt[f"S:st2:{NW - 1}"]
        sch.evt["D:aend"] = sch.evt["D:sga"]
        sch.evt["P:aend"] = sch.evt[f"P:qd:{ICN - 1}"]
        sch.evt["V:x1all"] = ("scq", sch.cnt["scq"])

        block = ent(nc.Block())
        sch.emit(block, sems, 0)
        a.close()

        ph = 1
        b1 = contextlib.ExitStack()
        ent3 = b1.enter_context
        ixf = ent3(nc.sbuf_tensor("ixf", [128, HC, 2048], BF16))
        tqb = [ent3(nc.sbuf_tensor(f"tqb{i}", [128, H], BF16)) for i in range(2)]
        qtmp = ent3(nc.sbuf_tensor("qtmp", [128, H], F32))
        x1r = [ent3(nc.sbuf_tensor(f"x1r{i}", [128, H], F32)) for i in range(2)]
        gch = [ent3(nc.sbuf_tensor(f"gch{i}", [128, HC, 128], BF16)) for i in range(2)]
        uch = [ent3(nc.sbuf_tensor(f"uch{i}", [128, HC, 128], BF16)) for i in range(2)]
        pgs = [ent3(nc.sbuf_tensor(f"pgs{i}", [128, 512], F32)) for i in range(2)]
        pus = [ent3(nc.sbuf_tensor(f"pus{i}", [128, 512], F32)) for i in range(2)]

        FENCE = ["T:aend", "S:aend", "D:aend", "P:aend", "G:iwg", "G:iwu", "V:x1all"]

        if DEBUG_TAPS:
            d_tixf = nc.dram_tensor("tap_ixf", [128, HC * 2048], BF16, kind="ExternalOutput").ap()
            d_tbc2 = nc.dram_tensor("tap_bc2", [128, 4], F32, kind="ExternalOutput").ap()
            d_tsg = nc.dram_tensor("tap_sg", [128, 2 * ICN], F32, kind="ExternalOutput").ap()

        kglob = 0
        pend_p = []
        for half in range(2):
            for wl in range(16):
                w = 16 * half + wl
                wts = list(FENCE) if w == 0 else []
                if w >= 2:
                    wts.append(f"D:tq:{w - 2}")
                sch.dma("sy", lambda e, w=w, wl=wl: e.dma_start(
                    out=x1r[wl % 2][:], in_=v_x1[:, w, :]), waits=wts, mark=f"L:x1:{w}", phase=ph)
                if wl == 15:
                    sch.dma("sy", lambda e, w=w, wl=wl: e.dma_start(
                        out=x1r[wl % 2][:], in_=v_x1[:, w, :]), mark=f"L:x1:{w}", phase=ph)
                wq0 = [f"L:x1:{w}"]
                if wl < 15:
                    wq0.append(f"L:x1:{w + 1}")
                if w == 0:
                    wq0.append("D:sga")
                if w == 16:
                    wq0.append(f"S:sil:0:{ICN - 1}:3")
                    if DEBUG_TAPS:
                        wq0.append("TAP:ixf")
                sch.op("dv", lambda e, wl=wl: e.tensor_scalar(
                    out=qtmp[:], in0=x1r[wl % 2][:], scalar1=bc2[:, 0:1], scalar2=MAGIC,
                    op0=ALU.mult, op1=ALU.add), waits=wq0, phase=ph)
                sch.op("dv", lambda e: e.tensor_scalar(
                    out=qtmp[:], in0=qtmp[:], scalar1=-MAGIC, scalar2=-128.0,
                    op0=ALU.add, op1=ALU.max), phase=ph)
                wq2 = [f"T:tr:{w - 2}"] if w >= 2 else []
                sch.op("dv", lambda e, wl=wl: e.tensor_scalar(
                    out=tqb[wl % 2][:], in0=qtmp[:], scalar1=127.0, scalar2=None, op0=ALU.min),
                    waits=wq2, mark=f"D:tq:{w}", phase=ph)
                bk0, bk1 = (0, 1) if wl % 2 == 0 else (2, 3)
                wt0 = [f"D:tq:{w}"]
                for hc in range(HC):
                    bank = pb[bk0] if hc < 4 else pb[bk1]
                    off = (hc % 4) * 128
                    sch.op("pe", lambda e, bank=bank, off=off, hc=hc, wl=wl: e.matmul(
                        bank[:, off:off + 128], tqb[wl % 2][:, hc * 128:(hc + 1) * 128],
                        eyeb[:], start=(hc % 4 == 0), stop=(hc % 4 == 3)),
                        waits=(wt0 if hc == 0 else []),
                        mark=(f"T:tr:{w}" if hc == HC - 1 else None), phase=ph)
                sch.op("dv", lambda e, bk0=bk0, wl=wl: e.tensor_copy(
                    out=ixf[:, 0:4, wl * 128:(wl + 1) * 128], in_=pb[bk0][:, 0:512]),
                    waits=[f"T:tr:{w}"], phase=ph)
                sch.op("dv", lambda e, bk1=bk1, wl=wl: e.tensor_copy(
                    out=ixf[:, 4:8, wl * 128:(wl + 1) * 128], in_=pb[bk1][:, 0:512]),
                    mark=f"D:ixe:{w}", phase=ph)

            for ic in range(ICN):
                if half == 0 and ic == 0:
                    wl0, wl1 = ["G:iwg"], ["G:iwu"]
                elif ic >= 2:
                    wl0 = [f"T:g:{half}:{ic - 2}"]
                    wl1 = [f"T:u:{half}:{ic - 2}"]
                elif half == 1:
                    wl0 = [f"T:g:0:{ICN - 2 + ic}"]
                    wl1 = [f"T:u:0:{ICN - 2 + ic}"]
                else:
                    wl0, wl1 = [], []
                sch.dma("sy", lambda e, ic=ic: e.dma_start(
                    out=gch[ic % 2][:], in_=v_iwg[:, :, ic * 128:(ic + 1) * 128]),
                    waits=wl0, mark=f"L:g:{half}:{ic}", phase=ph)
                sch.dma("sy", lambda e, ic=ic: e.dma_start(
                    out=uch[ic % 2][:], in_=v_iwu[:, :, ic * 128:(ic + 1) * 128]),
                    waits=wl1, mark=f"L:u:{half}:{ic}", phase=ph)
                if ic == 0:
                    sch.dma("sy", lambda e, ic=ic: e.dma_start(
                        out=uch[ic % 2][:], in_=v_iwu[:, :, ic * 128:(ic + 1) * 128]),
                        mark=f"L:u:{half}:{ic}", phase=ph)
                if DEBUG_TAPS and half == 0 and ic == 0:
                    sch.dma("sy", lambda e: e.dma_start(out=d_tixf, in_=ixf[:]),
                            waits=["D:ixe:15"], mark="TAP:ixf", phase=ph)
                    sch.dma("sy", lambda e: e.dma_start(out=d_tbc2, in_=bc2[:]),
                            waits=["D:sga"], mark="TAP:bc2", phase=ph)
                    sch.dma("sy", lambda e: e.dma_start(out=d_tsg[:, 0:ICN], in_=sg_all[:]),
                            waits=["D:sga"], mark="TAP:sg", phase=ph)
                    sch.dma("sy", lambda e: e.dma_start(out=d_tsg[:, ICN:2 * ICN], in_=su_all[:]),
                            waits=["D:sga"], mark="TAP:su", phase=ph)

                if ic == 0:
                    ixl = f"D:ixe:{16 * half + 15}"
                    wg0 = [ixl]
                    sch.push(ixl, f"L:g:{half}:0", f"L:u:{half}:0")
                else:
                    sil_prev = f"S:sil:{half}:{ic - 1}:3"
                    wg0 = [sil_prev]
                    sch.push(sil_prev, f"L:g:{half}:{ic}")
                first = True
                for hc in range(HC):
                    for tq4 in range(4):
                        sch.op("pe", lambda e, hc=hc, tq4=tq4, ic=ic: e.matmul(
                            pb[tq4][:, 0:512], gch[ic % 2][:, hc, :],
                            ixf[:, hc, tq4 * 512:(tq4 + 1) * 512],
                            start=(hc == 0), stop=(hc == HC - 1)),
                            waits=(wg0 if first else []),
                            mark=(f"T:g:{half}:{ic}" if (hc == HC - 1 and tq4 == 3) else None),
                            phase=ph)
                        first = False
                if ic == 0:
                    wu0 = [f"D:ixe:{16 * half + 15}"]
                else:
                    ue_prev = f"D:ue:{half}:{ic - 1}:3"
                    wu0 = [ue_prev]
                    sch.push(ue_prev, f"L:u:{half}:{ic}")
                first = True
                for hc in range(HC):
                    for tq4 in range(4):
                        sch.op("pe", lambda e, hc=hc, tq4=tq4, ic=ic: e.matmul(
                            pb[4 + tq4][:, 0:512], uch[ic % 2][:, hc, :],
                            ixf[:, hc, tq4 * 512:(tq4 + 1) * 512],
                            start=(hc == 0), stop=(hc == HC - 1)),
                            waits=(wu0 if first else []),
                            mark=(f"T:u:{half}:{ic}" if (hc == HC - 1 and tq4 == 3) else None),
                            phase=ph)
                        first = False

                for tq4 in range(4):
                    kk = kglob
                    kglob += 1
                    while len(pend_p) > 2:
                        pfn, pwaits, pmk = pend_p.pop(0)
                        sch.dma("sc", pfn, waits=pwaits, mark=pmk, phase=ph)
                    ws0 = [f"T:g:{half}:{ic}"]
                    if kk >= 2:
                        ws0.append(f"D:pp:{kk - 2}")
                    if kk == 0:
                        ws0.append("D:sga")
                    sch.op("sc", lambda e, tq4=tq4, ic=ic: e.activation(
                        out=pgs[tq4 % 2][:], in_=pb[tq4][:, 0:512], func=AF.Silu,
                        scale=sg_all[:, ic:ic + 1]),
                        waits=ws0, mark=f"S:sil:{half}:{ic}:{tq4}", phase=ph)
                    wu1 = [f"T:u:{half}:{ic}"]
                    if kk >= 2:
                        wu1.append(f"V:p:{kk - 2}")
                    sch.op("dv", lambda e, tq4=tq4, ic=ic: e.tensor_scalar(
                        out=pus[tq4 % 2][:], in0=pb[4 + tq4][:, 0:512],
                        scalar1=su_all[:, ic:ic + 1], scalar2=None, op0=ALU.mult),
                        waits=wu1, mark=f"D:ue:{half}:{ic}:{tq4}", phase=ph)
                    sch.op("dv", lambda e, tq4=tq4: e.tensor_tensor(
                        out=pus[tq4 % 2][:], in0=pgs[tq4 % 2][:], in1=pus[tq4 % 2][:],
                        op=ALU.mult), waits=[f"S:sil:{half}:{ic}:{tq4}"], phase=ph)
                    sch.op("dv", lambda e, tq4=tq4, kk=kk: e.reduce_max(
                        out=mp_all[:, kk:kk + 1], in_=pus[tq4 % 2][:], axis=AX.X,
                        apply_absolute_value=True), mark=f"D:pp:{kk}", phase=ph)
                    toff = half * 2048 + tq4 * 512
                    pend_p.append((
                        lambda e, tq4=tq4, ic=ic, toff=toff: e.dma_start(
                            out=v_p[:, ic, toff:toff + 512], in_=pus[tq4 % 2][:]),
                        [f"D:pp:{kk}"], f"V:p:{kk}"))

        while pend_p:
            pfn, pwaits, pmk = pend_p.pop(0)
            sch.dma("sc", pfn, waits=pwaits, mark=pmk, phase=ph)

        sch.op("dv", lambda e: e.reduce_max(out=mvec[:, 1:2], in_=mp_all[:], axis=AX.X),
               waits=["D:pp:175"], mark="D:mpv", phase=ph)
        sch.push("D:mpv", f"S:sil:1:{ICN - 1}:3")
        sch.op("pe", lambda e: e.matmul(pb[0][0:1, 0:128], mvec[:, 1:2], eyef[:],
                                        start=True, stop=True),
               waits=["D:mpv"], mark="T:mfl2", phase=ph)
        sch.op("dv", lambda e: e.tensor_copy(out=row_sb[0:1, :], in_=pb[0][0:1, 0:128]),
               waits=["T:mfl2"], mark="D:mfe2", phase=ph)
        sch.op("dv", lambda e: e.reduce_max(out=srow[0:1, 4:5], in_=row_sb[0:1, 0:128], axis=AX.X),
               waits=["D:mfe2"], mark="D:gp", phase=ph)
        sch.dma("sy", lambda e: e.dma_start(out=d_cc["in2"].ap(), in_=srow[0:1, 4:6]),
                waits=["D:gp"], mark="G:cc2i", phase=ph)
        sch.cc(lambda e: e.collective_compute(
            "AllReduce", ALU.max, replica_groups=[list(range(8))],
            ins=[d_cc["in2"].ap().opt()], outs=[d_cc["out2"].ap().opt()]),
            waits=["G:cc2i"], mark="C:2", phase=ph)
        sch.dma("sy", lambda e: e.dma_start(out=srow[0:1, 4:6], in_=d_cc["out2"].ap()),
                waits=["C:2"], mark="G:cc2o:a", phase=ph)
        sch.dma("sy", lambda e: e.dma_start(out=srow[0:1, 4:6], in_=d_cc["out2"].ap()),
                waits=["G:cc2o:a"], mark="G:cc2o", phase=ph)
        sch.op("dv", lambda e: e.tensor_scalar(
            out=srow[0:1, 6:7], in0=srow[0:1, 4:5], scalar1=srow[0:1, 9:10], scalar2=None,
            op0=ALU.mult), waits=["G:cc2o", "S:sigp"], mark="D:tp", phase=ph)
        sch.op("dv", lambda e: e.tensor_scalar(
            out=srow[0:1, 7:8], in0=srow[0:1, 6:7], scalar1=1.0 / 127.0, scalar2=1.0 / 127.0,
            op0=ALU.mult, op1=ALU.mult), waits=["D:tp"], mark="D:rqp", phase=ph)
        sch.op("pe", lambda e: e.matmul(pb[1][:, 0:2], ones32[0:1, :], srow[0:1, 6:8],
                                        start=True, stop=True),
               waits=["D:rqp"], mark="T:k1b", phase=ph)
        sch.op("dv", lambda e: e.tensor_copy(out=bc2[:, 2:4], in_=pb[1][:, 0:2]),
               waits=["T:k1b"], mark="D:bce2", phase=ph)
        sch.op("dv", lambda e: e.reciprocal(out=bc2[:, 2:3], in_=bc2[:, 2:3]),
               waits=["D:bce2"], mark="D:rcb", phase=ph)
        sch.op("dv", lambda e: e.tensor_scalar(
            out=bc2[:, 2:3], in0=bc2[:, 2:3], scalar1=127.0, scalar2=None, op0=ALU.mult),
            waits=["D:rcb"], mark="D:bcb", phase=ph)
        sch.op("pe", lambda e: e.matmul(pb[2][0:HC, 0:128], rs_d[:, 0:HC], eyef[:],
                                        start=True, stop=True),
               mark="T:fld2", phase=ph)
        sch.op("dv", lambda e: e.tensor_copy(out=row_sb[0:HC, :], in_=pb[2][0:HC, 0:128]),
               waits=["T:fld2"], mark="D:fld2e", phase=ph)
        sch.dma("sc", lambda e: e.dma_start(
            out=d_rr["d"].rearrange("o (a b) -> (o a) b", b=128)[0:HC, :],
            in_=row_sb[0:HC, :]), waits=["D:fld2e", "G:iwd"], mark="V:rrd2", phase=ph)
        sch.dma("sy", lambda e: e.dma_start(out=sdbc[:], in_=d_rr["d"].partition_broadcast(128)),
                waits=["V:rrd2"], mark="G:bcd2:a", phase=ph)
        sch.dma("sy", lambda e: e.dma_start(out=sdbc[:], in_=d_rr["d"].partition_broadcast(128)),
                waits=["G:bcd2:a"], mark="G:bcd2", phase=ph)
        sch.op("dv", lambda e: e.tensor_scalar(
            out=sdbc[:], in0=sdbc[:], scalar1=bc2[:, 3:4], scalar2=None, op0=ALU.mult),
            waits=["G:bcd2", "D:bcb"], mark="D:sdbc", phase=ph)

        sch.evt["T:p1end"] = sch.evt["T:fld2"]
        sch.evt["S:p1end"] = sch.evt[f"S:sil:1:{ICN - 1}:3"]
        sch.evt["D:p1end"] = sch.evt["D:sdbc"]
        sch.evt["V:pall"] = ("scq", sch.cnt["scq"])

        sch.emit(block, sems, 1)
        b1.close()

        ph = 2
        b2 = contextlib.ExitStack()
        ent4 = b2.enter_context
        iwd_sb = ent4(nc.sbuf_tensor("iwd_sb", [128, ICN, H], BF16))
        ih = [ent4(nc.sbuf_tensor(f"ih{i}", [128, ICN, 128], BF16)) for i in range(2)]
        pblk = [ent4(nc.sbuf_tensor(f"pblk{i}", [128, ICN, 128], F32)) for i in range(2)]
        x1r2 = [ent4(nc.sbuf_tensor(f"x1r2{i}", [128, H], F32)) for i in range(2)]
        dsb = [ent4(nc.sbuf_tensor(f"dsb{i}", [128, H], F32)) for i in range(2)]
        r3b = ent4(nc.sbuf_tensor("r3b", [128, H], F32))
        qt2 = ent4(nc.sbuf_tensor("qt2", [128, ICN * 128], F32))
        outb = [ent4(nc.sbuf_tensor(f"outb{i}", [128, H], F32)) for i in range(2)]

        FENCE2 = ["T:p1end", "S:p1end", "D:p1end"]
        pend_o = []
        if DEBUG_TAPS or DEBUG_TAPS2:
            d_tih = nc.dram_tensor("tap_ih", [128, ICN * 128], BF16, kind="ExternalOutput").ap()
            d_tdsb = nc.dram_tensor("tap_dsb", [128, H], F32, kind="ExternalOutput").ap()
            d_tdsc = nc.dram_tensor("tap_dsc", [128, H], F32, kind="ExternalOutput").ap()
            d_tr3 = nc.dram_tensor("tap_r3", [128, H], F32, kind="ExternalOutput").ap()
            d_tsd = nc.dram_tensor("tap_sd", [128, H], F32, kind="ExternalOutput").ap()
        sch.dma("sy", lambda e: e.dma_start(out=iwd_sb[:], in_=v_iwd),
                waits=FENCE2 + ["G:iwd"], mark="L:iwd", phase=ph)

        for blk in range(NW):
            par = blk % 2
            if blk >= 2:
                wpl = [f"D:ih:{blk - 2}"]
            else:
                wpl = list(FENCE2) + ["V:pall"]
            sch.dma("sy", lambda e, blk=blk, par=par: e.dma_start(
                out=pblk[par][:], in_=v_p[:, :, blk * 128:(blk + 1) * 128]),
                waits=wpl, mark=f"L:p:{blk}", phase=ph)
            wxl = [f"D:r3:{blk - 2}"] if blk >= 2 else list(FENCE2) + ["V:pall"]
            sch.dma("sy", lambda e, blk=blk, par=par: e.dma_start(
                out=x1r2[par][:], in_=v_x1[:, blk, :]), waits=wxl, mark=f"L:xb:{blk}", phase=ph)
            if blk == NW - 1:
                sch.dma("sy", lambda e, blk=blk, par=par: e.dma_start(
                    out=x1r2[par][:], in_=v_x1[:, blk, :]), mark=f"L:xb:{blk}", phase=ph)

            wq0 = [f"L:p:{blk}", f"L:xb:{blk}"]
            if blk == 0:
                wq0.append("D:bcb")
            if (DEBUG_TAPS or DEBUG_TAPS2) and blk == 2:
                wq0 += ["TAP:ih", "TAP:dsb", "TAP:dsc", "TAP:r3", "TAP:sd"]
            sch.op("dv", lambda e, par=par: e.tensor_scalar(
                out=qt2[:], in0=pblk[par][:], scalar1=bc2[:, 2:3], scalar2=MAGIC,
                op0=ALU.mult, op1=ALU.add), waits=wq0, phase=ph)
            sch.op("dv", lambda e: e.tensor_scalar(
                out=qt2[:], in0=qt2[:], scalar1=-MAGIC, scalar2=-128.0,
                op0=ALU.add, op1=ALU.max), phase=ph)
            wih = [f"T:d:{blk - 2}"] if blk >= 2 else []
            sch.op("dv", lambda e, par=par: e.tensor_scalar(
                out=ih[par][:], in0=qt2[:], scalar1=127.0, scalar2=None, op0=ALU.min),
                waits=wih, mark=f"D:ih:{blk}", phase=ph)

            wd0 = [f"D:ih:{blk}"]
            sch.push(f"D:ih:{blk}", "L:iwd")
            if blk >= 2:
                sch.push(f"D:ih:{blk}", f"S:de:{blk - 2}")
            first = True
            for half2 in range(2):
                for ic in range(ICN):
                    sch.op("pe", lambda e, half2=half2, ic=ic, par=par: e.matmul(
                        pb[2 * par + half2][:, 0:512], ih[par][:, ic, :],
                        iwd_sb[:, ic, half2 * 512:(half2 + 1) * 512],
                        start=(ic == 0), stop=(ic == ICN - 1)),
                        waits=(wd0 if first else []),
                        mark=(f"T:d:{blk}" if (half2 == 1 and ic == ICN - 1) else None),
                        phase=ph)
                    first = False
            while len(pend_o) > 1:
                pfn, pwaits, pmk = pend_o.pop(0)
                sch.dma("sc", pfn, waits=pwaits, mark=pmk, phase=ph)
            wde = [f"T:d:{blk}"]
            if blk >= 2:
                wde.append(f"D:r3:{blk - 2}")
            sch.op("sc", lambda e, par=par: e.copy(out=dsb[par][:, 0:512],
                                                   in_=pb[2 * par][:, 0:512]),
                   waits=wde, phase=ph)
            sch.op("sc", lambda e, par=par: e.copy(out=dsb[par][:, 512:1024],
                                                   in_=pb[2 * par + 1][:, 0:512]),
                   mark=f"S:de:{blk}", phase=ph)
            wsc = [f"S:de:{blk}"]
            if blk == 0:
                wsc.append("D:sdbc")
                if DEBUG_TAPS or DEBUG_TAPS2:
                    wsc.append("TAP:dsb")
            sch.op("dv", lambda e, par=par: e.tensor_tensor(
                out=dsb[par][:], in0=dsb[par][:], in1=sdbc[:], op=ALU.mult),
                waits=wsc, phase=ph)
            wr3 = [f"L:xb:{blk}"]
            if blk < NW - 1:
                wr3.append(f"L:p:{blk + 1}")
            sch.op("dv", lambda e, par=par: e.scalar_tensor_tensor(
                out=r3b[:], in0=dsb[par][:], scalar=1.0, in1=x1r2[par][:],
                op0=ALU.mult, op1=ALU.add, accum_out=sl(par, 5)),
                waits=wr3, mark=f"D:r3:{blk}", phase=ph)
            if (DEBUG_TAPS or DEBUG_TAPS2) and blk == 0:
                sch.dma("sy", lambda e: e.dma_start(out=d_tih, in_=ih[0][:]),
                        waits=["D:ih:0"], mark="TAP:ih", phase=ph)
                sch.dma("sy", lambda e: e.dma_start(out=d_tdsb, in_=dsb[0][:]),
                        waits=["S:de:0"], mark="TAP:dsb", phase=ph)
                sch.dma("sy", lambda e: e.dma_start(out=d_tdsc, in_=dsb[0][:]),
                        waits=["D:r3:0"], mark="TAP:dsc", phase=ph)
                sch.dma("sy", lambda e: e.dma_start(out=d_tr3, in_=r3b[:]),
                        waits=["D:r3:0"], mark="TAP:r3", phase=ph)
                sch.dma("sy", lambda e: e.dma_start(out=d_tsd, in_=sdbc[:]),
                        waits=["D:sdbc"], mark="TAP:sd", phase=ph)
            wsq3 = [f"D:r3:{blk}"]
            if (DEBUG_TAPS or DEBUG_TAPS2) and blk == 0:
                wsq3.append("TAP:dsc")
            sch.op("sc", lambda e, par=par: e.activation(
                out=dsb[par][:], in_=r3b[:], func=AF.Square, accum_out=sl(par, 6)),
                waits=wsq3, mark=f"S:sq3:{blk}", phase=ph)
            sch.op("dv", lambda e, par=par: e.tensor_scalar(
                out=sl(par, 7), in0=sl(par, 5), scalar1=1.0 / 1024.0, scalar2=None,
                op0=ALU.mult), waits=[f"S:sq3:{blk}"], mark=f"D:mn3:{blk}", phase=ph)
            sch.op("dv", lambda e, par=par: e.tensor_scalar(
                out=sl(par, 8), in0=sl(par, 7), scalar1=sl(par, 7), scalar2=None,
                op0=ALU.mult), waits=[f"D:mn3:{blk}"], mark=f"D:m23:{blk}", phase=ph)
            sch.op("dv", lambda e, par=par: e.tensor_scalar(
                out=sl(par, 8), in0=sl(par, 6), scalar1=1.0 / 1024.0, scalar2=sl(par, 8),
                op0=ALU.mult, op1=ALU.subtract),
                waits=[f"D:m23:{blk}"], mark=f"D:vr3:{blk}", phase=ph)
            sch.op("sc", lambda e, par=par: e.activation(
                out=sl(par, 9), in_=sl(par, 8), func=AF.Sqrt, bias=epsv[:]),
                waits=[f"D:vr3:{blk}"], mark=f"S:st3:{blk}", phase=ph)
            sch.op("dv", lambda e, par=par: e.reciprocal(out=sl(par, 9), in_=sl(par, 9)),
                   waits=[f"S:st3:{blk}"], mark=f"D:rc3:{blk}", phase=ph)
            wob = [f"D:rc3:{blk}"] + ([f"V:o:{blk - 2}"] if blk >= 2 else [])
            sch.op("dv", lambda e, par=par: e.tensor_scalar(
                out=outb[par][:], in0=r3b[:], scalar1=sl(par, 7), scalar2=sl(par, 9),
                op0=ALU.subtract, op1=ALU.mult), waits=wob, mark=f"D:ob:{blk}", phase=ph)
            pend_o.append((
                lambda e, blk=blk, par=par: e.dma_start(out=v_out[:, blk, :], in_=outb[par][:]),
                [f"D:ob:{blk}"], f"V:o:{blk}"))

        while pend_o:
            pfn, pwaits, pmk = pend_o.pop(0)
            sch.dma("sc", pfn, waits=pwaits, mark=pmk, phase=ph)

        if DEBUG_DUMPS:
            d_dbg1 = nc.dram_tensor("dbg_x1", [S, H], F32, kind="ExternalOutput").ap()
            d_dbg2 = nc.dram_tensor("dbg_p", [II, S], F32, kind="ExternalOutput").ap()
            d_dbg3 = nc.dram_tensor("dbg_iwg", [H, II], BF16, kind="ExternalOutput").ap()
            d_dbg4 = nc.dram_tensor("dbg_iwd", [II, H], BF16, kind="ExternalOutput").ap()
            d_dbg5 = nc.dram_tensor("dbg_sml", [128, 24], F32, kind="ExternalOutput").ap()
            d_dbg6 = nc.dram_tensor("dbg_srow", [1, 16], F32, kind="ExternalOutput").ap()
            sch.dma("sy", lambda e: e.dma_start(out=d_dbg1, in_=d_x1),
                    waits=[f"V:o:{NW - 1}"], mark="DBG1", phase=ph)
            sch.dma("sy", lambda e: e.dma_start(out=d_dbg2, in_=d_p), mark="DBG2", phase=ph)
            sch.dma("sy", lambda e: e.dma_start(out=d_dbg3, in_=d_iwg), mark="DBG3", phase=ph)
            sch.dma("sy", lambda e: e.dma_start(out=d_dbg4, in_=d_iwd), mark="DBG4", phase=ph)
            sch.dma("sy", lambda e: e.dma_start(out=d_dbg5, in_=sml[:]), mark="DBG5", phase=ph)
            sch.dma("sy", lambda e: e.dma_start(out=d_dbg6, in_=srow[:]), mark="DBG6", phase=ph)
            d_dbg7 = nc.dram_tensor("dbg_clip", [1, 2], F32, kind="ExternalOutput").ap()
            d_dbg8 = nc.dram_tensor("dbg_bc2", [128, 4], F32, kind="ExternalOutput").ap()
            sch.dma("sy", lambda e: e.dma_start(out=d_dbg7, in_=clip_sb[:]), mark="DBG7", phase=ph)
            sch.dma("sy", lambda e: e.dma_start(out=d_dbg8, in_=bc2[:]), mark="DBG8", phase=ph)
            sch.wait_only("sy", ["DBG8", "DBG7", "DBG6", "DBG5", "DBG4", "DBG3", "DBG2", "DBG1"], phase=ph)

        sch.wait_only("sy", [f"V:o:{NW - 1}"], phase=ph)

        sch.emit(block, sems, 2)
        b2.close()

    return nc


_NC_CACHE = []


def kernel(L0_attn_x, L0_attn_Wq, L0_attn_bq, L0_attn_Wk, L0_attn_bk, L0_attn_Wv,
           L0_attn_bv, L0_attn_gamma, L0_attn_beta, L0_ffn_clip_x, L0_ffn_clip_down,
           L0_ffn_w_gate, L0_ffn_w_up, L0_ffn_w_down):
    x = np.ascontiguousarray(np.asarray(L0_attn_x, dtype=np.float32))
    Wq = np.asarray(L0_attn_Wq, np.float32)
    Wk = np.asarray(L0_attn_Wk, np.float32)
    Wv = np.asarray(L0_attn_Wv, np.float32)
    wg = np.ascontiguousarray(np.asarray(L0_ffn_w_gate, np.float32))
    wu = np.ascontiguousarray(np.asarray(L0_ffn_w_up, np.float32))
    wd = np.ascontiguousarray(np.asarray(L0_ffn_w_down, np.float32))

    common = {
        "wqT": np.ascontiguousarray(Wq.T),
        "wkT": np.ascontiguousarray(Wk.T),
        "wvT": np.ascontiguousarray(Wv.T),
        "bq_pc": np.ascontiguousarray(np.asarray(L0_attn_bq, np.float32).reshape(HC, 128).T),
        "bk_pc": np.ascontiguousarray(np.asarray(L0_attn_bk, np.float32).reshape(HC, 128).T),
        "bv_row": np.ascontiguousarray(np.asarray(L0_attn_bv, np.float32).reshape(1, H)),
        "gam_row": np.ascontiguousarray(np.asarray(L0_attn_gamma, np.float32).reshape(1, H)),
        "bet_row": np.ascontiguousarray(np.asarray(L0_attn_beta, np.float32).reshape(1, H)),
        "clip2": np.ascontiguousarray(
            np.concatenate([np.asarray(L0_ffn_clip_x, np.float32).reshape(1),
                            np.asarray(L0_ffn_clip_down, np.float32).reshape(1)]).reshape(1, 2)),
        "w_gate": wg, "w_gateT": np.ascontiguousarray(wg.T),
        "w_up": wu, "w_upT": np.ascontiguousarray(wu.T),
        "w_down": wd, "w_downT": np.ascontiguousarray(wd.T),
    }

    if not _NC_CACHE:
        _NC_CACHE.append(build_program())
    nc = _NC_CACHE[0]

    in_maps = []
    for c in range(8):
        m = dict(common)
        m["x_tm"] = np.ascontiguousarray(x[c])
        m["x_fm"] = np.ascontiguousarray(x[c].T)
        in_maps.append(m)

    res = run_bass_kernel_spmd(nc, in_maps, core_ids=list(range(8)))
    out = np.stack([np.asarray(r["out"], np.float32) for r in res.results], axis=0)
    return out


# version 22
# speedup vs baseline: 1.0041x; 1.0041x over previous
import numpy as np
import ml_dtypes
from collections import defaultdict
import contextlib

import concourse.bass as bass
import concourse.mybir as mybir
from concourse.bass_utils import run_bass_kernel_spmd

DEBUG_DUMPS = False
DEBUG_TAPS2 = False
DEBUG_TAPS = False

F32 = mybir.dt.float32
BF16 = mybir.dt.bfloat16
AF = mybir.ActivationFunctionType
ALU = mybir.AluOpType
AX = mybir.AxisListType

B, S, H, II = 8, 4096, 1024, 2816
WIN = 128
NW = S // WIN
NT = NW // 2
HC = H // 128
ICN = II // 128
MAGIC = 12582912.0
EPS = 1e-5
BIGB = 1024.0
NEG = -1.0e30


class Sched:

    QSEM = {"pe": "spe", "sc": "ssc", "dv": "sdv", "pl": "spl", "sy": "ssy"}
    DSEM = {"sc": "scq", "dv": "svq", "pl": "sgq", "sy": "ssy2"}

    def __init__(self):
        self.items = defaultdict(list)
        self.evt = {}
        self.cnt = defaultdict(int)
        self.reg = {}

    def _add(self, stream, fn, waits, m, phase, mark):
        it = [fn, list(waits), m, phase]
        self.items[stream].append(it)
        if mark is not None:
            self.reg[mark] = it
        return mark

    def op(self, stream, fn, waits=(), mark=None, phase=0):
        semn = self.QSEM[stream]
        m = None
        if mark is not None:
            self.cnt[semn] += 1
            self.evt[mark] = (semn, self.cnt[semn])
            m = (semn, 1)
        return self._add(stream, fn, waits, m, phase, mark)

    def dma(self, stream, fn, waits=(), mark=None, phase=0, sem=None):
        semn = sem if sem is not None else self.DSEM[stream]
        self.cnt[semn] += 16
        if mark is not None:
            self.evt[mark] = (semn, self.cnt[semn])
        return self._add(stream, fn, waits, (semn, 16), phase, mark)

    def cc(self, fn, waits=(), mark=None, phase=0):
        self.cnt["scc"] += 1
        if mark is not None:
            self.evt[mark] = ("scc", self.cnt["scc"])
        self._add("pl", fn, waits, ("scc", 1), phase, mark)

    def push(self, mark, *extra_waits):
        self.reg[mark][1].extend(extra_waits)

    def latest(self, *keys):
        evs = [(self.evt[k], k) for k in keys]
        sems = {e[0][0] for e in evs}
        assert len(sems) == 1, f"latest() on mixed sems: {keys}"
        return max(evs)[1]

    def wait_only(self, stream, waits, phase=0):
        assert stream != "pe"
        self.items[stream].append([None, list(waits), None, phase])

    def emit(self, block, sems, phase):
        streams = {"pe": "tensor", "sc": "scalar", "dv": "vector",
                   "pl": "gpsimd", "sy": "sync"}
        for stream, attr in streams.items():
            its = [it for it in self.items[stream] if it[3] == phase]
            if not its:
                continue

            def run(e, its=its, stream=stream):
                for fn, waits, m, _ in its:
                    if fn is None:
                        for wk in waits:
                            sn, cv = self.evt[wk]
                            e.wait_ge(sems[sn], cv)
                        continue
                    best = {}
                    for wk in waits:
                        sn, cv = self.evt[wk]
                        if cv > best.get(sn, -1):
                            best[sn] = cv
                    resolved = sorted(best.items())
                    if stream == "pe":
                        assert len(resolved) <= 1, f"PE item with waits {waits}"
                    for sn, cv in resolved[1:]:
                        e.wait_ge(sems[sn], cv)
                    ins = fn(e)
                    if resolved:
                        sn, cv = resolved[0]
                        ins._wait_ge(sems[sn], cv)
                    if m is not None:
                        sn, dv_ = m
                        if sn == "scc":
                            ins.then_inc(sems[sn])
                        else:
                            ins.then_inc(sems[sn], dv_)

            getattr(block, attr)(run)


def build_program():
    nc = bass.Bass("TRN2", target_bir_lowering=False, detect_race_conditions=False)

    d_x = nc.dram_tensor("x_tm", [S, H], F32, kind="ExternalInput").ap()
    d_xT = nc.dram_tensor("x_fm", [H, S], F32, kind="ExternalInput").ap()
    d_wqT = nc.dram_tensor("wqT", [H, H], F32, kind="ExternalInput").ap()
    d_wkT = nc.dram_tensor("wkT", [H, H], F32, kind="ExternalInput").ap()
    d_wvT = nc.dram_tensor("wvT", [H, H], F32, kind="ExternalInput").ap()
    d_bq = nc.dram_tensor("bq_pc", [128, HC], F32, kind="ExternalInput").ap()
    d_bk = nc.dram_tensor("bk_pc", [128, HC], F32, kind="ExternalInput").ap()
    d_bv = nc.dram_tensor("bv_row", [1, H], F32, kind="ExternalInput").ap()
    d_gam = nc.dram_tensor("gam_row", [1, H], F32, kind="ExternalInput").ap()
    d_bet = nc.dram_tensor("bet_row", [1, H], F32, kind="ExternalInput").ap()
    d_clip = nc.dram_tensor("clip2", [1, 2], F32, kind="ExternalInput").ap()
    d_wg = nc.dram_tensor("w_gate", [II, H], F32, kind="ExternalInput").ap()
    d_wgT = nc.dram_tensor("w_gateT", [H, II], F32, kind="ExternalInput").ap()
    d_wu = nc.dram_tensor("w_up", [II, H], F32, kind="ExternalInput").ap()
    d_wuT = nc.dram_tensor("w_upT", [H, II], F32, kind="ExternalInput").ap()
    d_wd = nc.dram_tensor("w_down", [H, II], F32, kind="ExternalInput").ap()
    d_wdT = nc.dram_tensor("w_downT", [II, H], F32, kind="ExternalInput").ap()
    d_out = nc.dram_tensor("out", [S, H], F32, kind="ExternalOutput").ap()

    d_x1 = nc.dram_tensor("x1_scr", [S, H], F32, kind="Internal").ap()
    d_p = nc.dram_tensor("p_scr", [II, S], F32, kind="Internal").ap()
    d_iwg = nc.dram_tensor("iwg_scr", [H, II], BF16, kind="Internal").ap()
    d_iwu = nc.dram_tensor("iwu_scr", [H, II], BF16, kind="Internal").ap()
    d_iwd = nc.dram_tensor("iwd_scr", [II, H], BF16, kind="Internal").ap()
    d_rr = {w: nc.dram_tensor(f"rrow_{w}", [1, II if w != "d" else H], F32,
                              kind="Internal").ap() for w in ("g", "u", "d")}
    d_cc = {k: nc.dram_tensor(f"cc_{k}", [1, 2], F32, kind="Internal")
            for k in ("in1", "out1", "in2", "out2")}

    i_eyeb = nc.inline_tensor(np.eye(128, dtype=ml_dtypes.bfloat16), name="eye_bf").ap()
    i_eyef = nc.inline_tensor(np.eye(128, dtype=np.float32), name="eye_f32").ap()
    i_ones = nc.inline_tensor(np.ones((1, 128), dtype=np.float32), name="ones_f32").ap()
    i_onesb = nc.inline_tensor(np.ones((1, 128), dtype=ml_dtypes.bfloat16), name="ones_bf").ap()

    v_x = d_x.rearrange("(w p) h -> p w h", p=128)
    v_xT = d_xT.rearrange("(c p) t -> p c t", p=128)
    v_x1 = d_x1.rearrange("(w p) h -> p w h", p=128)
    v_out = d_out.rearrange("(w p) h -> p w h", p=128)
    v_p = d_p.rearrange("(c p) t -> p c t", p=128)
    v_wg = d_wg.rearrange("(c p) h -> p c h", p=128)
    v_wu = d_wu.rearrange("(c p) h -> p c h", p=128)
    v_wd = d_wd.rearrange("(c p) i -> p c i", p=128)
    v_wgT = d_wgT.rearrange("(c p) i -> p c i", p=128)
    v_wuT = d_wuT.rearrange("(c p) i -> p c i", p=128)
    v_wdT = d_wdT.rearrange("(c p) h -> p c h", p=128)
    v_iwg = d_iwg.rearrange("(c p) i -> p c i", p=128)
    v_iwu = d_iwu.rearrange("(c p) i -> p c i", p=128)
    v_iwd = d_iwd.rearrange("(c p) h -> p c h", p=128)

    sch = Sched()

    ctx = contextlib.ExitStack()
    with ctx:
        ent = ctx.enter_context

        semnames = ["spe", "ssc", "sdv", "spl", "ssy", "ssy2", "scq", "svq", "sgq",
                    "scc", "siwg", "siwu", "siwd", "sxb"]
        sems = {n: ent(nc.semaphore(n)) for n in semnames}

        pb = [ent(nc.psum_tensor(f"pb{i}", [128, 512], F32)) for i in range(8)]

        gb_bc = ent(nc.sbuf_tensor("gb_bc", [128, 2 * H], BF16))
        bvb = ent(nc.sbuf_tensor("bvb", [1, H], BF16))
        onesb = ent(nc.sbuf_tensor("onesb", [1, 128], BF16))
        ones32 = ent(nc.sbuf_tensor("ones32", [1, 128], F32))
        eyeb = ent(nc.sbuf_tensor("eyeb", [128, 128], BF16))
        eyef = ent(nc.sbuf_tensor("eyef", [128, 128], F32))
        bq_sb = ent(nc.sbuf_tensor("bq_sb", [128, HC], F32))
        bk_sb = ent(nc.sbuf_tensor("bk_sb", [128, HC], F32))
        clip_sb = ent(nc.sbuf_tensor("clip_sb", [1, 2], F32))
        rs_g = ent(nc.sbuf_tensor("rs_g", [128, ICN], F32))
        rs_u = ent(nc.sbuf_tensor("rs_u", [128, ICN], F32))
        rs_d = ent(nc.sbuf_tensor("rs_d", [128, HC], F32))
        sg_all = ent(nc.sbuf_tensor("sg_all", [128, ICN], F32))
        su_all = ent(nc.sbuf_tensor("su_all", [128, ICN], F32))
        mx_all = ent(nc.sbuf_tensor("mx_all", [128, NW], F32))
        mp_all = ent(nc.sbuf_tensor("mp_all", [128, 176], F32))
        srow = ent(nc.sbuf_tensor("srow", [1, 16], F32))
        bc2 = ent(nc.sbuf_tensor("bc2", [128, 4], F32))
        mvec = ent(nc.sbuf_tensor("mvec", [128, 2], F32))
        row_sb = ent(nc.sbuf_tensor("row_sb", [ICN, 128], F32))
        rqbc = ent(nc.sbuf_tensor("rqbc", [128, II], F32))
        sdbc = ent(nc.sbuf_tensor("sdbc", [128, H], F32))
        sml = ent(nc.sbuf_tensor("sml", [128, 24], F32))
        epsv = ent(nc.sbuf_tensor("epsv", [128, 1], F32))

        def sl(par, idx):
            return sml[:, par * 12 + idx: par * 12 + idx + 1]

        ph = 0
        a = contextlib.ExitStack()
        ent2 = a.enter_context
        wq_sb = ent2(nc.sbuf_tensor("wq_sb", [128, HC, H], BF16))
        wk_sb = ent2(nc.sbuf_tensor("wk_sb", [128, HC, H], BF16))
        wv_sb = ent2(nc.sbuf_tensor("wv_sb", [128, HC, H], BF16))
        xw = [ent2(nc.sbuf_tensor(f"xw{i}", [128, 2, H], F32)) for i in range(2)]
        xbf = [ent2(nc.sbuf_tensor(f"xbf{i}", [128, HC, 256], BF16)) for i in range(2)]
        qf = ent2(nc.sbuf_tensor("qf", [128, HC, 256], BF16))
        kf = ent2(nc.sbuf_tensor("kf", [128, HC, 256], BF16))
        vsb = [ent2(nc.sbuf_tensor(f"vsb{i}", [128, 2, H], BF16)) for i in range(2)]
        ssb = ent2(nc.sbuf_tensor("ssb", [128, 2, 128], F32))
        wrk = ent2(nc.sbuf_tensor("wrk", [128, 2, 128], F32))
        smb = ent2(nc.sbuf_tensor("smb", [128, 2, 128], F32))
        mskb = ent2(nc.sbuf_tensor("mskb", [128, 2, 128], F32))
        m8 = ent2(nc.sbuf_tensor("m8", [128, 8], F32))
        atn = ent2(nc.sbuf_tensor("atn", [128, 2, 128], BF16))
        atnT = ent2(nc.sbuf_tensor("atnT", [128, 2, 128], BF16))
        osb = ent2(nc.sbuf_tensor("osb", [128, H], F32))
        rsb = ent2(nc.sbuf_tensor("rsb", [128, H], F32))
        x1w = [ent2(nc.sbuf_tensor(f"x1w{i}", [128, H], F32)) for i in range(2)]
        wtl = [ent2(nc.sbuf_tensor(f"wtl{i}", [128, II], F32)) for i in range(5)]
        iw_st = [ent2(nc.sbuf_tensor(f"iwst{i}", [128, II], BF16)) for i in range(2)]

        def dup_sy(fn, mark):
            sch.dma("sy", fn, phase=ph)
            sch.dma("sy", fn, mark=mark, phase=ph)

        def dup_pl(fn, mark, waits=()):
            sch.dma("pl", fn, waits=list(waits), mark=mark + ":a", phase=ph)
            sch.dma("pl", fn, waits=[mark + ":a"], mark=mark, phase=ph)

        dup_sy(lambda e: e.dma_start(out=bq_sb[:], in_=d_bq), "L:bq")
        dup_sy(lambda e: e.dma_start(out=bk_sb[:], in_=d_bk), "L:bk")
        dup_sy(lambda e: e.dma_start(out=clip_sb[:], in_=d_clip), "L:clip")
        dup_sy(lambda e: e.dma_start(out=eyeb[:], in_=i_eyeb), "L:eyeb")
        dup_sy(lambda e: e.dma_start(out=eyef[:], in_=i_eyef), "L:eyef")
        dup_sy(lambda e: e.dma_start(out=ones32[:], in_=i_ones), "L:ones")
        dup_sy(lambda e: e.dma_start(out=onesb[:], in_=i_onesb), "L:onesb")
        dup_pl(lambda e: e.dma_start(out=gb_bc[:, 0:H], in_=d_gam.partition_broadcast(128)),
               "L:gam")
        dup_pl(lambda e: e.dma_start(out=gb_bc[:, H:2 * H], in_=d_bet.partition_broadcast(128)),
               "L:bet")
        dup_pl(lambda e: e.dma_start(out=wq_sb[:], in_=d_wqT.rearrange("(c p) o -> p c o", p=128)),
               "L:wq")
        dup_pl(lambda e: e.dma_start(out=wk_sb[:], in_=d_wkT.rearrange("(c p) o -> p c o", p=128)),
               "L:wk")
        dup_pl(lambda e: e.dma_start(out=wv_sb[:], in_=d_wvT.rearrange("(c p) o -> p c o", p=128)),
               "L:wv")
        dup_pl(lambda e: e.dma_start(out=bvb[:], in_=d_bv), "L:bv")

        for _k in ("L:wq", "L:wk", "L:wv", "L:bv", "L:gam", "L:bet"):
            sch.evt[_k] = ("sgq", sch.cnt["sgq"])

        def x_load(t):
            waits = [f"D:r2:{2 * (t - 2) + 1}"] if t >= 2 else []
            sch.dma("sy", lambda e, t=t: e.dma_start(out=xw[t % 2][:], in_=v_x[:, 2 * t:2 * t + 2, :]),
                    waits=waits, mark=f"L:x:{t}", phase=ph)
            waits2 = [f"L:xT:{t - 1}:a"] if t >= 1 else                 ["L:wq", "L:wk", "L:wv", "L:bv", "L:onesb", "L:eyeb"]
            if t >= 2:
                waits2.append(f"T:v:{2 * (t - 2) + 1}")
            sch.dma("pl", lambda e, t=t: e.dma_start(
                out=xbf[t % 2][:], in_=v_xT[:, :, 256 * t:256 * (t + 1)]),
                waits=waits2, mark=f"L:xT:{t}:a", phase=ph, sem="sxb")
            sch.evt[f"L:xT:{t}"] = sch.evt[f"L:xT:{t}:a"]
            if t in (0, 15):
                sch.dma("pl", lambda e, t=t: e.dma_start(
                    out=xbf[t % 2][:], in_=v_xT[:, :, 256 * t:256 * (t + 1)]),
                    waits=[f"L:xT:{t}:a"], mark=f"L:xT:{t}", phase=ph, sem="sxb")

        x_load(0)
        x_load(1)

        sch.op("sc", lambda e: e.activation(out=srow[0:1, 8:9], in_=clip_sb[0:1, 0:1], func=AF.Sigmoid),
               waits=["L:clip"], mark="S:sigx", phase=ph)
        sch.op("sc", lambda e: e.activation(out=srow[0:1, 9:10], in_=clip_sb[0:1, 1:2], func=AF.Sigmoid),
               mark="S:sigp", phase=ph)
        sch.op("dv", lambda e: e.memset(srow[0:1, 0:8], 0.0),
               waits=["L:eyef", "L:ones"], mark="D:srz", phase=ph)
        sch.op("dv", lambda e: e.memset(epsv[:], EPS), mark="D:eps", phase=ph)
        if DEBUG_TAPS:
            sch.op("dv", lambda e: e.memset(sml[:], 0.0), phase=ph)
            sch.op("dv", lambda e: e.memset(mskb[:], 0.0), phase=ph)
            sch.op("dv", lambda e: e.memset(smb[:], 0.0), phase=ph)
            sch.op("dv", lambda e: e.memset(wrk[:], 0.0), phase=ph)
            sch.op("dv", lambda e: e.memset(atn[:], 0.0), phase=ph)
            sch.op("dv", lambda e: e.memset(atnT[:], 0.0), phase=ph)

        WROT = 5
        wq_entries = []
        wq_state = {"li": 0, "ri": 0, "last_li": 0}

        def rs_stream(name, vsrc, ntile, rs_dst):
            for kk in range(ntile):
                bi = len(wq_entries) % WROT
                wq_entries.append(dict(
                    lf=lambda e, kk=kk, vsrc=vsrc, bi=bi: e.dma_start(
                        out=wtl[bi][:, 0:vsrc.shape[2]], in_=vsrc[:, kk, :]),
                    lm=f"Lw:rs{name}:{kk}",
                    cons=f"P:rs{name}:{kk}",
                    dv=(lambda e, kk=kk, vsrc=vsrc, rs_dst=rs_dst, bi=bi: e.reduce_max(
                        out=rs_dst[:, kk:kk + 1], in_=wtl[bi][:, 0:vsrc.shape[2]],
                        axis=AX.X, apply_absolute_value=True),
                        [f"Lw:rs{name}:{kk}"] + (["D:srz"] if (name == "g" and kk == 0) else []),
                        f"P:rs{name}:{kk}"),
                ))

        rs_stream("g", v_wg, ICN, rs_g)
        rs_stream("u", v_wu, ICN, rs_u)
        rs_stream("d", v_wd, HC, rs_d)

        def wq_drain():
            dv_hi = wq_state["last_li"]
            while wq_state["ri"] < dv_hi:
                ri = wq_state["ri"]
                ent_ = wq_entries[ri]
                for fn, ws, mk, sm in ent_.get("pl_pre", ()):
                    sch.dma("pl", fn, waits=list(ws), mark=mk, phase=ph, sem=sm)
                if ent_.get("pl_tt") is not None:
                    fn, ws, mk = ent_["pl_tt"]
                    sch.op("pl", fn, waits=list(ws), mark=mk, phase=ph)
                if ent_["dv"] is not None:
                    dvs = ent_["dv"]
                    if not isinstance(dvs, list):
                        dvs = [dvs]
                    for fn, ws, mk in dvs:
                        sch.op("dv", fn, waits=list(ws), mark=mk, phase=ph)
                if ent_.get("pl_st") is not None:
                    fn, ws, mk, sm = ent_["pl_st"]
                    sch.dma("pl", fn, waits=list(ws), mark=mk, phase=ph, sem=sm)
                wq_state["ri"] += 1
            n = 0
            while wq_state["li"] < len(wq_entries) and n < 3:
                idx = wq_state["li"]
                ws = []
                if idx >= WROT:
                    ws.append(wq_entries[idx - WROT]["cons"])
                sch.dma("sc", wq_entries[idx]["lf"], waits=ws,
                        mark=wq_entries[idx]["lm"], phase=ph)
                wq_state["li"] += 1
                n += 1
            wq_state["last_li"] = wq_state["li"]

        def quant_stream(name, vsrcT, ntile, ncols, viw, pre_pl=()):
            for kk in range(ntile):
                bi = len(wq_entries) % WROT
                w0 = [f"Lw:q{name}:{kk}", f"G:bc:{name}"]
                if kk < ntile - 1:
                    w0.append(f"Lw:q{name}:{kk + 1}")
                pl_tt = (lambda e, bi=bi, ncols=ncols: e.tensor_tensor(
                    out=wtl[bi][:, 0:ncols], in0=wtl[bi][:, 0:ncols],
                    in1=rqbc[:, 0:ncols], op=ALU.mult), w0, f"P:tt{name}:{kk}")
                wts_ts = [f"P:tt{name}:{kk}"]
                if kk >= 2:
                    wts_ts.append(f"G:iw{name}:{kk - 1}")
                dv_ops = [
                    (lambda e, kk=kk, bi=bi, ncols=ncols: e.tensor_scalar(
                        out=iw_st[kk % 2][:, 0:ncols], in0=wtl[bi][:, 0:ncols],
                        scalar1=MAGIC, scalar2=-MAGIC, op0=ALU.add, op1=ALU.add),
                        wts_ts, f"P:q{name}:{kk}"),
                ]
                stw = [f"P:q{name}:{kk}"]
                if kk > 0:
                    stw.append(f"G:iw{name}:{kk - 1}")
                pl_st = (lambda e, kk=kk, viw=viw, ncols=ncols: e.dma_start(
                    out=viw[:, kk, :], in_=iw_st[kk % 2][:, 0:ncols]),
                    stw, f"G:iw{name}:{kk}", f"siw{name}")
                wq_entries.append(dict(
                    lf=lambda e, kk=kk, vsrcT=vsrcT, ncols=ncols, bi=bi: e.dma_start(
                        out=wtl[bi][:, 0:ncols], in_=vsrcT[:, kk, :]),
                    lm=f"Lw:q{name}:{kk}",
                    cons=f"P:q{name}:{kk}",
                    dv=dv_ops,
                    pl_pre=(list(pre_pl) if kk == 0 else []),
                    pl_tt=pl_tt,
                    pl_st=pl_st,
                    st=None,
                ))

        b5_reader = ["L:eyeb"]

        def rs_flip(name, rs_src, nicol, prev_store, rs_done, bc_extra=()):
            sch.op("pe", lambda e, rs_src=rs_src, nicol=nicol: e.matmul(
                pb[5][0:nicol, 0:128], rs_src[:, 0:nicol], eyef[:], start=True, stop=True),
                waits=[sch.latest(rs_done, *b5_reader)], mark=f"T:fl{name}", phase=ph)
            evw = [f"T:fl{name}"] + ([prev_store] if prev_store else [])
            sch.op("dv", lambda e, nicol=nicol: e.tensor_scalar(
                out=row_sb[0:nicol, :], in0=pb[5][0:nicol, 0:128],
                scalar1=1.0 / 127.0, scalar2=None, op0=ALU.mult),
                waits=evw, mark=f"D:fle:{name}", phase=ph)
            b5_reader[:] = [f"D:fle:{name}"]
            sch.op("dv", lambda e, nicol=nicol: e.reciprocal(
                out=row_sb[0:nicol, :], in_=row_sb[0:nicol, :]),
                mark=f"D:flr:{name}", phase=ph)
            sch.dma("sc", lambda e, name=name, nicol=nicol: e.dma_start(
                out=d_rr[name].rearrange("o (a b) -> (o a) b", b=128)[0:nicol, :],
                in_=row_sb[0:nicol, :]), waits=[f"D:flr:{name}"],
                mark=f"V:rr:{name}", phase=ph)
            ncols = II if name != "d" else H
            bc1 = (lambda e, name=name, ncols=ncols: e.dma_start(
                out=rqbc[:, 0:ncols], in_=d_rr[name].partition_broadcast(128)),
                [f"V:rr:{name}"] + list(bc_extra), f"G:bc:{name}:a", None)
            bc2_ = (lambda e, name=name, ncols=ncols: e.dma_start(
                out=rqbc[:, 0:ncols], in_=d_rr[name].partition_broadcast(128)),
                [f"G:bc:{name}:a"], f"G:bc:{name}", None)
            return [bc1, bc2_]

        pend_sc = []
        for t in range(NT):
            if DEBUG_TAPS and t == 1:
                taps = [
                    ("xbf", xbf[0], BF16, "L:xT:0"),
                    ("qf", qf, BF16, "S:pje:0:7"),
                    ("kf", kf, BF16, "S:pje:0:15"),
                    ("o0", osb, F32, "S:oe:0"),
                    ("r0", rsb, F32, "D:r:0"),
                    ("y0", osb, F32, "D:r2:0"),
                    ("x1w0", x1w[0], F32, "D:x1:0"),
                    ("vsb", vsb[0], BF16, "S:ve:1"),
                    ("ssb", ssb, F32, "D:se:1"),
                    ("mskb", mskb, F32, "D:sm:1"),
                    ("smb", smb, F32, "D:sm:1"),
                    ("wrk", wrk, F32, "S:ex:1"),
                    ("atn", atn, BF16, "D:at:1"),
                    ("atnT", atnT, BF16, "D:ae:1"),
                    ("sml", sml, F32, "D:x1:1"),
                ]
                for nm, tens, dtp, wk in taps:
                    free = 1
                    for d_ in tens.shape[1:]:
                        free *= d_
                    dbg = nc.dram_tensor(f"tap_{nm}", [128, free], dtp,
                                         kind="ExternalOutput").ap()
                    sch.dma("sy", lambda e, tens=tens, dbg=dbg: e.dma_start(
                        out=dbg, in_=tens[:]), waits=[wk], mark=f"TAP:{nm}", phase=ph)

            if t == 6:
                bcl = rs_flip("g", rs_g, ICN, None, f"P:rsg:{ICN - 1}")
                quant_stream("g", v_wgT, HC, II, v_iwg, pre_pl=bcl)
            if t == 9:
                bcl = rs_flip("u", rs_u, ICN, "V:rr:g", f"P:rsu:{ICN - 1}",
                              bc_extra=[f"P:qg:{HC - 1}"])
                quant_stream("u", v_wuT, HC, II, v_iwu, pre_pl=bcl)
            if t == 12:
                bcl = rs_flip("d", rs_d, HC, "V:rr:u", f"P:rsd:{HC - 1}",
                              bc_extra=[f"P:qu:{HC - 1}"])
                quant_stream("d", v_wdT, ICN, H, v_iwd, pre_pl=bcl)
            wq_drain()
            if t + 2 < NT:
                if DEBUG_TAPS and t == 0:
                    pass
                elif DEBUG_TAPS and t == 1:
                    x_load(2)
                    x_load(3)
                else:
                    x_load(t + 2)

            for wi, (wsb, bsb, dst) in enumerate(((wq_sb, bq_sb, qf), (wk_sb, bk_sb, kf))):
                for oc in range(HC):
                    g = wi * 8 + oc
                    bank = pb[g % 2]
                    if g == 0:
                        w0 = [f"L:xT:{t}"]
                    elif g == 1:
                        w0 = [f"S:pje:{t - 1}:15"] if t > 0 else []
                    else:
                        w0 = [f"S:pje:{t}:{g - 2}"]
                    for hc in range(HC):
                        sch.op("pe", lambda e, bank=bank, wsb=wsb, oc=oc, hc=hc, t=t: e.matmul(
                            bank[:, 0:256], wsb[:, hc, oc * 128:(oc + 1) * 128],
                            xbf[t % 2][:, hc, :], start=(hc == 0), stop=(hc == HC - 1)),
                            waits=(w0 if hc == 0 else []),
                            mark=(f"T:pj:{t}:{g}" if hc == HC - 1 else None), phase=ph)
                    wE = [f"T:pj:{t}:{g}"]
                    if g == 0:
                        wE.append("L:bq")
                        if t == 0:
                            wE.append("L:bk")
                    sch.op("sc", lambda e, bank=bank, dst=dst, bsb=bsb, oc=oc, t=t: e.activation(
                        out=dst[:, oc, :], in_=bank[:, 0:256], func=AF.Identity,
                        bias=bsb[:, oc:oc + 1], scale=1.0),
                        waits=wE, mark=f"S:pje:{t}:{g}", phase=ph)
            sch.evt[f"T:pj:{t}"] = sch.evt[f"T:pj:{t}:15"]

            for j in range(2):
                if j == 1:
                    wq_drain()
                w = 2 * t + j
                par = w % 2

                wv0 = [f"S:ve:{w - 1}"] if w >= 1 else []
                for half in range(2):
                    for hc in range(HC):
                        sch.op("pe", lambda e, half=half, hc=hc, t=t, j=j: e.matmul(
                            pb[2 + half][:, 0:512],
                            xbf[t % 2][:, hc, j * 128:(j + 1) * 128],
                            wv_sb[:, hc, half * 512:(half + 1) * 512],
                            start=(hc == 0), stop=False),
                            waits=(wv0 if (half == 0 and hc == 0) else []), phase=ph)
                    sch.op("pe", lambda e, half=half: e.matmul(
                        pb[2 + half][:, 0:512], onesb[0:1, :],
                        bvb[0:1, half * 512:(half + 1) * 512], start=False, stop=True),
                        mark=(f"T:v:{w}" if half == 1 else None), phase=ph)
                sch.op("sc", lambda e, t=t, j=j: e.copy(out=vsb[t % 2][:, j, 0:512],
                                                        in_=pb[2][:, 0:512]),
                       waits=[f"T:v:{w}"], phase=ph)
                sch.op("sc", lambda e, t=t, j=j: e.copy(out=vsb[t % 2][:, j, 512:1024],
                                                        in_=pb[3][:, 0:512]),
                       mark=f"S:ve:{w}", phase=ph)

                ws0 = [f"S:pje:{t}:15"] if j == 0 else [f"D:se:{w - 1}"]
                if j == 0 and w >= 1:
                    sch.push(f"S:pje:{t}:15", f"D:se:{w - 1}")
                for oc in range(HC):
                    sch.op("pe", lambda e, oc=oc, j=j: e.matmul(
                        pb[4][:, 0:128], qf[:, oc, j * 128:(j + 1) * 128],
                        kf[:, oc, j * 128:(j + 1) * 128],
                        start=(oc == 0), stop=(oc == HC - 1)),
                        waits=(ws0 if oc == 0 else []),
                        mark=(f"T:s:{w}" if oc == HC - 1 else None), phase=ph)
                wse = [f"T:s:{w}"]
                if w >= 2:
                    wse.append(f"D:sm:{w - 2}")
                sch.op("dv", lambda e, j=j: e.tensor_scalar(
                    out=ssb[:, j, :], in0=pb[4][:, 0:128], scalar1=0.03125,
                    scalar2=None, op0=ALU.mult),
                    waits=wse, mark=f"D:se:{w}", phase=ph)
                wtk0 = [f"S:ex:{w - 2}"] if w >= 2 else []
                sch.op("dv", lambda e, j=j: e.tensor_copy(
                    out=wrk[:, j, :], in_=ssb[:, j, :]),
                    waits=wtk0, mark=f"D:tk:{w}:0", phase=ph)
                tki = 1
                for it in range(4):
                    sch.op("dv", lambda e, j=j: e.max(out=m8[:], in_=wrk[:, j, :]),
                           waits=[f"D:tk:{w}:{tki - 1}"], mark=f"D:tk:{w}:{tki}", phase=ph)
                    tki += 1
                    if it < 3:
                        sch.op("dv", lambda e, j=j: e.match_replace(
                            out=wrk[:, j, :], in_to_replace=m8[:], in_values=wrk[:, j, :],
                            imm_value=NEG),
                            waits=[f"D:tk:{w}:{tki - 1}"], mark=f"D:tk:{w}:{tki}", phase=ph)
                        tki += 1
                sch.op("dv", lambda e, par=par: e.tensor_copy(out=sl(par, 0), in_=m8[:, 7:8]),
                       waits=[f"D:tk:{w}:{tki - 1}"], mark=f"D:tk:{w}:{tki}", phase=ph)
                sch.op("dv", lambda e, j=j, par=par: e.reduce_max(
                    out=sl(par, 1), in_=ssb[:, j, :], axis=AX.X), mark=f"D:rx:{w}", phase=ph)
                sch.op("dv", lambda e, par=par: e.tensor_scalar(
                    out=sl(par, 2), in0=sl(par, 1), scalar1=BIGB, scalar2=-1.0,
                    op0=ALU.add, op1=ALU.mult), waits=[f"D:rx:{w}"], phase=ph)
                sch.op("dv", lambda e, j=j, par=par: e.tensor_scalar(
                    out=mskb[:, j, :], in0=ssb[:, j, :], scalar1=sl(par, 0), scalar2=None,
                    op0=ALU.is_ge), waits=[f"D:tk:{w}:8"], phase=ph)
                sch.op("dv", lambda e, j=j: e.scalar_tensor_tensor(
                    out=smb[:, j, :], in0=ssb[:, j, :], scalar=BIGB, in1=mskb[:, j, :],
                    op0=ALU.add, op1=ALU.mult), mark=f"D:sm:{w}", phase=ph)
                while pend_sc:
                    pfn, pwaits, pmk = pend_sc.pop(0)
                    sch.dma("sc", pfn, waits=pwaits, mark=pmk, phase=ph)
                sch.op("sc", lambda e, j=j, par=par: e.activation(
                    out=wrk[:, j, :], in_=smb[:, j, :], func=AF.Exp, bias=sl(par, 2),
                    scale=1.0, accum_out=sl(par, 3)),
                    waits=[f"D:sm:{w}"], mark=f"S:ex:{w}", phase=ph)
                sch.op("dv", lambda e, par=par: e.reciprocal(out=sl(par, 4), in_=sl(par, 3)),
                       waits=[f"S:ex:{w}"], mark=f"D:rc0:{w}", phase=ph)
                wat = [f"D:rc0:{w}"] + ([f"T:aT:{w - 2}"] if w >= 2 else [])
                sch.op("dv", lambda e, j=j, par=par: e.tensor_scalar(
                    out=atn[:, j, :], in0=wrk[:, j, :], scalar1=sl(par, 4), scalar2=None,
                    op0=ALU.mult), waits=wat, mark=f"D:at:{w}", phase=ph)

                sch.op("pe", lambda e, j=j: e.matmul(
                    pb[5][:, 0:128], atn[:, j, :], eyeb[:], start=True, stop=True),
                    waits=[f"D:at:{w}"], mark=f"T:aT:{w}", phase=ph)
                wae = [f"T:aT:{w}"]
                if w >= 2:
                    wae.append(f"T:o:{w - 2}")
                if w >= 1:
                    wae.append(f"S:oe:{w - 1}")
                sch.op("dv", lambda e, j=j: e.tensor_copy(out=atnT[:, j, :], in_=pb[5][:, 0:128]),
                       waits=wae, mark=f"D:ae:{w}", phase=ph)
                b5_reader[:] = [f"D:ae:{w}"]

                wo0 = [f"D:ae:{w}"]
                for half in range(2):
                    sch.op("pe", lambda e, half=half, t=t, j=j: e.matmul(
                        pb[6 + half][:, 0:512], atnT[:, j, :],
                        vsb[t % 2][:, j, half * 512:(half + 1) * 512], start=True, stop=True),
                        waits=(wo0 if half == 0 else []),
                        mark=(f"T:o:{w}" if half == 1 else None), phase=ph)
                sch.op("sc", lambda e: e.copy(out=osb[:, 0:512], in_=pb[6][:, 0:512]),
                       waits=[f"T:o:{w}"], phase=ph)
                sch.op("sc", lambda e: e.copy(out=osb[:, 512:1024], in_=pb[7][:, 0:512]),
                       mark=f"S:oe:{w}", phase=ph)

                wlk = [f"S:oe:{w}"]
                if DEBUG_TAPS and w == 0:
                    wlk.append("TAP:o0")
                sch.op("dv", lambda e: e.scalar_tensor_tensor(
                    out=osb[:], in0=osb[:], scalar=0.01, in1=osb[:], op0=ALU.mult, op1=ALU.max),
                    waits=wlk, phase=ph)
                wr0 = [f"L:x:{t}"] if j == 0 else []
                sch.op("dv", lambda e, t=t, j=j, par=par: e.scalar_tensor_tensor(
                    out=rsb[:], in0=osb[:], scalar=1.0, in1=xw[t % 2][:, j, :],
                    op0=ALU.mult, op1=ALU.add, accum_out=sl(par, 5)),
                    waits=wr0, mark=f"D:r:{w}", phase=ph)
                wsq1 = [f"D:r:{w}"]
                if DEBUG_TAPS and w == 0:
                    wsq1.append("TAP:r0")
                sch.op("sc", lambda e, par=par: e.activation(
                    out=osb[:], in_=rsb[:], func=AF.Square, accum_out=sl(par, 6)),
                    waits=wsq1, mark=f"S:sq1:{w}", phase=ph)
                sch.op("dv", lambda e, par=par: e.tensor_scalar(
                    out=sl(par, 7), in0=sl(par, 5), scalar1=1.0 / 1024.0, scalar2=None,
                    op0=ALU.mult), waits=[f"S:sq1:{w}"], mark=f"D:mn1:{w}", phase=ph)
                sch.op("dv", lambda e, par=par: e.tensor_scalar(
                    out=sl(par, 8), in0=sl(par, 7), scalar1=sl(par, 7), scalar2=None,
                    op0=ALU.mult), waits=[f"D:mn1:{w}"], mark=f"D:m21:{w}", phase=ph)
                sch.op("dv", lambda e, par=par: e.tensor_scalar(
                    out=sl(par, 8), in0=sl(par, 6), scalar1=1.0 / 1024.0, scalar2=sl(par, 8),
                    op0=ALU.mult, op1=ALU.subtract),
                    waits=[f"D:m21:{w}"], mark=f"D:vr1:{w}", phase=ph)
                sch.op("sc", lambda e, par=par: e.activation(
                    out=sl(par, 9), in_=sl(par, 8), func=AF.Sqrt, bias=epsv[:]),
                    waits=[f"D:vr1:{w}"], mark=f"S:st1:{w}", phase=ph)
                sch.op("dv", lambda e, par=par: e.reciprocal(out=sl(par, 9), in_=sl(par, 9)),
                       waits=[f"S:st1:{w}"], mark=f"D:rc1:{w}", phase=ph)
                sch.op("dv", lambda e, par=par: e.tensor_scalar(
                    out=osb[:], in0=rsb[:], scalar1=sl(par, 7), scalar2=sl(par, 9),
                    op0=ALU.subtract, op1=ALU.mult), waits=[f"D:rc1:{w}"], phase=ph)
                wg0 = ["L:gam", "L:bet"] if w == 0 else []
                sch.op("dv", lambda e: e.tensor_tensor(
                    out=osb[:], in0=osb[:], in1=gb_bc[:, 0:H], op=ALU.mult), waits=wg0, phase=ph)
                sch.op("dv", lambda e: e.tensor_tensor(
                    out=osb[:], in0=osb[:], in1=gb_bc[:, H:2 * H], op=ALU.add), phase=ph)
                sch.op("dv", lambda e, t=t, j=j, par=par: e.scalar_tensor_tensor(
                    out=rsb[:], in0=osb[:], scalar=1.0, in1=xw[t % 2][:, j, :],
                    op0=ALU.mult, op1=ALU.add, accum_out=sl(par, 5)),
                    mark=f"D:r2:{w}", phase=ph)
                wsq2 = [f"D:r2:{w}"]
                if DEBUG_TAPS and w == 0:
                    wsq2.append("TAP:y0")
                sch.op("sc", lambda e, par=par: e.activation(
                    out=osb[:], in_=rsb[:], func=AF.Square, accum_out=sl(par, 6)),
                    waits=wsq2, mark=f"S:sq2:{w}", phase=ph)
                sch.op("dv", lambda e, par=par: e.tensor_scalar(
                    out=sl(par, 7), in0=sl(par, 5), scalar1=1.0 / 1024.0, scalar2=None,
                    op0=ALU.mult), waits=[f"S:sq2:{w}"], mark=f"D:mn2:{w}", phase=ph)
                sch.op("dv", lambda e, par=par: e.tensor_scalar(
                    out=sl(par, 8), in0=sl(par, 7), scalar1=sl(par, 7), scalar2=None,
                    op0=ALU.mult), waits=[f"D:mn2:{w}"], mark=f"D:m22:{w}", phase=ph)
                sch.op("dv", lambda e, par=par: e.tensor_scalar(
                    out=sl(par, 8), in0=sl(par, 6), scalar1=1.0 / 1024.0, scalar2=sl(par, 8),
                    op0=ALU.mult, op1=ALU.subtract),
                    waits=[f"D:m22:{w}"], mark=f"D:vr2:{w}", phase=ph)
                sch.op("sc", lambda e, par=par: e.activation(
                    out=sl(par, 9), in_=sl(par, 8), func=AF.Sqrt, bias=epsv[:]),
                    waits=[f"D:vr2:{w}"], mark=f"S:st2:{w}", phase=ph)
                sch.op("dv", lambda e, par=par: e.reciprocal(out=sl(par, 9), in_=sl(par, 9)),
                       waits=[f"S:st2:{w}"], mark=f"D:rc2:{w}", phase=ph)
                wx1 = [f"D:rc2:{w}"] + ([f"V:x1:{w - 2}"] if w >= 2 else [])
                sch.op("dv", lambda e, par=par: e.tensor_scalar(
                    out=x1w[par][:], in0=rsb[:], scalar1=sl(par, 7), scalar2=sl(par, 9),
                    op0=ALU.subtract, op1=ALU.mult), waits=wx1, phase=ph)
                sch.op("dv", lambda e, w=w, par=par: e.reduce_max(
                    out=mx_all[:, w:w + 1], in_=x1w[par][:], axis=AX.X,
                    apply_absolute_value=True), mark=f"D:x1:{w}", phase=ph)
                pend_sc.append((
                    lambda e, w=w, par=par: e.dma_start(out=v_x1[:, w, :], in_=x1w[par][:]),
                    [f"D:x1:{w}"], f"V:x1:{w}"))

        while pend_sc:
            pfn, pwaits, pmk = pend_sc.pop(0)
            sch.dma("sc", pfn, waits=pwaits, mark=pmk, phase=ph)
        while wq_state["li"] < len(wq_entries) or wq_state["ri"] < len(wq_entries):
            wq_drain()
        assert sch.cnt["siwg"] == 16 * HC and sch.cnt["siwu"] == 16 * HC
        assert sch.cnt["siwd"] == 16 * ICN
        sch.evt["G:iwg"] = ("siwg", 16 * HC)
        sch.evt["G:iwu"] = ("siwu", 16 * HC)
        sch.evt["G:iwd"] = ("siwd", 16 * ICN)

        sch.op("dv", lambda e: e.reduce_max(out=mvec[:, 0:1], in_=mx_all[:], axis=AX.X),
               waits=[f"D:x1:{NW - 1}"], mark="D:mxv", phase=ph)
        sch.op("pe", lambda e: e.matmul(pb[5][0:1, 0:128], mvec[:, 0:1], eyef[:],
                                        start=True, stop=True),
               waits=["D:mxv"], mark="T:mfl", phase=ph)
        sch.op("dv", lambda e: e.tensor_copy(out=row_sb[0:1, :], in_=pb[5][0:1, 0:128]),
               waits=["T:mfl", "V:rr:d"], mark="D:mfe", phase=ph)
        b5_reader[:] = ["D:mfe"]
        sch.op("dv", lambda e: e.reduce_max(out=srow[0:1, 0:1], in_=row_sb[0:1, 0:128], axis=AX.X),
               waits=["D:srz", "D:mfe"], mark="D:gx", phase=ph)
        sch.dma("sy", lambda e: e.dma_start(out=d_cc["in1"].ap(), in_=srow[0:1, 0:2]),
                waits=["D:gx"], mark="G:cc1i", phase=ph)
        sch.cc(lambda e: e.collective_compute(
            "AllReduce", ALU.max, replica_groups=[list(range(8))],
            ins=[d_cc["in1"].ap().opt()], outs=[d_cc["out1"].ap().opt()]),
            waits=["G:cc1i"], mark="C:1", phase=ph)
        sch.dma("sy", lambda e: e.dma_start(out=srow[0:1, 0:2], in_=d_cc["out1"].ap()),
                waits=["C:1"], mark="G:cc1o:a", phase=ph)
        sch.dma("sy", lambda e: e.dma_start(out=srow[0:1, 0:2], in_=d_cc["out1"].ap()),
                waits=["G:cc1o:a"], mark="G:cc1o", phase=ph)
        sch.op("dv", lambda e: e.tensor_scalar(
            out=srow[0:1, 2:3], in0=srow[0:1, 0:1], scalar1=srow[0:1, 8:9], scalar2=None,
            op0=ALU.mult), waits=["G:cc1o", "S:sigx"], mark="D:tx", phase=ph)
        sch.op("dv", lambda e: e.tensor_scalar(
            out=srow[0:1, 3:4], in0=srow[0:1, 2:3], scalar1=1.0 / 127.0, scalar2=1.0 / 127.0,
            op0=ALU.mult, op1=ALU.mult), waits=["D:tx"], mark="D:rqx", phase=ph)
        sch.op("pe", lambda e: e.matmul(pb[5][:, 0:2], ones32[0:1, :], srow[0:1, 2:4],
                                        start=True, stop=True),
               waits=["D:rqx"], mark="T:k1a", phase=ph)
        sch.op("dv", lambda e: e.tensor_copy(out=bc2[:, 0:2], in_=pb[5][:, 0:2]),
               waits=["T:k1a"], mark="D:bce", phase=ph)
        b5_reader[:] = ["D:bca"]
        sch.op("dv", lambda e: e.reciprocal(out=bc2[:, 0:1], in_=bc2[:, 0:1]),
               waits=["D:bce"], mark="D:rca", phase=ph)
        sch.op("dv", lambda e: e.tensor_scalar(
            out=bc2[:, 0:1], in0=bc2[:, 0:1], scalar1=127.0, scalar2=None, op0=ALU.mult),
            waits=["D:rca"], mark="D:bca", phase=ph)
        sch.op("dv", lambda e: e.tensor_scalar(
            out=sg_all[:], in0=rs_g[:], scalar1=bc2[:, 1:2], scalar2=None, op0=ALU.mult),
            waits=["D:bca", f"P:rsg:{ICN - 1}", f"P:rsu:{ICN - 1}"], phase=ph)
        sch.op("dv", lambda e: e.tensor_scalar(
            out=su_all[:], in0=rs_u[:], scalar1=bc2[:, 1:2], scalar2=None, op0=ALU.mult),
            mark="D:sga", phase=ph)

        sch.evt["T:aend"] = sch.evt["T:k1a"]
        sch.evt["S:aend"] = sch.evt[f"S:st2:{NW - 1}"]
        sch.evt["D:aend"] = sch.evt["D:sga"]
        sch.evt["P:aend"] = sch.evt[f"P:qd:{ICN - 1}"]
        sch.evt["V:x1all"] = ("scq", sch.cnt["scq"])

        block = ent(nc.Block())
        sch.emit(block, sems, 0)
        a.close()

        ph = 1
        b1 = contextlib.ExitStack()
        ent3 = b1.enter_context
        ixf = ent3(nc.sbuf_tensor("ixf", [128, HC, 2048], BF16))
        tqb = [ent3(nc.sbuf_tensor(f"tqb{i}", [128, H], BF16)) for i in range(2)]
        qtmp = ent3(nc.sbuf_tensor("qtmp", [128, H], F32))
        x1r = [ent3(nc.sbuf_tensor(f"x1r{i}", [128, H], F32)) for i in range(2)]
        gch = [ent3(nc.sbuf_tensor(f"gch{i}", [128, HC, 128], BF16)) for i in range(2)]
        uch = [ent3(nc.sbuf_tensor(f"uch{i}", [128, HC, 128], BF16)) for i in range(2)]
        pgs = [ent3(nc.sbuf_tensor(f"pgs{i}", [128, 512], F32)) for i in range(2)]
        pus = [ent3(nc.sbuf_tensor(f"pus{i}", [128, 512], F32)) for i in range(2)]

        FENCE = ["T:aend", "S:aend", "D:aend", "P:aend", "G:iwg", "G:iwu", "V:x1all"]

        if DEBUG_TAPS:
            d_tixf = nc.dram_tensor("tap_ixf", [128, HC * 2048], BF16, kind="ExternalOutput").ap()
            d_tbc2 = nc.dram_tensor("tap_bc2", [128, 4], F32, kind="ExternalOutput").ap()
            d_tsg = nc.dram_tensor("tap_sg", [128, 2 * ICN], F32, kind="ExternalOutput").ap()

        kglob = 0
        pend_p = []
        for half in range(2):
            for wl in range(16):
                w = 16 * half + wl
                wts = list(FENCE) if w == 0 else []
                if w >= 2:
                    wts.append(f"D:tq:{w - 2}")
                sch.dma("sy", lambda e, w=w, wl=wl: e.dma_start(
                    out=x1r[wl % 2][:], in_=v_x1[:, w, :]), waits=wts, mark=f"L:x1:{w}", phase=ph)
                if wl == 15:
                    sch.dma("sy", lambda e, w=w, wl=wl: e.dma_start(
                        out=x1r[wl % 2][:], in_=v_x1[:, w, :]), mark=f"L:x1:{w}", phase=ph)
                wq0 = [f"L:x1:{w}"]
                if wl < 15:
                    wq0.append(f"L:x1:{w + 1}")
                if w == 0:
                    wq0.append("D:sga")
                if w == 16:
                    wq0.append(f"S:sil:0:{ICN - 1}:3")
                    if DEBUG_TAPS:
                        wq0.append("TAP:ixf")
                sch.op("dv", lambda e, wl=wl: e.tensor_scalar(
                    out=qtmp[:], in0=x1r[wl % 2][:], scalar1=bc2[:, 0:1], scalar2=MAGIC,
                    op0=ALU.mult, op1=ALU.add), waits=wq0, phase=ph)
                sch.op("dv", lambda e: e.tensor_scalar(
                    out=qtmp[:], in0=qtmp[:], scalar1=-MAGIC, scalar2=-128.0,
                    op0=ALU.add, op1=ALU.max), phase=ph)
                wq2 = [f"T:tr:{w - 2}"] if w >= 2 else []
                sch.op("dv", lambda e, wl=wl: e.tensor_scalar(
                    out=tqb[wl % 2][:], in0=qtmp[:], scalar1=127.0, scalar2=None, op0=ALU.min),
                    waits=wq2, mark=f"D:tq:{w}", phase=ph)
                bk0, bk1 = (0, 1) if wl % 2 == 0 else (2, 3)
                wt0 = [f"D:tq:{w}"]
                for hc in range(HC):
                    bank = pb[bk0] if hc < 4 else pb[bk1]
                    off = (hc % 4) * 128
                    sch.op("pe", lambda e, bank=bank, off=off, hc=hc, wl=wl: e.matmul(
                        bank[:, off:off + 128], tqb[wl % 2][:, hc * 128:(hc + 1) * 128],
                        eyeb[:], start=(hc % 4 == 0), stop=(hc % 4 == 3)),
                        waits=(wt0 if hc == 0 else []),
                        mark=(f"T:tr:{w}" if hc == HC - 1 else None), phase=ph)
                sch.op("dv", lambda e, bk0=bk0, wl=wl: e.tensor_copy(
                    out=ixf[:, 0:4, wl * 128:(wl + 1) * 128], in_=pb[bk0][:, 0:512]),
                    waits=[f"T:tr:{w}"], phase=ph)
                sch.op("dv", lambda e, bk1=bk1, wl=wl: e.tensor_copy(
                    out=ixf[:, 4:8, wl * 128:(wl + 1) * 128], in_=pb[bk1][:, 0:512]),
                    mark=f"D:ixe:{w}", phase=ph)

            for ic in range(ICN):
                if half == 0 and ic == 0:
                    wl0, wl1 = ["G:iwg"], ["G:iwu"]
                elif ic >= 2:
                    wl0 = [f"T:g:{half}:{ic - 2}"]
                    wl1 = [f"T:u:{half}:{ic - 2}"]
                elif half == 1:
                    wl0 = [f"T:g:0:{ICN - 2 + ic}"]
                    wl1 = [f"T:u:0:{ICN - 2 + ic}"]
                else:
                    wl0, wl1 = [], []
                sch.dma("sy", lambda e, ic=ic: e.dma_start(
                    out=gch[ic % 2][:], in_=v_iwg[:, :, ic * 128:(ic + 1) * 128]),
                    waits=wl0, mark=f"L:g:{half}:{ic}", phase=ph)
                sch.dma("sy", lambda e, ic=ic: e.dma_start(
                    out=uch[ic % 2][:], in_=v_iwu[:, :, ic * 128:(ic + 1) * 128]),
                    waits=wl1, mark=f"L:u:{half}:{ic}", phase=ph)
                if ic == 0:
                    sch.dma("sy", lambda e, ic=ic: e.dma_start(
                        out=uch[ic % 2][:], in_=v_iwu[:, :, ic * 128:(ic + 1) * 128]),
                        mark=f"L:u:{half}:{ic}", phase=ph)
                if DEBUG_TAPS and half == 0 and ic == 0:
                    sch.dma("sy", lambda e: e.dma_start(out=d_tixf, in_=ixf[:]),
                            waits=["D:ixe:15"], mark="TAP:ixf", phase=ph)
                    sch.dma("sy", lambda e: e.dma_start(out=d_tbc2, in_=bc2[:]),
                            waits=["D:sga"], mark="TAP:bc2", phase=ph)
                    sch.dma("sy", lambda e: e.dma_start(out=d_tsg[:, 0:ICN], in_=sg_all[:]),
                            waits=["D:sga"], mark="TAP:sg", phase=ph)
                    sch.dma("sy", lambda e: e.dma_start(out=d_tsg[:, ICN:2 * ICN], in_=su_all[:]),
                            waits=["D:sga"], mark="TAP:su", phase=ph)

                if ic == 0:
                    ixl = f"D:ixe:{16 * half + 15}"
                    wg0 = [ixl]
                    sch.push(ixl, f"L:g:{half}:0", f"L:u:{half}:0")
                else:
                    sil_prev = f"S:sil:{half}:{ic - 1}:3"
                    wg0 = [sil_prev]
                    sch.push(sil_prev, f"L:g:{half}:{ic}")
                first = True
                for hc in range(HC):
                    for tq4 in range(4):
                        sch.op("pe", lambda e, hc=hc, tq4=tq4, ic=ic: e.matmul(
                            pb[tq4][:, 0:512], gch[ic % 2][:, hc, :],
                            ixf[:, hc, tq4 * 512:(tq4 + 1) * 512],
                            start=(hc == 0), stop=(hc == HC - 1)),
                            waits=(wg0 if first else []),
                            mark=(f"T:g:{half}:{ic}" if (hc == HC - 1 and tq4 == 3) else None),
                            phase=ph)
                        first = False
                if ic == 0:
                    wu0 = [f"D:ixe:{16 * half + 15}"]
                else:
                    ue_prev = f"D:ue:{half}:{ic - 1}:3"
                    wu0 = [ue_prev]
                    sch.push(ue_prev, f"L:u:{half}:{ic}")
                first = True
                for hc in range(HC):
                    for tq4 in range(4):
                        sch.op("pe", lambda e, hc=hc, tq4=tq4, ic=ic: e.matmul(
                            pb[4 + tq4][:, 0:512], uch[ic % 2][:, hc, :],
                            ixf[:, hc, tq4 * 512:(tq4 + 1) * 512],
                            start=(hc == 0), stop=(hc == HC - 1)),
                            waits=(wu0 if first else []),
                            mark=(f"T:u:{half}:{ic}" if (hc == HC - 1 and tq4 == 3) else None),
                            phase=ph)
                        first = False

                for tq4 in range(4):
                    kk = kglob
                    kglob += 1
                    while len(pend_p) > 2:
                        pfn, pwaits, pmk = pend_p.pop(0)
                        sch.dma("sc", pfn, waits=pwaits, mark=pmk, phase=ph)
                    ws0 = [f"T:g:{half}:{ic}"]
                    if kk >= 2:
                        ws0.append(f"D:pp:{kk - 2}")
                    if kk == 0:
                        ws0.append("D:sga")
                    sch.op("sc", lambda e, tq4=tq4, ic=ic: e.activation(
                        out=pgs[tq4 % 2][:], in_=pb[tq4][:, 0:512], func=AF.Silu,
                        scale=sg_all[:, ic:ic + 1]),
                        waits=ws0, mark=f"S:sil:{half}:{ic}:{tq4}", phase=ph)
                    wu1 = [f"T:u:{half}:{ic}"]
                    if kk >= 2:
                        wu1.append(f"V:p:{kk - 2}")
                    sch.op("dv", lambda e, tq4=tq4, ic=ic: e.tensor_scalar(
                        out=pus[tq4 % 2][:], in0=pb[4 + tq4][:, 0:512],
                        scalar1=su_all[:, ic:ic + 1], scalar2=None, op0=ALU.mult),
                        waits=wu1, mark=f"D:ue:{half}:{ic}:{tq4}", phase=ph)
                    sch.op("dv", lambda e, tq4=tq4: e.tensor_tensor(
                        out=pus[tq4 % 2][:], in0=pgs[tq4 % 2][:], in1=pus[tq4 % 2][:],
                        op=ALU.mult), waits=[f"S:sil:{half}:{ic}:{tq4}"], phase=ph)
                    sch.op("dv", lambda e, tq4=tq4, kk=kk: e.reduce_max(
                        out=mp_all[:, kk:kk + 1], in_=pus[tq4 % 2][:], axis=AX.X,
                        apply_absolute_value=True), mark=f"D:pp:{kk}", phase=ph)
                    toff = half * 2048 + tq4 * 512
                    pend_p.append((
                        lambda e, tq4=tq4, ic=ic, toff=toff: e.dma_start(
                            out=v_p[:, ic, toff:toff + 512], in_=pus[tq4 % 2][:]),
                        [f"D:pp:{kk}"], f"V:p:{kk}"))

        while pend_p:
            pfn, pwaits, pmk = pend_p.pop(0)
            sch.dma("sc", pfn, waits=pwaits, mark=pmk, phase=ph)

        sch.op("dv", lambda e: e.reduce_max(out=mvec[:, 1:2], in_=mp_all[:], axis=AX.X),
               waits=["D:pp:175"], mark="D:mpv", phase=ph)
        sch.push("D:mpv", f"S:sil:1:{ICN - 1}:3")
        sch.op("pe", lambda e: e.matmul(pb[0][0:1, 0:128], mvec[:, 1:2], eyef[:],
                                        start=True, stop=True),
               waits=["D:mpv"], mark="T:mfl2", phase=ph)
        sch.op("dv", lambda e: e.tensor_copy(out=row_sb[0:1, :], in_=pb[0][0:1, 0:128]),
               waits=["T:mfl2"], mark="D:mfe2", phase=ph)
        sch.op("dv", lambda e: e.reduce_max(out=srow[0:1, 4:5], in_=row_sb[0:1, 0:128], axis=AX.X),
               waits=["D:mfe2"], mark="D:gp", phase=ph)
        sch.dma("sy", lambda e: e.dma_start(out=d_cc["in2"].ap(), in_=srow[0:1, 4:6]),
                waits=["D:gp"], mark="G:cc2i", phase=ph)
        sch.cc(lambda e: e.collective_compute(
            "AllReduce", ALU.max, replica_groups=[list(range(8))],
            ins=[d_cc["in2"].ap().opt()], outs=[d_cc["out2"].ap().opt()]),
            waits=["G:cc2i"], mark="C:2", phase=ph)
        sch.dma("sy", lambda e: e.dma_start(out=srow[0:1, 4:6], in_=d_cc["out2"].ap()),
                waits=["C:2"], mark="G:cc2o:a", phase=ph)
        sch.dma("sy", lambda e: e.dma_start(out=srow[0:1, 4:6], in_=d_cc["out2"].ap()),
                waits=["G:cc2o:a"], mark="G:cc2o", phase=ph)
        sch.op("dv", lambda e: e.tensor_scalar(
            out=srow[0:1, 6:7], in0=srow[0:1, 4:5], scalar1=srow[0:1, 9:10], scalar2=None,
            op0=ALU.mult), waits=["G:cc2o", "S:sigp"], mark="D:tp", phase=ph)
        sch.op("dv", lambda e: e.tensor_scalar(
            out=srow[0:1, 7:8], in0=srow[0:1, 6:7], scalar1=1.0 / 127.0, scalar2=1.0 / 127.0,
            op0=ALU.mult, op1=ALU.mult), waits=["D:tp"], mark="D:rqp", phase=ph)
        sch.op("pe", lambda e: e.matmul(pb[1][:, 0:2], ones32[0:1, :], srow[0:1, 6:8],
                                        start=True, stop=True),
               waits=["D:rqp"], mark="T:k1b", phase=ph)
        sch.op("dv", lambda e: e.tensor_copy(out=bc2[:, 2:4], in_=pb[1][:, 0:2]),
               waits=["T:k1b"], mark="D:bce2", phase=ph)
        sch.op("dv", lambda e: e.reciprocal(out=bc2[:, 2:3], in_=bc2[:, 2:3]),
               waits=["D:bce2"], mark="D:rcb", phase=ph)
        sch.op("dv", lambda e: e.tensor_scalar(
            out=bc2[:, 2:3], in0=bc2[:, 2:3], scalar1=127.0, scalar2=None, op0=ALU.mult),
            waits=["D:rcb"], mark="D:bcb", phase=ph)
        sch.op("pe", lambda e: e.matmul(pb[2][0:HC, 0:128], rs_d[:, 0:HC], eyef[:],
                                        start=True, stop=True),
               mark="T:fld2", phase=ph)
        sch.op("dv", lambda e: e.tensor_copy(out=row_sb[0:HC, :], in_=pb[2][0:HC, 0:128]),
               waits=["T:fld2"], mark="D:fld2e", phase=ph)
        sch.dma("sc", lambda e: e.dma_start(
            out=d_rr["d"].rearrange("o (a b) -> (o a) b", b=128)[0:HC, :],
            in_=row_sb[0:HC, :]), waits=["D:fld2e", "G:iwd"], mark="V:rrd2", phase=ph)
        sch.dma("sy", lambda e: e.dma_start(out=sdbc[:], in_=d_rr["d"].partition_broadcast(128)),
                waits=["V:rrd2"], mark="G:bcd2:a", phase=ph)
        sch.dma("sy", lambda e: e.dma_start(out=sdbc[:], in_=d_rr["d"].partition_broadcast(128)),
                waits=["G:bcd2:a"], mark="G:bcd2", phase=ph)
        sch.op("dv", lambda e: e.tensor_scalar(
            out=sdbc[:], in0=sdbc[:], scalar1=bc2[:, 3:4], scalar2=None, op0=ALU.mult),
            waits=["G:bcd2", "D:bcb"], mark="D:sdbc", phase=ph)

        sch.evt["T:p1end"] = sch.evt["T:fld2"]
        sch.evt["S:p1end"] = sch.evt[f"S:sil:1:{ICN - 1}:3"]
        sch.evt["D:p1end"] = sch.evt["D:sdbc"]
        sch.evt["V:pall"] = ("scq", sch.cnt["scq"])

        sch.emit(block, sems, 1)
        b1.close()

        ph = 2
        b2 = contextlib.ExitStack()
        ent4 = b2.enter_context
        iwd_sb = ent4(nc.sbuf_tensor("iwd_sb", [128, ICN, H], BF16))
        ih = [ent4(nc.sbuf_tensor(f"ih{i}", [128, ICN, 128], BF16)) for i in range(2)]
        pblk = [ent4(nc.sbuf_tensor(f"pblk{i}", [128, ICN, 128], F32)) for i in range(2)]
        x1r2 = [ent4(nc.sbuf_tensor(f"x1r2{i}", [128, H], F32)) for i in range(2)]
        dsb = [ent4(nc.sbuf_tensor(f"dsb{i}", [128, H], F32)) for i in range(2)]
        r3b = ent4(nc.sbuf_tensor("r3b", [128, H], F32))
        qt2 = ent4(nc.sbuf_tensor("qt2", [128, ICN * 128], F32))
        outb = [ent4(nc.sbuf_tensor(f"outb{i}", [128, H], F32)) for i in range(2)]

        FENCE2 = ["T:p1end", "S:p1end", "D:p1end"]
        pend_o = []
        if DEBUG_TAPS or DEBUG_TAPS2:
            d_tih = nc.dram_tensor("tap_ih", [128, ICN * 128], BF16, kind="ExternalOutput").ap()
            d_tdsb = nc.dram_tensor("tap_dsb", [128, H], F32, kind="ExternalOutput").ap()
            d_tdsc = nc.dram_tensor("tap_dsc", [128, H], F32, kind="ExternalOutput").ap()
            d_tr3 = nc.dram_tensor("tap_r3", [128, H], F32, kind="ExternalOutput").ap()
            d_tsd = nc.dram_tensor("tap_sd", [128, H], F32, kind="ExternalOutput").ap()
        sch.dma("sy", lambda e: e.dma_start(out=iwd_sb[:], in_=v_iwd),
                waits=FENCE2 + ["G:iwd"], mark="L:iwd", phase=ph)

        for blk in range(NW):
            par = blk % 2
            if blk >= 2:
                wpl = [f"D:ih:{blk - 2}"]
            else:
                wpl = list(FENCE2) + ["V:pall"]
            sch.dma("sy", lambda e, blk=blk, par=par: e.dma_start(
                out=pblk[par][:], in_=v_p[:, :, blk * 128:(blk + 1) * 128]),
                waits=wpl, mark=f"L:p:{blk}", phase=ph)
            wxl = [f"D:r3:{blk - 2}"] if blk >= 2 else list(FENCE2) + ["V:pall"]
            sch.dma("sy", lambda e, blk=blk, par=par: e.dma_start(
                out=x1r2[par][:], in_=v_x1[:, blk, :]), waits=wxl, mark=f"L:xb:{blk}", phase=ph)
            if blk == NW - 1:
                sch.dma("sy", lambda e, blk=blk, par=par: e.dma_start(
                    out=x1r2[par][:], in_=v_x1[:, blk, :]), mark=f"L:xb:{blk}", phase=ph)

            wq0 = [f"L:p:{blk}", f"L:xb:{blk}"]
            if blk == 0:
                wq0.append("D:bcb")
            if (DEBUG_TAPS or DEBUG_TAPS2) and blk == 2:
                wq0 += ["TAP:ih", "TAP:dsb", "TAP:dsc", "TAP:r3", "TAP:sd"]
            sch.op("dv", lambda e, par=par: e.tensor_scalar(
                out=qt2[:], in0=pblk[par][:], scalar1=bc2[:, 2:3], scalar2=MAGIC,
                op0=ALU.mult, op1=ALU.add), waits=wq0, phase=ph)
            sch.op("dv", lambda e: e.tensor_scalar(
                out=qt2[:], in0=qt2[:], scalar1=-MAGIC, scalar2=-128.0,
                op0=ALU.add, op1=ALU.max), phase=ph)
            wih = [f"T:d:{blk - 2}"] if blk >= 2 else []
            sch.op("dv", lambda e, par=par: e.tensor_scalar(
                out=ih[par][:], in0=qt2[:], scalar1=127.0, scalar2=None, op0=ALU.min),
                waits=wih, mark=f"D:ih:{blk}", phase=ph)

            wd0 = [f"D:ih:{blk}"]
            sch.push(f"D:ih:{blk}", "L:iwd")
            if blk >= 2:
                sch.push(f"D:ih:{blk}", f"S:de:{blk - 2}")
            first = True
            for half2 in range(2):
                for ic in range(ICN):
                    sch.op("pe", lambda e, half2=half2, ic=ic, par=par: e.matmul(
                        pb[2 * par + half2][:, 0:512], ih[par][:, ic, :],
                        iwd_sb[:, ic, half2 * 512:(half2 + 1) * 512],
                        start=(ic == 0), stop=(ic == ICN - 1)),
                        waits=(wd0 if first else []),
                        mark=(f"T:d:{blk}" if (half2 == 1 and ic == ICN - 1) else None),
                        phase=ph)
                    first = False
            while len(pend_o) > 1:
                pfn, pwaits, pmk = pend_o.pop(0)
                sch.dma("sc", pfn, waits=pwaits, mark=pmk, phase=ph)
            wde = [f"T:d:{blk}"]
            if blk >= 2:
                wde.append(f"D:r3:{blk - 2}")
            sch.op("sc", lambda e, par=par: e.copy(out=dsb[par][:, 0:512],
                                                   in_=pb[2 * par][:, 0:512]),
                   waits=wde, phase=ph)
            sch.op("sc", lambda e, par=par: e.copy(out=dsb[par][:, 512:1024],
                                                   in_=pb[2 * par + 1][:, 0:512]),
                   mark=f"S:de:{blk}", phase=ph)
            wsc = [f"S:de:{blk}"]
            if blk == 0:
                wsc.append("D:sdbc")
                if DEBUG_TAPS or DEBUG_TAPS2:
                    wsc.append("TAP:dsb")
            sch.op("dv", lambda e, par=par: e.tensor_tensor(
                out=dsb[par][:], in0=dsb[par][:], in1=sdbc[:], op=ALU.mult),
                waits=wsc, phase=ph)
            wr3 = [f"L:xb:{blk}"]
            if blk < NW - 1:
                wr3.append(f"L:p:{blk + 1}")
            sch.op("dv", lambda e, par=par: e.scalar_tensor_tensor(
                out=r3b[:], in0=dsb[par][:], scalar=1.0, in1=x1r2[par][:],
                op0=ALU.mult, op1=ALU.add, accum_out=sl(par, 5)),
                waits=wr3, mark=f"D:r3:{blk}", phase=ph)
            if (DEBUG_TAPS or DEBUG_TAPS2) and blk == 0:
                sch.dma("sy", lambda e: e.dma_start(out=d_tih, in_=ih[0][:]),
                        waits=["D:ih:0"], mark="TAP:ih", phase=ph)
                sch.dma("sy", lambda e: e.dma_start(out=d_tdsb, in_=dsb[0][:]),
                        waits=["S:de:0"], mark="TAP:dsb", phase=ph)
                sch.dma("sy", lambda e: e.dma_start(out=d_tdsc, in_=dsb[0][:]),
                        waits=["D:r3:0"], mark="TAP:dsc", phase=ph)
                sch.dma("sy", lambda e: e.dma_start(out=d_tr3, in_=r3b[:]),
                        waits=["D:r3:0"], mark="TAP:r3", phase=ph)
                sch.dma("sy", lambda e: e.dma_start(out=d_tsd, in_=sdbc[:]),
                        waits=["D:sdbc"], mark="TAP:sd", phase=ph)
            wsq3 = [f"D:r3:{blk}"]
            if (DEBUG_TAPS or DEBUG_TAPS2) and blk == 0:
                wsq3.append("TAP:dsc")
            sch.op("sc", lambda e, par=par: e.activation(
                out=dsb[par][:], in_=r3b[:], func=AF.Square, accum_out=sl(par, 6)),
                waits=wsq3, mark=f"S:sq3:{blk}", phase=ph)
            sch.op("dv", lambda e, par=par: e.tensor_scalar(
                out=sl(par, 7), in0=sl(par, 5), scalar1=1.0 / 1024.0, scalar2=None,
                op0=ALU.mult), waits=[f"S:sq3:{blk}"], mark=f"D:mn3:{blk}", phase=ph)
            sch.op("dv", lambda e, par=par: e.tensor_scalar(
                out=sl(par, 8), in0=sl(par, 7), scalar1=sl(par, 7), scalar2=None,
                op0=ALU.mult), waits=[f"D:mn3:{blk}"], mark=f"D:m23:{blk}", phase=ph)
            sch.op("dv", lambda e, par=par: e.tensor_scalar(
                out=sl(par, 8), in0=sl(par, 6), scalar1=1.0 / 1024.0, scalar2=sl(par, 8),
                op0=ALU.mult, op1=ALU.subtract),
                waits=[f"D:m23:{blk}"], mark=f"D:vr3:{blk}", phase=ph)
            sch.op("sc", lambda e, par=par: e.activation(
                out=sl(par, 9), in_=sl(par, 8), func=AF.Sqrt, bias=epsv[:]),
                waits=[f"D:vr3:{blk}"], mark=f"S:st3:{blk}", phase=ph)
            sch.op("dv", lambda e, par=par: e.reciprocal(out=sl(par, 9), in_=sl(par, 9)),
                   waits=[f"S:st3:{blk}"], mark=f"D:rc3:{blk}", phase=ph)
            wob = [f"D:rc3:{blk}"] + ([f"V:o:{blk - 2}"] if blk >= 2 else [])
            sch.op("dv", lambda e, par=par: e.tensor_scalar(
                out=outb[par][:], in0=r3b[:], scalar1=sl(par, 7), scalar2=sl(par, 9),
                op0=ALU.subtract, op1=ALU.mult), waits=wob, mark=f"D:ob:{blk}", phase=ph)
            pend_o.append((
                lambda e, blk=blk, par=par: e.dma_start(out=v_out[:, blk, :], in_=outb[par][:]),
                [f"D:ob:{blk}"], f"V:o:{blk}"))

        while pend_o:
            pfn, pwaits, pmk = pend_o.pop(0)
            sch.dma("sc", pfn, waits=pwaits, mark=pmk, phase=ph)

        if DEBUG_DUMPS:
            d_dbg1 = nc.dram_tensor("dbg_x1", [S, H], F32, kind="ExternalOutput").ap()
            d_dbg2 = nc.dram_tensor("dbg_p", [II, S], F32, kind="ExternalOutput").ap()
            d_dbg3 = nc.dram_tensor("dbg_iwg", [H, II], BF16, kind="ExternalOutput").ap()
            d_dbg4 = nc.dram_tensor("dbg_iwd", [II, H], BF16, kind="ExternalOutput").ap()
            d_dbg5 = nc.dram_tensor("dbg_sml", [128, 24], F32, kind="ExternalOutput").ap()
            d_dbg6 = nc.dram_tensor("dbg_srow", [1, 16], F32, kind="ExternalOutput").ap()
            sch.dma("sy", lambda e: e.dma_start(out=d_dbg1, in_=d_x1),
                    waits=[f"V:o:{NW - 1}"], mark="DBG1", phase=ph)
            sch.dma("sy", lambda e: e.dma_start(out=d_dbg2, in_=d_p), mark="DBG2", phase=ph)
            sch.dma("sy", lambda e: e.dma_start(out=d_dbg3, in_=d_iwg), mark="DBG3", phase=ph)
            sch.dma("sy", lambda e: e.dma_start(out=d_dbg4, in_=d_iwd), mark="DBG4", phase=ph)
            sch.dma("sy", lambda e: e.dma_start(out=d_dbg5, in_=sml[:]), mark="DBG5", phase=ph)
            sch.dma("sy", lambda e: e.dma_start(out=d_dbg6, in_=srow[:]), mark="DBG6", phase=ph)
            d_dbg7 = nc.dram_tensor("dbg_clip", [1, 2], F32, kind="ExternalOutput").ap()
            d_dbg8 = nc.dram_tensor("dbg_bc2", [128, 4], F32, kind="ExternalOutput").ap()
            sch.dma("sy", lambda e: e.dma_start(out=d_dbg7, in_=clip_sb[:]), mark="DBG7", phase=ph)
            sch.dma("sy", lambda e: e.dma_start(out=d_dbg8, in_=bc2[:]), mark="DBG8", phase=ph)
            sch.wait_only("sy", ["DBG8", "DBG7", "DBG6", "DBG5", "DBG4", "DBG3", "DBG2", "DBG1"], phase=ph)

        sch.wait_only("sy", [f"V:o:{NW - 1}"], phase=ph)

        sch.emit(block, sems, 2)
        b2.close()

    return nc


_NC_CACHE = []


def kernel(L0_attn_x, L0_attn_Wq, L0_attn_bq, L0_attn_Wk, L0_attn_bk, L0_attn_Wv,
           L0_attn_bv, L0_attn_gamma, L0_attn_beta, L0_ffn_clip_x, L0_ffn_clip_down,
           L0_ffn_w_gate, L0_ffn_w_up, L0_ffn_w_down):
    x = np.ascontiguousarray(np.asarray(L0_attn_x, dtype=np.float32))
    Wq = np.asarray(L0_attn_Wq, np.float32)
    Wk = np.asarray(L0_attn_Wk, np.float32)
    Wv = np.asarray(L0_attn_Wv, np.float32)
    wg = np.ascontiguousarray(np.asarray(L0_ffn_w_gate, np.float32))
    wu = np.ascontiguousarray(np.asarray(L0_ffn_w_up, np.float32))
    wd = np.ascontiguousarray(np.asarray(L0_ffn_w_down, np.float32))

    common = {
        "wqT": np.ascontiguousarray(Wq.T),
        "wkT": np.ascontiguousarray(Wk.T),
        "wvT": np.ascontiguousarray(Wv.T),
        "bq_pc": np.ascontiguousarray(np.asarray(L0_attn_bq, np.float32).reshape(HC, 128).T),
        "bk_pc": np.ascontiguousarray(np.asarray(L0_attn_bk, np.float32).reshape(HC, 128).T),
        "bv_row": np.ascontiguousarray(np.asarray(L0_attn_bv, np.float32).reshape(1, H)),
        "gam_row": np.ascontiguousarray(np.asarray(L0_attn_gamma, np.float32).reshape(1, H)),
        "bet_row": np.ascontiguousarray(np.asarray(L0_attn_beta, np.float32).reshape(1, H)),
        "clip2": np.ascontiguousarray(
            np.concatenate([np.asarray(L0_ffn_clip_x, np.float32).reshape(1),
                            np.asarray(L0_ffn_clip_down, np.float32).reshape(1)]).reshape(1, 2)),
        "w_gate": wg, "w_gateT": np.ascontiguousarray(wg.T),
        "w_up": wu, "w_upT": np.ascontiguousarray(wu.T),
        "w_down": wd, "w_downT": np.ascontiguousarray(wd.T),
    }

    if not _NC_CACHE:
        _NC_CACHE.append(build_program())
    nc = _NC_CACHE[0]

    in_maps = []
    for c in range(8):
        m = dict(common)
        m["x_tm"] = np.ascontiguousarray(x[c])
        m["x_fm"] = np.ascontiguousarray(x[c].T)
        in_maps.append(m)

    res = run_bass_kernel_spmd(nc, in_maps, core_ids=list(range(8)))
    out = np.stack([np.asarray(r["out"], np.float32) for r in res.results], axis=0)
    return out


# version 24 (final)
# speedup vs baseline: 1.0557x; 1.0513x over previous
import numpy as np
import ml_dtypes
from collections import defaultdict
import contextlib

import concourse.bass as bass
import concourse.mybir as mybir
from concourse.bass_utils import run_bass_kernel_spmd

DEBUG_DUMPS = False
DEBUG_TAPS2 = False
DEBUG_TAPS = False

F32 = mybir.dt.float32
BF16 = mybir.dt.bfloat16
AF = mybir.ActivationFunctionType
ALU = mybir.AluOpType
AX = mybir.AxisListType

B, S, H, II = 8, 4096, 1024, 2816
WIN = 128
NW = S // WIN
NT = NW // 2
HC = H // 128
ICN = II // 128
MAGIC = 12582912.0
EPS = 1e-5
BIGB = 1024.0
NEG = -1.0e30


class Sched:

    QSEM = {"pe": "spe", "sc": "ssc", "dv": "sdv", "pl": "spl", "sy": "ssy"}
    DSEM = {"sc": "scq", "dv": "svq", "pl": "sgq", "sy": "ssy2"}

    def __init__(self):
        self.items = defaultdict(list)
        self.evt = {}
        self.cnt = defaultdict(int)
        self.reg = {}

    def _add(self, stream, fn, waits, m, phase, mark):
        it = [fn, list(waits), m, phase]
        self.items[stream].append(it)
        if mark is not None:
            self.reg[mark] = it
        return mark

    def op(self, stream, fn, waits=(), mark=None, phase=0):
        semn = self.QSEM[stream]
        m = None
        if mark is not None:
            self.cnt[semn] += 1
            self.evt[mark] = (semn, self.cnt[semn])
            m = (semn, 1)
        return self._add(stream, fn, waits, m, phase, mark)

    def dma(self, stream, fn, waits=(), mark=None, phase=0, sem=None):
        semn = sem if sem is not None else self.DSEM[stream]
        self.cnt[semn] += 16
        if mark is not None:
            self.evt[mark] = (semn, self.cnt[semn])
        return self._add(stream, fn, waits, (semn, 16), phase, mark)

    def cc(self, fn, waits=(), mark=None, phase=0):
        self.cnt["scc"] += 1
        if mark is not None:
            self.evt[mark] = ("scc", self.cnt["scc"])
        self._add("pl", fn, waits, ("scc", 1), phase, mark)

    def push(self, mark, *extra_waits):
        self.reg[mark][1].extend(extra_waits)

    def latest(self, *keys):
        evs = [(self.evt[k], k) for k in keys]
        sems = {e[0][0] for e in evs}
        assert len(sems) == 1, f"latest() on mixed sems: {keys}"
        return max(evs)[1]

    def wait_only(self, stream, waits, phase=0):
        assert stream != "pe"
        self.items[stream].append([None, list(waits), None, phase])

    def emit(self, block, sems, phase):
        streams = {"pe": "tensor", "sc": "scalar", "dv": "vector",
                   "pl": "gpsimd", "sy": "sync"}
        for stream, attr in streams.items():
            its = [it for it in self.items[stream] if it[3] == phase]
            if not its:
                continue

            def run(e, its=its, stream=stream):
                for fn, waits, m, _ in its:
                    if fn is None:
                        for wk in waits:
                            sn, cv = self.evt[wk]
                            e.wait_ge(sems[sn], cv)
                        continue
                    best = {}
                    for wk in waits:
                        sn, cv = self.evt[wk]
                        if cv > best.get(sn, -1):
                            best[sn] = cv
                    resolved = sorted(best.items())
                    if stream == "pe":
                        assert len(resolved) <= 1, f"PE item with waits {waits}"
                    for sn, cv in resolved[1:]:
                        e.wait_ge(sems[sn], cv)
                    ins = fn(e)
                    if resolved:
                        sn, cv = resolved[0]
                        ins._wait_ge(sems[sn], cv)
                    if m is not None:
                        sn, dv_ = m
                        if sn == "scc":
                            ins.then_inc(sems[sn])
                        else:
                            ins.then_inc(sems[sn], dv_)

            getattr(block, attr)(run)


def build_program():
    nc = bass.Bass("TRN2", target_bir_lowering=False, detect_race_conditions=False)

    d_x = nc.dram_tensor("x_tm", [S, H], F32, kind="ExternalInput").ap()
    d_xT = nc.dram_tensor("x_fm", [H, S], F32, kind="ExternalInput").ap()
    d_wqT = nc.dram_tensor("wqT", [H, H], F32, kind="ExternalInput").ap()
    d_wkT = nc.dram_tensor("wkT", [H, H], F32, kind="ExternalInput").ap()
    d_wvT = nc.dram_tensor("wvT", [H, H], F32, kind="ExternalInput").ap()
    d_bq = nc.dram_tensor("bq_pc", [128, HC], F32, kind="ExternalInput").ap()
    d_bk = nc.dram_tensor("bk_pc", [128, HC], F32, kind="ExternalInput").ap()
    d_bv = nc.dram_tensor("bv_row", [1, H], F32, kind="ExternalInput").ap()
    d_gam = nc.dram_tensor("gam_row", [1, H], F32, kind="ExternalInput").ap()
    d_bet = nc.dram_tensor("bet_row", [1, H], F32, kind="ExternalInput").ap()
    d_clip = nc.dram_tensor("clip2", [1, 2], F32, kind="ExternalInput").ap()
    d_wg = nc.dram_tensor("w_gate", [II, H], F32, kind="ExternalInput").ap()
    d_wgT = nc.dram_tensor("w_gateT", [H, II], F32, kind="ExternalInput").ap()
    d_wu = nc.dram_tensor("w_up", [II, H], F32, kind="ExternalInput").ap()
    d_wuT = nc.dram_tensor("w_upT", [H, II], F32, kind="ExternalInput").ap()
    d_wd = nc.dram_tensor("w_down", [H, II], F32, kind="ExternalInput").ap()
    d_wdT = nc.dram_tensor("w_downT", [II, H], F32, kind="ExternalInput").ap()
    d_out = nc.dram_tensor("out", [S, H], F32, kind="ExternalOutput").ap()

    d_x1 = nc.dram_tensor("x1_scr", [S, H], F32, kind="Internal").ap()
    d_p = nc.dram_tensor("p_scr", [II, S], F32, kind="Internal").ap()
    d_iwg = nc.dram_tensor("iwg_scr", [H, II], BF16, kind="Internal").ap()
    d_iwu = nc.dram_tensor("iwu_scr", [H, II], BF16, kind="Internal").ap()
    d_iwd = nc.dram_tensor("iwd_scr", [II, H], BF16, kind="Internal").ap()
    d_rr = {w: nc.dram_tensor(f"rrow_{w}", [1, II if w != "d" else H], F32,
                              kind="Internal").ap() for w in ("g", "u", "d")}
    d_cc = {k: nc.dram_tensor(f"cc_{k}", [1, 2], F32, kind="Internal")
            for k in ("in1", "out1", "in2", "out2")}

    i_eyeb = nc.inline_tensor(np.eye(128, dtype=ml_dtypes.bfloat16), name="eye_bf").ap()
    i_eyef = nc.inline_tensor(np.eye(128, dtype=np.float32), name="eye_f32").ap()
    i_ones = nc.inline_tensor(np.ones((1, 128), dtype=np.float32), name="ones_f32").ap()
    i_onesb = nc.inline_tensor(np.ones((1, 128), dtype=ml_dtypes.bfloat16), name="ones_bf").ap()

    v_x = d_x.rearrange("(w p) h -> p w h", p=128)
    v_xT = d_xT.rearrange("(c p) t -> p c t", p=128)
    v_x1 = d_x1.rearrange("(w p) h -> p w h", p=128)
    v_out = d_out.rearrange("(w p) h -> p w h", p=128)
    v_p = d_p.rearrange("(c p) t -> p c t", p=128)
    v_wg = d_wg.rearrange("(c p) h -> p c h", p=128)
    v_wu = d_wu.rearrange("(c p) h -> p c h", p=128)
    v_wd = d_wd.rearrange("(c p) i -> p c i", p=128)
    v_wgT = d_wgT.rearrange("(c p) i -> p c i", p=128)
    v_wuT = d_wuT.rearrange("(c p) i -> p c i", p=128)
    v_wdT = d_wdT.rearrange("(c p) h -> p c h", p=128)
    v_iwg = d_iwg.rearrange("(c p) i -> p c i", p=128)
    v_iwu = d_iwu.rearrange("(c p) i -> p c i", p=128)
    v_iwd = d_iwd.rearrange("(c p) h -> p c h", p=128)

    sch = Sched()

    ctx = contextlib.ExitStack()
    with ctx:
        ent = ctx.enter_context

        semnames = ["spe", "ssc", "sdv", "spl", "ssy", "ssy2", "scq", "svq", "sgq",
                    "scc", "siwg", "siwu", "siwd", "sxb"]
        sems = {n: ent(nc.semaphore(n)) for n in semnames}

        pb = [ent(nc.psum_tensor(f"pb{i}", [128, 512], F32)) for i in range(8)]

        gb_bc = ent(nc.sbuf_tensor("gb_bc", [128, 2 * H], BF16))
        bvb = ent(nc.sbuf_tensor("bvb", [1, H], BF16))
        onesb = ent(nc.sbuf_tensor("onesb", [1, 128], BF16))
        ones32 = ent(nc.sbuf_tensor("ones32", [1, 128], F32))
        eyeb = ent(nc.sbuf_tensor("eyeb", [128, 128], BF16))
        eyef = ent(nc.sbuf_tensor("eyef", [128, 128], F32))
        bq_sb = ent(nc.sbuf_tensor("bq_sb", [128, HC], F32))
        bk_sb = ent(nc.sbuf_tensor("bk_sb", [128, HC], F32))
        clip_sb = ent(nc.sbuf_tensor("clip_sb", [1, 2], F32))
        rs_g = ent(nc.sbuf_tensor("rs_g", [128, ICN], F32))
        rs_u = ent(nc.sbuf_tensor("rs_u", [128, ICN], F32))
        rs_d = ent(nc.sbuf_tensor("rs_d", [128, HC], F32))
        sg_all = ent(nc.sbuf_tensor("sg_all", [128, ICN], F32))
        su_all = ent(nc.sbuf_tensor("su_all", [128, ICN], F32))
        mx_all = ent(nc.sbuf_tensor("mx_all", [128, NW], F32))
        mp_all = ent(nc.sbuf_tensor("mp_all", [128, 176], F32))
        srow = ent(nc.sbuf_tensor("srow", [1, 16], F32))
        bc2 = ent(nc.sbuf_tensor("bc2", [128, 4], F32))
        mvec = ent(nc.sbuf_tensor("mvec", [128, 2], F32))
        row_sb = ent(nc.sbuf_tensor("row_sb", [ICN, 128], F32))
        rqbc = ent(nc.sbuf_tensor("rqbc", [128, II], F32))
        sdbc = ent(nc.sbuf_tensor("sdbc", [128, H], F32))
        sml = ent(nc.sbuf_tensor("sml", [128, 24], F32))
        epsv = ent(nc.sbuf_tensor("epsv", [128, 1], F32))

        def sl(par, idx):
            return sml[:, par * 12 + idx: par * 12 + idx + 1]

        ph = 0
        a = contextlib.ExitStack()
        ent2 = a.enter_context
        wq_sb = ent2(nc.sbuf_tensor("wq_sb", [128, HC, H], BF16))
        wk_sb = ent2(nc.sbuf_tensor("wk_sb", [128, HC, H], BF16))
        wv_sb = ent2(nc.sbuf_tensor("wv_sb", [128, HC, H], BF16))
        xw = [ent2(nc.sbuf_tensor(f"xw{i}", [128, 2, H], F32)) for i in range(2)]
        xbf = [ent2(nc.sbuf_tensor(f"xbf{i}", [128, HC, 256], BF16)) for i in range(2)]
        qf = ent2(nc.sbuf_tensor("qf", [128, HC, 256], BF16))
        kf = ent2(nc.sbuf_tensor("kf", [128, HC, 256], BF16))
        vsb = [ent2(nc.sbuf_tensor(f"vsb{i}", [128, 2, H], BF16)) for i in range(2)]
        ssb = ent2(nc.sbuf_tensor("ssb", [128, 2, 128], F32))
        wrk = ent2(nc.sbuf_tensor("wrk", [128, 2, 128], F32))
        smb = ent2(nc.sbuf_tensor("smb", [128, 2, 128], F32))
        mskb = ent2(nc.sbuf_tensor("mskb", [128, 2, 128], F32))
        m8 = ent2(nc.sbuf_tensor("m8", [128, 8], F32))
        atn = ent2(nc.sbuf_tensor("atn", [128, 2, 128], BF16))
        atnT = ent2(nc.sbuf_tensor("atnT", [128, 2, 128], BF16))
        osb = ent2(nc.sbuf_tensor("osb", [128, H], F32))
        rsb = ent2(nc.sbuf_tensor("rsb", [128, H], F32))
        x1w = [ent2(nc.sbuf_tensor(f"x1w{i}", [128, H], F32)) for i in range(2)]
        wtl = [ent2(nc.sbuf_tensor(f"wtl{i}", [128, II], F32)) for i in range(5)]
        iw_st = [ent2(nc.sbuf_tensor(f"iwst{i}", [128, II], BF16)) for i in range(2)]

        def dup_sy(fn, mark):
            sch.dma("sy", fn, phase=ph)
            sch.dma("sy", fn, mark=mark, phase=ph)

        def dup_pl(fn, mark, waits=()):
            sch.dma("pl", fn, waits=list(waits), mark=mark + ":a", phase=ph)
            sch.dma("pl", fn, waits=[mark + ":a"], mark=mark, phase=ph)

        dup_sy(lambda e: e.dma_start(out=bq_sb[:], in_=d_bq), "L:bq")
        dup_sy(lambda e: e.dma_start(out=bk_sb[:], in_=d_bk), "L:bk")
        dup_sy(lambda e: e.dma_start(out=clip_sb[:], in_=d_clip), "L:clip")
        dup_sy(lambda e: e.dma_start(out=eyeb[:], in_=i_eyeb), "L:eyeb")
        dup_sy(lambda e: e.dma_start(out=eyef[:], in_=i_eyef), "L:eyef")
        dup_sy(lambda e: e.dma_start(out=ones32[:], in_=i_ones), "L:ones")
        dup_sy(lambda e: e.dma_start(out=onesb[:], in_=i_onesb), "L:onesb")
        dup_pl(lambda e: e.dma_start(out=gb_bc[:, 0:H], in_=d_gam.partition_broadcast(128)),
               "L:gam")
        dup_pl(lambda e: e.dma_start(out=gb_bc[:, H:2 * H], in_=d_bet.partition_broadcast(128)),
               "L:bet")
        dup_pl(lambda e: e.dma_start(out=wq_sb[:], in_=d_wqT.rearrange("(c p) o -> p c o", p=128)),
               "L:wq")
        dup_pl(lambda e: e.dma_start(out=wk_sb[:], in_=d_wkT.rearrange("(c p) o -> p c o", p=128)),
               "L:wk")
        dup_pl(lambda e: e.dma_start(out=wv_sb[:], in_=d_wvT.rearrange("(c p) o -> p c o", p=128)),
               "L:wv")
        dup_pl(lambda e: e.dma_start(out=bvb[:], in_=d_bv), "L:bv")

        for _k in ("L:wq", "L:wk", "L:wv", "L:bv", "L:gam", "L:bet"):
            sch.evt[_k] = ("sgq", sch.cnt["sgq"])

        def x_load(t):
            waits = [f"D:r2:{2 * (t - 2) + 1}"] if t >= 2 else []
            sch.dma("sy", lambda e, t=t: e.dma_start(out=xw[t % 2][:], in_=v_x[:, 2 * t:2 * t + 2, :]),
                    waits=waits, mark=f"L:x:{t}", phase=ph)
            waits2 = [f"L:xT:{t - 1}:a"] if t >= 1 else                 ["L:wq", "L:wk", "L:wv", "L:bv", "L:onesb", "L:eyeb"]
            if t >= 2:
                waits2.append(f"T:v:{2 * (t - 2) + 1}")
            sch.dma("pl", lambda e, t=t: e.dma_start(
                out=xbf[t % 2][:], in_=v_xT[:, :, 256 * t:256 * (t + 1)]),
                waits=waits2, mark=f"L:xT:{t}:a", phase=ph, sem="sxb")
            sch.evt[f"L:xT:{t}"] = sch.evt[f"L:xT:{t}:a"]
            if t in (0, 15):
                sch.dma("pl", lambda e, t=t: e.dma_start(
                    out=xbf[t % 2][:], in_=v_xT[:, :, 256 * t:256 * (t + 1)]),
                    waits=[f"L:xT:{t}:a"], mark=f"L:xT:{t}", phase=ph, sem="sxb")

        x_load(0)
        x_load(1)

        sch.op("sc", lambda e: e.activation(out=srow[0:1, 8:9], in_=clip_sb[0:1, 0:1], func=AF.Sigmoid),
               waits=["L:clip"], mark="S:sigx", phase=ph)
        sch.op("sc", lambda e: e.activation(out=srow[0:1, 9:10], in_=clip_sb[0:1, 1:2], func=AF.Sigmoid),
               mark="S:sigp", phase=ph)
        sch.op("dv", lambda e: e.memset(srow[0:1, 0:8], 0.0),
               waits=["L:eyef", "L:ones"], mark="D:srz", phase=ph)
        sch.op("dv", lambda e: e.memset(epsv[:], EPS), mark="D:eps", phase=ph)
        if DEBUG_TAPS:
            sch.op("dv", lambda e: e.memset(sml[:], 0.0), phase=ph)
            sch.op("dv", lambda e: e.memset(mskb[:], 0.0), phase=ph)
            sch.op("dv", lambda e: e.memset(smb[:], 0.0), phase=ph)
            sch.op("dv", lambda e: e.memset(wrk[:], 0.0), phase=ph)
            sch.op("dv", lambda e: e.memset(atn[:], 0.0), phase=ph)
            sch.op("dv", lambda e: e.memset(atnT[:], 0.0), phase=ph)

        WROT = 5
        wq_entries = []
        wq_state = {"li": 0, "ri": 0, "last_li": 0}

        def rs_stream(name, vsrc, ntile, rs_dst):
            for kk in range(ntile):
                bi = len(wq_entries) % WROT
                wq_entries.append(dict(
                    lf=lambda e, kk=kk, vsrc=vsrc, bi=bi: e.dma_start(
                        out=wtl[bi][:, 0:vsrc.shape[2]], in_=vsrc[:, kk, :]),
                    lm=f"Lw:rs{name}:{kk}",
                    cons=f"P:rs{name}:{kk}",
                    dv=(lambda e, kk=kk, vsrc=vsrc, rs_dst=rs_dst, bi=bi: e.reduce_max(
                        out=rs_dst[:, kk:kk + 1], in_=wtl[bi][:, 0:vsrc.shape[2]],
                        axis=AX.X, apply_absolute_value=True),
                        [f"Lw:rs{name}:{kk}"] + (["D:srz"] if (name == "g" and kk == 0) else []),
                        f"P:rs{name}:{kk}"),
                ))

        rs_stream("g", v_wg, ICN, rs_g)
        rs_stream("u", v_wu, ICN, rs_u)
        rs_stream("d", v_wd, HC, rs_d)

        def wq_drain():
            dv_hi = wq_state["last_li"]
            while wq_state["ri"] < dv_hi:
                ri = wq_state["ri"]
                ent_ = wq_entries[ri]
                for fn, ws, mk, sm in ent_.get("pl_pre", ()):
                    sch.dma("pl", fn, waits=list(ws), mark=mk, phase=ph, sem=sm)
                if ent_.get("pl_tt") is not None:
                    fn, ws, mk = ent_["pl_tt"]
                    sch.op("pl", fn, waits=list(ws), mark=mk, phase=ph)
                if ent_["dv"] is not None:
                    dvs = ent_["dv"]
                    if not isinstance(dvs, list):
                        dvs = [dvs]
                    for fn, ws, mk in dvs:
                        sch.op("dv", fn, waits=list(ws), mark=mk, phase=ph)
                if ent_.get("pl_st") is not None:
                    fn, ws, mk, sm = ent_["pl_st"]
                    sch.dma("pl", fn, waits=list(ws), mark=mk, phase=ph, sem=sm)
                wq_state["ri"] += 1
            n = 0
            while wq_state["li"] < len(wq_entries) and n < 3:
                idx = wq_state["li"]
                ws = []
                if idx >= WROT:
                    ws.append(wq_entries[idx - WROT]["cons"])
                sch.dma("sc", wq_entries[idx]["lf"], waits=ws,
                        mark=wq_entries[idx]["lm"], phase=ph)
                wq_state["li"] += 1
                n += 1
            wq_state["last_li"] = wq_state["li"]

        def quant_stream(name, vsrcT, ntile, ncols, viw, pre_pl=()):
            for kk in range(ntile):
                bi = len(wq_entries) % WROT
                w0 = [f"Lw:q{name}:{kk}", f"G:bc:{name}"]
                if kk < ntile - 1:
                    w0.append(f"Lw:q{name}:{kk + 1}")
                pl_tt = (lambda e, bi=bi, ncols=ncols: e.tensor_tensor(
                    out=wtl[bi][:, 0:ncols], in0=wtl[bi][:, 0:ncols],
                    in1=rqbc[:, 0:ncols], op=ALU.mult), w0, f"P:tt{name}:{kk}")
                wts_ts = [f"P:tt{name}:{kk}"]
                if kk >= 2:
                    wts_ts.append(f"G:iw{name}:{kk - 1}")
                dv_ops = [
                    (lambda e, kk=kk, bi=bi, ncols=ncols: e.tensor_scalar(
                        out=iw_st[kk % 2][:, 0:ncols], in0=wtl[bi][:, 0:ncols],
                        scalar1=MAGIC, scalar2=-MAGIC, op0=ALU.add, op1=ALU.add),
                        wts_ts, f"P:q{name}:{kk}"),
                ]
                stw = [f"P:q{name}:{kk}"]
                if kk > 0:
                    stw.append(f"G:iw{name}:{kk - 1}")
                pl_st = (lambda e, kk=kk, viw=viw, ncols=ncols: e.dma_start(
                    out=viw[:, kk, :], in_=iw_st[kk % 2][:, 0:ncols]),
                    stw, f"G:iw{name}:{kk}", f"siw{name}")
                wq_entries.append(dict(
                    lf=lambda e, kk=kk, vsrcT=vsrcT, ncols=ncols, bi=bi: e.dma_start(
                        out=wtl[bi][:, 0:ncols], in_=vsrcT[:, kk, :]),
                    lm=f"Lw:q{name}:{kk}",
                    cons=f"P:q{name}:{kk}",
                    dv=dv_ops,
                    pl_pre=(list(pre_pl) if kk == 0 else []),
                    pl_tt=pl_tt,
                    pl_st=pl_st,
                    st=None,
                ))

        b5_reader = ["L:eyeb"]

        def rs_flip(name, rs_src, nicol, prev_store, rs_done, bc_extra=()):
            sch.op("pe", lambda e, rs_src=rs_src, nicol=nicol: e.matmul(
                pb[5][0:nicol, 0:128], rs_src[:, 0:nicol], eyef[:], start=True, stop=True),
                waits=[sch.latest(rs_done, *b5_reader)], mark=f"T:fl{name}", phase=ph)
            evw = [f"T:fl{name}"] + ([prev_store] if prev_store else [])
            sch.op("dv", lambda e, nicol=nicol: e.tensor_scalar(
                out=row_sb[0:nicol, :], in0=pb[5][0:nicol, 0:128],
                scalar1=1.0 / 127.0, scalar2=None, op0=ALU.mult),
                waits=evw, mark=f"D:fle:{name}", phase=ph)
            b5_reader[:] = [f"D:fle:{name}"]
            sch.op("dv", lambda e, nicol=nicol: e.reciprocal(
                out=row_sb[0:nicol, :], in_=row_sb[0:nicol, :]),
                mark=f"D:flr:{name}", phase=ph)
            sch.dma("sc", lambda e, name=name, nicol=nicol: e.dma_start(
                out=d_rr[name].rearrange("o (a b) -> (o a) b", b=128)[0:nicol, :],
                in_=row_sb[0:nicol, :]), waits=[f"D:flr:{name}"],
                mark=f"V:rr:{name}", phase=ph)
            ncols = II if name != "d" else H
            bc1 = (lambda e, name=name, ncols=ncols: e.dma_start(
                out=rqbc[:, 0:ncols], in_=d_rr[name].partition_broadcast(128)),
                [f"V:rr:{name}"] + list(bc_extra), f"G:bc:{name}:a", None)
            bc2_ = (lambda e, name=name, ncols=ncols: e.dma_start(
                out=rqbc[:, 0:ncols], in_=d_rr[name].partition_broadcast(128)),
                [f"G:bc:{name}:a"], f"G:bc:{name}", None)
            return [bc1, bc2_]

        pend_sc = []
        for t in range(NT):
            if DEBUG_TAPS and t == 1:
                taps = [
                    ("xbf", xbf[0], BF16, "L:xT:0"),
                    ("qf", qf, BF16, "S:pje:0:7"),
                    ("kf", kf, BF16, "S:pje:0:15"),
                    ("o0", osb, F32, "S:oe:0"),
                    ("r0", rsb, F32, "D:r:0"),
                    ("y0", osb, F32, "D:r2:0"),
                    ("x1w0", x1w[0], F32, "D:x1:0"),
                    ("vsb", vsb[0], BF16, "S:ve:1"),
                    ("ssb", ssb, F32, "D:se:1"),
                    ("mskb", mskb, F32, "D:sm:1"),
                    ("smb", smb, F32, "D:sm:1"),
                    ("wrk", wrk, F32, "S:ex:1"),
                    ("atn", atn, BF16, "D:at:1"),
                    ("atnT", atnT, BF16, "D:ae:1"),
                    ("sml", sml, F32, "D:x1:1"),
                ]
                for nm, tens, dtp, wk in taps:
                    free = 1
                    for d_ in tens.shape[1:]:
                        free *= d_
                    dbg = nc.dram_tensor(f"tap_{nm}", [128, free], dtp,
                                         kind="ExternalOutput").ap()
                    sch.dma("sy", lambda e, tens=tens, dbg=dbg: e.dma_start(
                        out=dbg, in_=tens[:]), waits=[wk], mark=f"TAP:{nm}", phase=ph)

            if t == 6:
                bcl = rs_flip("g", rs_g, ICN, None, f"P:rsg:{ICN - 1}")
                quant_stream("g", v_wgT, HC, II, v_iwg, pre_pl=bcl)
            if t == 9:
                bcl = rs_flip("u", rs_u, ICN, "V:rr:g", f"P:rsu:{ICN - 1}",
                              bc_extra=[f"P:qg:{HC - 1}"])
                quant_stream("u", v_wuT, HC, II, v_iwu, pre_pl=bcl)
            if t == 12:
                bcl = rs_flip("d", rs_d, HC, "V:rr:u", f"P:rsd:{HC - 1}",
                              bc_extra=[f"P:qu:{HC - 1}"])
                quant_stream("d", v_wdT, ICN, H, v_iwd, pre_pl=bcl)
            wq_drain()
            if t + 2 < NT:
                if DEBUG_TAPS and t == 0:
                    pass
                elif DEBUG_TAPS and t == 1:
                    x_load(2)
                    x_load(3)
                else:
                    x_load(t + 2)

            for wi, (wsb, bsb, dst) in enumerate(((wq_sb, bq_sb, qf), (wk_sb, bk_sb, kf))):
                for oc in range(HC):
                    g = wi * 8 + oc
                    bank = pb[g % 2]
                    if g == 0:
                        w0 = [f"L:xT:{t}"]
                    elif g == 1:
                        w0 = [f"S:pje:{t - 1}:15"] if t > 0 else []
                    else:
                        w0 = [f"S:pje:{t}:{g - 2}"]
                    for hc in range(HC):
                        sch.op("pe", lambda e, bank=bank, wsb=wsb, oc=oc, hc=hc, t=t: e.matmul(
                            bank[:, 0:256], wsb[:, hc, oc * 128:(oc + 1) * 128],
                            xbf[t % 2][:, hc, :], start=(hc == 0), stop=(hc == HC - 1)),
                            waits=(w0 if hc == 0 else []),
                            mark=(f"T:pj:{t}:{g}" if hc == HC - 1 else None), phase=ph)
                    wE = [f"T:pj:{t}:{g}"]
                    if g == 0:
                        wE.append("L:bq")
                        if t == 0:
                            wE.append("L:bk")
                    sch.op("sc", lambda e, bank=bank, dst=dst, bsb=bsb, oc=oc, t=t: e.activation(
                        out=dst[:, oc, :], in_=bank[:, 0:256], func=AF.Identity,
                        bias=bsb[:, oc:oc + 1], scale=1.0),
                        waits=wE, mark=f"S:pje:{t}:{g}", phase=ph)
            sch.evt[f"T:pj:{t}"] = sch.evt[f"T:pj:{t}:15"]

            for j in range(2):
                if j == 1:
                    wq_drain()
                w = 2 * t + j
                par = w % 2

                wv0 = [f"S:ve:{w - 1}"] if w >= 1 else []
                for half in range(2):
                    for hc in range(HC):
                        sch.op("pe", lambda e, half=half, hc=hc, t=t, j=j: e.matmul(
                            pb[2 + half][:, 0:512],
                            xbf[t % 2][:, hc, j * 128:(j + 1) * 128],
                            wv_sb[:, hc, half * 512:(half + 1) * 512],
                            start=(hc == 0), stop=False),
                            waits=(wv0 if (half == 0 and hc == 0) else []), phase=ph)
                    sch.op("pe", lambda e, half=half: e.matmul(
                        pb[2 + half][:, 0:512], onesb[0:1, :],
                        bvb[0:1, half * 512:(half + 1) * 512], start=False, stop=True),
                        mark=(f"T:v:{w}" if half == 1 else None), phase=ph)
                sch.op("sc", lambda e, t=t, j=j: e.copy(out=vsb[t % 2][:, j, 0:512],
                                                        in_=pb[2][:, 0:512]),
                       waits=[f"T:v:{w}"], phase=ph)
                sch.op("sc", lambda e, t=t, j=j: e.copy(out=vsb[t % 2][:, j, 512:1024],
                                                        in_=pb[3][:, 0:512]),
                       mark=f"S:ve:{w}", phase=ph)

                ws0 = [f"S:pje:{t}:15"] if j == 0 else [f"D:se:{w - 1}"]
                if j == 0 and w >= 1:
                    sch.push(f"S:pje:{t}:15", f"D:se:{w - 1}")
                for oc in range(HC):
                    sch.op("pe", lambda e, oc=oc, j=j: e.matmul(
                        pb[4][:, 0:128], qf[:, oc, j * 128:(j + 1) * 128],
                        kf[:, oc, j * 128:(j + 1) * 128],
                        start=(oc == 0), stop=(oc == HC - 1)),
                        waits=(ws0 if oc == 0 else []),
                        mark=(f"T:s:{w}" if oc == HC - 1 else None), phase=ph)
                wse = [f"T:s:{w}"]
                if w >= 2:
                    wse.append(f"D:sm:{w - 2}")
                sch.op("dv", lambda e, j=j: e.tensor_scalar(
                    out=ssb[:, j, :], in0=pb[4][:, 0:128], scalar1=0.03125,
                    scalar2=None, op0=ALU.mult),
                    waits=wse, mark=f"D:se:{w}", phase=ph)
                wtk0 = [f"S:ex:{w - 2}"] if w >= 2 else []
                sch.op("dv", lambda e, j=j: e.tensor_copy(
                    out=wrk[:, j, :], in_=ssb[:, j, :]),
                    waits=wtk0, mark=f"D:tk:{w}:0", phase=ph)
                tki = 1
                for it in range(4):
                    sch.op("dv", lambda e, j=j: e.max(out=m8[:], in_=wrk[:, j, :]),
                           waits=[f"D:tk:{w}:{tki - 1}"], mark=f"D:tk:{w}:{tki}", phase=ph)
                    tki += 1
                    if it < 3:
                        sch.op("dv", lambda e, j=j: e.match_replace(
                            out=wrk[:, j, :], in_to_replace=m8[:], in_values=wrk[:, j, :],
                            imm_value=NEG),
                            waits=[f"D:tk:{w}:{tki - 1}"], mark=f"D:tk:{w}:{tki}", phase=ph)
                        tki += 1
                sch.op("dv", lambda e, par=par: e.tensor_copy(out=sl(par, 0), in_=m8[:, 7:8]),
                       waits=[f"D:tk:{w}:{tki - 1}"], mark=f"D:tk:{w}:{tki}", phase=ph)
                sch.op("dv", lambda e, j=j, par=par: e.reduce_max(
                    out=sl(par, 1), in_=ssb[:, j, :], axis=AX.X), mark=f"D:rx:{w}", phase=ph)
                sch.op("dv", lambda e, par=par: e.tensor_scalar(
                    out=sl(par, 2), in0=sl(par, 1), scalar1=BIGB, scalar2=-1.0,
                    op0=ALU.add, op1=ALU.mult), waits=[f"D:rx:{w}"], phase=ph)
                sch.op("dv", lambda e, j=j, par=par: e.tensor_scalar(
                    out=mskb[:, j, :], in0=ssb[:, j, :], scalar1=sl(par, 0), scalar2=None,
                    op0=ALU.is_ge), waits=[f"D:tk:{w}:8"], phase=ph)
                sch.op("dv", lambda e, j=j: e.scalar_tensor_tensor(
                    out=smb[:, j, :], in0=ssb[:, j, :], scalar=BIGB, in1=mskb[:, j, :],
                    op0=ALU.add, op1=ALU.mult), mark=f"D:sm:{w}", phase=ph)
                while pend_sc:
                    pfn, pwaits, pmk = pend_sc.pop(0)
                    sch.dma("sc", pfn, waits=pwaits, mark=pmk, phase=ph)
                sch.op("sc", lambda e, j=j, par=par: e.activation(
                    out=wrk[:, j, :], in_=smb[:, j, :], func=AF.Exp, bias=sl(par, 2),
                    scale=1.0, accum_out=sl(par, 3)),
                    waits=[f"D:sm:{w}"], mark=f"S:ex:{w}", phase=ph)
                sch.op("dv", lambda e, par=par: e.reciprocal(out=sl(par, 4), in_=sl(par, 3)),
                       waits=[f"S:ex:{w}"], mark=f"D:rc0:{w}", phase=ph)
                wat = [f"D:rc0:{w}"] + ([f"T:aT:{w - 2}"] if w >= 2 else [])
                sch.op("dv", lambda e, j=j, par=par: e.tensor_scalar(
                    out=atn[:, j, :], in0=wrk[:, j, :], scalar1=sl(par, 4), scalar2=None,
                    op0=ALU.mult), waits=wat, mark=f"D:at:{w}", phase=ph)

                sch.op("pe", lambda e, j=j: e.matmul(
                    pb[5][:, 0:128], atn[:, j, :], eyeb[:], start=True, stop=True),
                    waits=[f"D:at:{w}"], mark=f"T:aT:{w}", phase=ph)
                wae = [f"T:aT:{w}"]
                if w >= 2:
                    wae.append(f"T:o:{w - 2}")
                if w >= 1:
                    wae.append(f"S:oe:{w - 1}")
                sch.op("dv", lambda e, j=j: e.tensor_copy(out=atnT[:, j, :], in_=pb[5][:, 0:128]),
                       waits=wae, mark=f"D:ae:{w}", phase=ph)
                b5_reader[:] = [f"D:ae:{w}"]

                wo0 = [f"D:ae:{w}"]
                for half in range(2):
                    sch.op("pe", lambda e, half=half, t=t, j=j: e.matmul(
                        pb[6 + half][:, 0:512], atnT[:, j, :],
                        vsb[t % 2][:, j, half * 512:(half + 1) * 512], start=True, stop=True),
                        waits=(wo0 if half == 0 else []),
                        mark=(f"T:o:{w}" if half == 1 else None), phase=ph)
                sch.op("sc", lambda e: e.copy(out=osb[:, 0:512], in_=pb[6][:, 0:512]),
                       waits=[f"T:o:{w}"], phase=ph)
                sch.op("sc", lambda e: e.copy(out=osb[:, 512:1024], in_=pb[7][:, 0:512]),
                       mark=f"S:oe:{w}", phase=ph)

                wlk = [f"S:oe:{w}"]
                if DEBUG_TAPS and w == 0:
                    wlk.append("TAP:o0")
                sch.op("dv", lambda e: e.scalar_tensor_tensor(
                    out=osb[:], in0=osb[:], scalar=0.01, in1=osb[:], op0=ALU.mult, op1=ALU.max),
                    waits=wlk, phase=ph)
                wr0 = [f"L:x:{t}"] if j == 0 else []
                sch.op("dv", lambda e, t=t, j=j, par=par: e.scalar_tensor_tensor(
                    out=rsb[:], in0=osb[:], scalar=1.0, in1=xw[t % 2][:, j, :],
                    op0=ALU.mult, op1=ALU.add, accum_out=sl(par, 5)),
                    waits=wr0, mark=f"D:r:{w}", phase=ph)
                wsq1 = [f"D:r:{w}"]
                if DEBUG_TAPS and w == 0:
                    wsq1.append("TAP:r0")
                sch.op("sc", lambda e, par=par: e.activation(
                    out=osb[:], in_=rsb[:], func=AF.Square, accum_out=sl(par, 6)),
                    waits=wsq1, mark=f"S:sq1:{w}", phase=ph)
                sch.op("dv", lambda e, par=par: e.tensor_scalar(
                    out=sl(par, 7), in0=sl(par, 5), scalar1=1.0 / 1024.0, scalar2=None,
                    op0=ALU.mult), waits=[f"S:sq1:{w}"], mark=f"D:mn1:{w}", phase=ph)
                sch.op("dv", lambda e, par=par: e.tensor_scalar(
                    out=sl(par, 8), in0=sl(par, 7), scalar1=sl(par, 7), scalar2=None,
                    op0=ALU.mult), waits=[f"D:mn1:{w}"], mark=f"D:m21:{w}", phase=ph)
                sch.op("dv", lambda e, par=par: e.tensor_scalar(
                    out=sl(par, 8), in0=sl(par, 6), scalar1=1.0 / 1024.0, scalar2=sl(par, 8),
                    op0=ALU.mult, op1=ALU.subtract),
                    waits=[f"D:m21:{w}"], mark=f"D:vr1:{w}", phase=ph)
                sch.op("sc", lambda e, par=par: e.activation(
                    out=sl(par, 9), in_=sl(par, 8), func=AF.Sqrt, bias=epsv[:]),
                    waits=[f"D:vr1:{w}"], mark=f"S:st1:{w}", phase=ph)
                sch.op("dv", lambda e, par=par: e.reciprocal(out=sl(par, 9), in_=sl(par, 9)),
                       waits=[f"S:st1:{w}"], mark=f"D:rc1:{w}", phase=ph)
                sch.op("dv", lambda e, par=par: e.tensor_scalar(
                    out=osb[:], in0=rsb[:], scalar1=sl(par, 7), scalar2=sl(par, 9),
                    op0=ALU.subtract, op1=ALU.mult), waits=[f"D:rc1:{w}"], phase=ph)
                wg0 = ["L:gam", "L:bet"] if w == 0 else []
                sch.op("dv", lambda e: e.tensor_tensor(
                    out=osb[:], in0=osb[:], in1=gb_bc[:, 0:H], op=ALU.mult), waits=wg0, phase=ph)
                sch.op("dv", lambda e: e.tensor_tensor(
                    out=osb[:], in0=osb[:], in1=gb_bc[:, H:2 * H], op=ALU.add), phase=ph)
                sch.op("dv", lambda e, t=t, j=j, par=par: e.scalar_tensor_tensor(
                    out=rsb[:], in0=osb[:], scalar=1.0, in1=xw[t % 2][:, j, :],
                    op0=ALU.mult, op1=ALU.add, accum_out=sl(par, 5)),
                    mark=f"D:r2:{w}", phase=ph)
                wsq2 = [f"D:r2:{w}"]
                if DEBUG_TAPS and w == 0:
                    wsq2.append("TAP:y0")
                sch.op("sc", lambda e, par=par: e.activation(
                    out=osb[:], in_=rsb[:], func=AF.Square, accum_out=sl(par, 6)),
                    waits=wsq2, mark=f"S:sq2:{w}", phase=ph)
                sch.op("dv", lambda e, par=par: e.tensor_scalar(
                    out=sl(par, 7), in0=sl(par, 5), scalar1=1.0 / 1024.0, scalar2=None,
                    op0=ALU.mult), waits=[f"S:sq2:{w}"], mark=f"D:mn2:{w}", phase=ph)
                sch.op("dv", lambda e, par=par: e.tensor_scalar(
                    out=sl(par, 8), in0=sl(par, 7), scalar1=sl(par, 7), scalar2=None,
                    op0=ALU.mult), waits=[f"D:mn2:{w}"], mark=f"D:m22:{w}", phase=ph)
                sch.op("dv", lambda e, par=par: e.tensor_scalar(
                    out=sl(par, 8), in0=sl(par, 6), scalar1=1.0 / 1024.0, scalar2=sl(par, 8),
                    op0=ALU.mult, op1=ALU.subtract),
                    waits=[f"D:m22:{w}"], mark=f"D:vr2:{w}", phase=ph)
                sch.op("sc", lambda e, par=par: e.activation(
                    out=sl(par, 9), in_=sl(par, 8), func=AF.Sqrt, bias=epsv[:]),
                    waits=[f"D:vr2:{w}"], mark=f"S:st2:{w}", phase=ph)
                sch.op("dv", lambda e, par=par: e.reciprocal(out=sl(par, 9), in_=sl(par, 9)),
                       waits=[f"S:st2:{w}"], mark=f"D:rc2:{w}", phase=ph)
                wx1 = [f"D:rc2:{w}"] + ([f"V:x1:{w - 2}"] if w >= 2 else [])
                sch.op("dv", lambda e, par=par: e.tensor_scalar(
                    out=x1w[par][:], in0=rsb[:], scalar1=sl(par, 7), scalar2=sl(par, 9),
                    op0=ALU.subtract, op1=ALU.mult), waits=wx1, phase=ph)
                sch.op("dv", lambda e, w=w, par=par: e.reduce_max(
                    out=mx_all[:, w:w + 1], in_=x1w[par][:], axis=AX.X,
                    apply_absolute_value=True), mark=f"D:x1:{w}", phase=ph)
                pend_sc.append((
                    lambda e, w=w, par=par: e.dma_start(out=v_x1[:, w, :], in_=x1w[par][:]),
                    [f"D:x1:{w}"], f"V:x1:{w}"))

        while pend_sc:
            pfn, pwaits, pmk = pend_sc.pop(0)
            sch.dma("sc", pfn, waits=pwaits, mark=pmk, phase=ph)
        while wq_state["li"] < len(wq_entries) or wq_state["ri"] < len(wq_entries):
            wq_drain()
        assert sch.cnt["siwg"] == 16 * HC and sch.cnt["siwu"] == 16 * HC
        assert sch.cnt["siwd"] == 16 * ICN
        sch.evt["G:iwg"] = ("siwg", 16 * HC)
        sch.evt["G:iwu"] = ("siwu", 16 * HC)
        sch.evt["G:iwd"] = ("siwd", 16 * ICN)

        sch.op("dv", lambda e: e.reduce_max(out=mvec[:, 0:1], in_=mx_all[:], axis=AX.X),
               waits=[f"D:x1:{NW - 1}"], mark="D:mxv", phase=ph)
        sch.op("pe", lambda e: e.matmul(pb[5][0:1, 0:128], mvec[:, 0:1], eyef[:],
                                        start=True, stop=True),
               waits=["D:mxv"], mark="T:mfl", phase=ph)
        sch.op("dv", lambda e: e.tensor_copy(out=row_sb[0:1, :], in_=pb[5][0:1, 0:128]),
               waits=["T:mfl", "V:rr:d"], mark="D:mfe", phase=ph)
        b5_reader[:] = ["D:mfe"]
        sch.op("dv", lambda e: e.reduce_max(out=srow[0:1, 0:1], in_=row_sb[0:1, 0:128], axis=AX.X),
               waits=["D:srz", "D:mfe"], mark="D:gx", phase=ph)
        sch.dma("sy", lambda e: e.dma_start(out=d_cc["in1"].ap(), in_=srow[0:1, 0:2]),
                waits=["D:gx"], mark="G:cc1i", phase=ph)
        sch.cc(lambda e: e.collective_compute(
            "AllReduce", ALU.max, replica_groups=[list(range(8))],
            ins=[d_cc["in1"].ap().opt()], outs=[d_cc["out1"].ap().opt()]),
            waits=["G:cc1i"], mark="C:1", phase=ph)
        sch.dma("sy", lambda e: e.dma_start(out=srow[0:1, 0:2], in_=d_cc["out1"].ap()),
                waits=["C:1"], mark="G:cc1o:a", phase=ph)
        sch.dma("sy", lambda e: e.dma_start(out=srow[0:1, 0:2], in_=d_cc["out1"].ap()),
                waits=["G:cc1o:a"], mark="G:cc1o", phase=ph)
        sch.op("dv", lambda e: e.tensor_scalar(
            out=srow[0:1, 2:3], in0=srow[0:1, 0:1], scalar1=srow[0:1, 8:9], scalar2=None,
            op0=ALU.mult), waits=["G:cc1o", "S:sigx"], mark="D:tx", phase=ph)
        sch.op("dv", lambda e: e.tensor_scalar(
            out=srow[0:1, 3:4], in0=srow[0:1, 2:3], scalar1=1.0 / 127.0, scalar2=1.0 / 127.0,
            op0=ALU.mult, op1=ALU.mult), waits=["D:tx"], mark="D:rqx", phase=ph)
        sch.op("pe", lambda e: e.matmul(pb[5][:, 0:2], ones32[0:1, :], srow[0:1, 2:4],
                                        start=True, stop=True),
               waits=["D:rqx"], mark="T:k1a", phase=ph)
        sch.op("dv", lambda e: e.tensor_copy(out=bc2[:, 0:2], in_=pb[5][:, 0:2]),
               waits=["T:k1a"], mark="D:bce", phase=ph)
        b5_reader[:] = ["D:bca"]
        sch.op("dv", lambda e: e.reciprocal(out=bc2[:, 0:1], in_=bc2[:, 0:1]),
               waits=["D:bce"], mark="D:rca", phase=ph)
        sch.op("dv", lambda e: e.tensor_scalar(
            out=bc2[:, 0:1], in0=bc2[:, 0:1], scalar1=127.0, scalar2=None, op0=ALU.mult),
            waits=["D:rca"], mark="D:bca", phase=ph)
        sch.op("dv", lambda e: e.tensor_scalar(
            out=sg_all[:], in0=rs_g[:], scalar1=bc2[:, 1:2], scalar2=None, op0=ALU.mult),
            waits=["D:bca", f"P:rsg:{ICN - 1}", f"P:rsu:{ICN - 1}"], phase=ph)
        sch.op("dv", lambda e: e.tensor_scalar(
            out=su_all[:], in0=rs_u[:], scalar1=bc2[:, 1:2], scalar2=None, op0=ALU.mult),
            mark="D:sga", phase=ph)

        sch.evt["T:aend"] = sch.evt["T:k1a"]
        sch.evt["S:aend"] = sch.evt[f"S:st2:{NW - 1}"]
        sch.evt["D:aend"] = sch.evt["D:sga"]
        sch.evt["P:aend"] = sch.evt[f"P:qd:{ICN - 1}"]
        sch.evt["V:x1all"] = ("scq", sch.cnt["scq"])

        block = ent(nc.Block())
        sch.emit(block, sems, 0)
        a.close()

        ph = 1
        b1 = contextlib.ExitStack()
        ent3 = b1.enter_context
        ixf = ent3(nc.sbuf_tensor("ixf", [128, HC, 2048], BF16))
        tqb = [ent3(nc.sbuf_tensor(f"tqb{i}", [128, H], BF16)) for i in range(2)]
        qtmp = ent3(nc.sbuf_tensor("qtmp", [128, H], F32))
        x1r = [ent3(nc.sbuf_tensor(f"x1r{i}", [128, H], F32)) for i in range(2)]
        gch = [ent3(nc.sbuf_tensor(f"gch{i}", [128, HC, 128], BF16)) for i in range(2)]
        uch = [ent3(nc.sbuf_tensor(f"uch{i}", [128, HC, 128], BF16)) for i in range(2)]
        pgs = [ent3(nc.sbuf_tensor(f"pgs{i}", [128, 512], F32)) for i in range(2)]
        pus = [ent3(nc.sbuf_tensor(f"pus{i}", [128, 512], F32)) for i in range(2)]

        FENCE = ["T:aend", "S:aend", "D:aend", "P:aend", "G:iwg", "G:iwu", "V:x1all"]

        if DEBUG_TAPS:
            d_tixf = nc.dram_tensor("tap_ixf", [128, HC * 2048], BF16, kind="ExternalOutput").ap()
            d_tbc2 = nc.dram_tensor("tap_bc2", [128, 4], F32, kind="ExternalOutput").ap()
            d_tsg = nc.dram_tensor("tap_sg", [128, 2 * ICN], F32, kind="ExternalOutput").ap()

        kglob = 0
        pend_p = []
        for half in range(2):
            for wl in range(16):
                w = 16 * half + wl
                wts = list(FENCE) if w == 0 else []
                if w >= 2:
                    wts.append(f"D:tq:{w - 2}")
                sch.dma("sy", lambda e, w=w, wl=wl: e.dma_start(
                    out=x1r[wl % 2][:], in_=v_x1[:, w, :]), waits=wts, mark=f"L:x1:{w}", phase=ph)
                if wl == 15:
                    sch.dma("sy", lambda e, w=w, wl=wl: e.dma_start(
                        out=x1r[wl % 2][:], in_=v_x1[:, w, :]), mark=f"L:x1:{w}", phase=ph)
                wq0 = [f"L:x1:{w}"]
                if wl < 15:
                    wq0.append(f"L:x1:{w + 1}")
                if w == 0:
                    wq0.append("D:sga")
                if w == 16:
                    wq0.append(f"S:sil:0:{ICN - 1}:3")
                    if DEBUG_TAPS:
                        wq0.append("TAP:ixf")
                sch.op("dv", lambda e, wl=wl: e.tensor_scalar(
                    out=qtmp[:], in0=x1r[wl % 2][:], scalar1=bc2[:, 0:1], scalar2=MAGIC,
                    op0=ALU.mult, op1=ALU.add), waits=wq0, phase=ph)
                sch.op("dv", lambda e: e.tensor_scalar(
                    out=qtmp[:], in0=qtmp[:], scalar1=-MAGIC, scalar2=-128.0,
                    op0=ALU.add, op1=ALU.max), phase=ph)
                wq2 = [f"T:tr:{w - 2}"] if w >= 2 else []
                sch.op("dv", lambda e, wl=wl: e.tensor_scalar(
                    out=tqb[wl % 2][:], in0=qtmp[:], scalar1=127.0, scalar2=None, op0=ALU.min),
                    waits=wq2, mark=f"D:tq:{w}", phase=ph)
                bk0, bk1 = (0, 1) if wl % 2 == 0 else (2, 3)
                wt0 = [f"D:tq:{w}"]
                for hc in range(HC):
                    bank = pb[bk0] if hc < 4 else pb[bk1]
                    off = (hc % 4) * 128
                    sch.op("pe", lambda e, bank=bank, off=off, hc=hc, wl=wl: e.matmul(
                        bank[:, off:off + 128], tqb[wl % 2][:, hc * 128:(hc + 1) * 128],
                        eyeb[:], start=(hc % 4 == 0), stop=(hc % 4 == 3)),
                        waits=(wt0 if hc == 0 else []),
                        mark=(f"T:tr:{w}" if hc == HC - 1 else None), phase=ph)
                sch.op("dv", lambda e, bk0=bk0, wl=wl: e.tensor_copy(
                    out=ixf[:, 0:4, wl * 128:(wl + 1) * 128], in_=pb[bk0][:, 0:512]),
                    waits=[f"T:tr:{w}"], phase=ph)
                sch.op("dv", lambda e, bk1=bk1, wl=wl: e.tensor_copy(
                    out=ixf[:, 4:8, wl * 128:(wl + 1) * 128], in_=pb[bk1][:, 0:512]),
                    mark=f"D:ixe:{w}", phase=ph)

            for ic in range(ICN):
                if half == 0 and ic == 0:
                    wl0, wl1 = ["G:iwg"], ["G:iwu"]
                elif ic >= 2:
                    wl0 = [f"T:g:{half}:{ic - 2}"]
                    wl1 = [f"T:u:{half}:{ic - 2}"]
                elif half == 1:
                    wl0 = [f"T:g:0:{ICN - 2 + ic}"]
                    wl1 = [f"T:u:0:{ICN - 2 + ic}"]
                else:
                    wl0, wl1 = [], []
                sch.dma("sy", lambda e, ic=ic: e.dma_start(
                    out=gch[ic % 2][:], in_=v_iwg[:, :, ic * 128:(ic + 1) * 128]),
                    waits=wl0, mark=f"L:g:{half}:{ic}", phase=ph)
                sch.dma("sy", lambda e, ic=ic: e.dma_start(
                    out=uch[ic % 2][:], in_=v_iwu[:, :, ic * 128:(ic + 1) * 128]),
                    waits=wl1, mark=f"L:u:{half}:{ic}", phase=ph)
                if ic == 0:
                    sch.dma("sy", lambda e, ic=ic: e.dma_start(
                        out=uch[ic % 2][:], in_=v_iwu[:, :, ic * 128:(ic + 1) * 128]),
                        mark=f"L:u:{half}:{ic}", phase=ph)
                if DEBUG_TAPS and half == 0 and ic == 0:
                    sch.dma("sy", lambda e: e.dma_start(out=d_tixf, in_=ixf[:]),
                            waits=["D:ixe:15"], mark="TAP:ixf", phase=ph)
                    sch.dma("sy", lambda e: e.dma_start(out=d_tbc2, in_=bc2[:]),
                            waits=["D:sga"], mark="TAP:bc2", phase=ph)
                    sch.dma("sy", lambda e: e.dma_start(out=d_tsg[:, 0:ICN], in_=sg_all[:]),
                            waits=["D:sga"], mark="TAP:sg", phase=ph)
                    sch.dma("sy", lambda e: e.dma_start(out=d_tsg[:, ICN:2 * ICN], in_=su_all[:]),
                            waits=["D:sga"], mark="TAP:su", phase=ph)

                if ic == 0:
                    ixl = f"D:ixe:{16 * half + 15}"
                    wg0 = [ixl]
                    sch.push(ixl, f"L:g:{half}:0", f"L:u:{half}:0")
                else:
                    sil_prev = f"S:sil:{half}:{ic - 1}:3"
                    wg0 = [sil_prev]
                    sch.push(sil_prev, f"L:g:{half}:{ic}")
                first = True
                for hc in range(HC):
                    for tq4 in range(4):
                        sch.op("pe", lambda e, hc=hc, tq4=tq4, ic=ic: e.matmul(
                            pb[tq4][:, 0:512], gch[ic % 2][:, hc, :],
                            ixf[:, hc, tq4 * 512:(tq4 + 1) * 512],
                            start=(hc == 0), stop=(hc == HC - 1)),
                            waits=(wg0 if first else []),
                            mark=(f"T:g:{half}:{ic}" if (hc == HC - 1 and tq4 == 3) else None),
                            phase=ph)
                        first = False
                if ic == 0:
                    wu0 = [f"D:ixe:{16 * half + 15}"]
                else:
                    ue_prev = f"D:ue:{half}:{ic - 1}:3"
                    wu0 = [ue_prev]
                    sch.push(ue_prev, f"L:u:{half}:{ic}")
                first = True
                for hc in range(HC):
                    for tq4 in range(4):
                        sch.op("pe", lambda e, hc=hc, tq4=tq4, ic=ic: e.matmul(
                            pb[4 + tq4][:, 0:512], uch[ic % 2][:, hc, :],
                            ixf[:, hc, tq4 * 512:(tq4 + 1) * 512],
                            start=(hc == 0), stop=(hc == HC - 1)),
                            waits=(wu0 if first else []),
                            mark=(f"T:u:{half}:{ic}" if (hc == HC - 1 and tq4 == 3) else None),
                            phase=ph)
                        first = False

                for tq4 in range(4):
                    kk = kglob
                    kglob += 1
                    while len(pend_p) > 2:
                        pfn, pwaits, pmk = pend_p.pop(0)
                        sch.dma("sc", pfn, waits=pwaits, mark=pmk, phase=ph)
                    ws0 = [f"T:g:{half}:{ic}"]
                    if kk >= 2:
                        ws0.append(f"D:pp:{kk - 2}")
                    if kk == 0:
                        ws0.append("D:sga")
                    sch.op("sc", lambda e, tq4=tq4, ic=ic: e.activation(
                        out=pgs[tq4 % 2][:], in_=pb[tq4][:, 0:512], func=AF.Silu,
                        scale=sg_all[:, ic:ic + 1]),
                        waits=ws0, mark=f"S:sil:{half}:{ic}:{tq4}", phase=ph)
                    wu1 = [f"T:u:{half}:{ic}"]
                    if kk >= 2:
                        wu1.append(f"V:p:{kk - 2}")
                    sch.op("dv", lambda e, tq4=tq4, ic=ic: e.tensor_scalar(
                        out=pus[tq4 % 2][:], in0=pb[4 + tq4][:, 0:512],
                        scalar1=su_all[:, ic:ic + 1], scalar2=None, op0=ALU.mult),
                        waits=wu1, mark=f"D:ue:{half}:{ic}:{tq4}", phase=ph)
                    sch.op("dv", lambda e, tq4=tq4: e.tensor_tensor(
                        out=pus[tq4 % 2][:], in0=pgs[tq4 % 2][:], in1=pus[tq4 % 2][:],
                        op=ALU.mult), waits=[f"S:sil:{half}:{ic}:{tq4}"], phase=ph)
                    sch.op("dv", lambda e, tq4=tq4, kk=kk: e.reduce_max(
                        out=mp_all[:, kk:kk + 1], in_=pus[tq4 % 2][:], axis=AX.X,
                        apply_absolute_value=True), mark=f"D:pp:{kk}", phase=ph)
                    toff = half * 2048 + tq4 * 512
                    pend_p.append((
                        lambda e, tq4=tq4, ic=ic, toff=toff: e.dma_start(
                            out=v_p[:, ic, toff:toff + 512], in_=pus[tq4 % 2][:]),
                        [f"D:pp:{kk}"], f"V:p:{kk}"))

        while pend_p:
            pfn, pwaits, pmk = pend_p.pop(0)
            sch.dma("sc", pfn, waits=pwaits, mark=pmk, phase=ph)

        sch.op("dv", lambda e: e.reduce_max(out=mvec[:, 1:2], in_=mp_all[:], axis=AX.X),
               waits=["D:pp:175"], mark="D:mpv", phase=ph)
        sch.push("D:mpv", f"S:sil:1:{ICN - 1}:3")
        sch.op("pe", lambda e: e.matmul(pb[0][0:1, 0:128], mvec[:, 1:2], eyef[:],
                                        start=True, stop=True),
               waits=["D:mpv"], mark="T:mfl2", phase=ph)
        sch.op("dv", lambda e: e.tensor_copy(out=row_sb[0:1, :], in_=pb[0][0:1, 0:128]),
               waits=["T:mfl2"], mark="D:mfe2", phase=ph)
        sch.op("dv", lambda e: e.reduce_max(out=srow[0:1, 4:5], in_=row_sb[0:1, 0:128], axis=AX.X),
               waits=["D:mfe2"], mark="D:gp", phase=ph)
        sch.dma("sy", lambda e: e.dma_start(out=d_cc["in2"].ap(), in_=srow[0:1, 4:6]),
                waits=["D:gp"], mark="G:cc2i", phase=ph)
        sch.cc(lambda e: e.collective_compute(
            "AllReduce", ALU.max, replica_groups=[list(range(8))],
            ins=[d_cc["in2"].ap().opt()], outs=[d_cc["out2"].ap().opt()]),
            waits=["G:cc2i"], mark="C:2", phase=ph)
        sch.dma("sy", lambda e: e.dma_start(out=srow[0:1, 4:6], in_=d_cc["out2"].ap()),
                waits=["C:2"], mark="G:cc2o:a", phase=ph)
        sch.dma("sy", lambda e: e.dma_start(out=srow[0:1, 4:6], in_=d_cc["out2"].ap()),
                waits=["G:cc2o:a"], mark="G:cc2o", phase=ph)
        sch.op("dv", lambda e: e.tensor_scalar(
            out=srow[0:1, 6:7], in0=srow[0:1, 4:5], scalar1=srow[0:1, 9:10], scalar2=None,
            op0=ALU.mult), waits=["G:cc2o", "S:sigp"], mark="D:tp", phase=ph)
        sch.op("dv", lambda e: e.tensor_scalar(
            out=srow[0:1, 7:8], in0=srow[0:1, 6:7], scalar1=1.0 / 127.0, scalar2=1.0 / 127.0,
            op0=ALU.mult, op1=ALU.mult), waits=["D:tp"], mark="D:rqp", phase=ph)
        sch.op("pe", lambda e: e.matmul(pb[1][:, 0:2], ones32[0:1, :], srow[0:1, 6:8],
                                        start=True, stop=True),
               waits=["D:rqp"], mark="T:k1b", phase=ph)
        sch.op("dv", lambda e: e.tensor_copy(out=bc2[:, 2:4], in_=pb[1][:, 0:2]),
               waits=["T:k1b"], mark="D:bce2", phase=ph)
        sch.op("dv", lambda e: e.reciprocal(out=bc2[:, 2:3], in_=bc2[:, 2:3]),
               waits=["D:bce2"], mark="D:rcb", phase=ph)
        sch.op("dv", lambda e: e.tensor_scalar(
            out=bc2[:, 2:3], in0=bc2[:, 2:3], scalar1=127.0, scalar2=None, op0=ALU.mult),
            waits=["D:rcb"], mark="D:bcb", phase=ph)
        sch.op("pe", lambda e: e.matmul(pb[2][0:HC, 0:128], rs_d[:, 0:HC], eyef[:],
                                        start=True, stop=True),
               mark="T:fld2", phase=ph)
        sch.op("dv", lambda e: e.tensor_copy(out=row_sb[0:HC, :], in_=pb[2][0:HC, 0:128]),
               waits=["T:fld2"], mark="D:fld2e", phase=ph)
        sch.dma("sc", lambda e: e.dma_start(
            out=d_rr["d"].rearrange("o (a b) -> (o a) b", b=128)[0:HC, :],
            in_=row_sb[0:HC, :]), waits=["D:fld2e", "G:iwd"], mark="V:rrd2", phase=ph)
        sch.dma("sy", lambda e: e.dma_start(out=sdbc[:], in_=d_rr["d"].partition_broadcast(128)),
                waits=["V:rrd2"], mark="G:bcd2:a", phase=ph)
        sch.dma("sy", lambda e: e.dma_start(out=sdbc[:], in_=d_rr["d"].partition_broadcast(128)),
                waits=["G:bcd2:a"], mark="G:bcd2", phase=ph)
        sch.op("dv", lambda e: e.tensor_scalar(
            out=sdbc[:], in0=sdbc[:], scalar1=bc2[:, 3:4], scalar2=None, op0=ALU.mult),
            waits=["G:bcd2", "D:bcb"], mark="D:sdbc", phase=ph)

        sch.evt["T:p1end"] = sch.evt["T:fld2"]
        sch.evt["S:p1end"] = sch.evt[f"S:sil:1:{ICN - 1}:3"]
        sch.evt["D:p1end"] = sch.evt["D:sdbc"]
        sch.evt["V:pall"] = ("scq", sch.cnt["scq"])

        sch.emit(block, sems, 1)
        b1.close()

        ph = 2
        b2 = contextlib.ExitStack()
        ent4 = b2.enter_context
        iwd_sb = ent4(nc.sbuf_tensor("iwd_sb", [128, ICN, H], BF16))
        ih = [ent4(nc.sbuf_tensor(f"ih{i}", [128, ICN, 128], BF16)) for i in range(2)]
        pblk = [ent4(nc.sbuf_tensor(f"pblk{i}", [128, ICN, 128], F32)) for i in range(2)]
        x1r2 = [ent4(nc.sbuf_tensor(f"x1r2{i}", [128, H], F32)) for i in range(2)]
        dsb = [ent4(nc.sbuf_tensor(f"dsb{i}", [128, H], F32)) for i in range(2)]
        r3b = ent4(nc.sbuf_tensor("r3b", [128, H], F32))
        qt2 = ent4(nc.sbuf_tensor("qt2", [128, ICN * 128], F32))
        outb = [ent4(nc.sbuf_tensor(f"outb{i}", [128, H], F32)) for i in range(2)]

        FENCE2 = ["T:p1end", "S:p1end", "D:p1end"]
        pend_o = []
        if DEBUG_TAPS or DEBUG_TAPS2:
            d_tih = nc.dram_tensor("tap_ih", [128, ICN * 128], BF16, kind="ExternalOutput").ap()
            d_tdsb = nc.dram_tensor("tap_dsb", [128, H], F32, kind="ExternalOutput").ap()
            d_tdsc = nc.dram_tensor("tap_dsc", [128, H], F32, kind="ExternalOutput").ap()
            d_tr3 = nc.dram_tensor("tap_r3", [128, H], F32, kind="ExternalOutput").ap()
            d_tsd = nc.dram_tensor("tap_sd", [128, H], F32, kind="ExternalOutput").ap()
        sch.dma("sy", lambda e: e.dma_start(out=iwd_sb[:], in_=v_iwd),
                waits=FENCE2 + ["G:iwd"], mark="L:iwd", phase=ph)

        def emit_quant2(b):
            pr = b % 2
            wq0 = [f"L:p:{b}"]
            if b <= 1:
                wq0.append("L:xb:0")
            elif b + 1 < NW:
                wq0.append(f"L:p:{b + 1}")
            if b == 0:
                wq0.append("D:bcb")
            if (DEBUG_TAPS or DEBUG_TAPS2) and b == 2:
                wq0 += ["TAP:ih", "TAP:dsb", "TAP:dsc", "TAP:r3", "TAP:sd"]
            sch.op("dv", lambda e, pr=pr: e.tensor_scalar(
                out=qt2[:], in0=pblk[pr][:], scalar1=bc2[:, 2:3], scalar2=MAGIC,
                op0=ALU.mult, op1=ALU.add), waits=wq0, phase=ph)
            sch.op("dv", lambda e: e.tensor_scalar(
                out=qt2[:], in0=qt2[:], scalar1=-MAGIC, scalar2=-128.0,
                op0=ALU.add, op1=ALU.max), phase=ph)
            wih = [f"T:d:{b - 2}"] if b >= 2 else []
            sch.op("dv", lambda e, pr=pr: e.tensor_scalar(
                out=ih[pr][:], in0=qt2[:], scalar1=127.0, scalar2=None, op0=ALU.min),
                waits=wih, mark=f"D:ih:{b}", phase=ph)

        def emit_pload(b):
            wpl = [f"D:ih:{b - 2}"] if b >= 2 else list(FENCE2) + ["V:pall"]
            sch.dma("sy", lambda e, b=b: e.dma_start(
                out=pblk[b % 2][:], in_=v_p[:, :, b * 128:(b + 1) * 128]),
                waits=wpl, mark=f"L:p:{b}", phase=ph)

        for blk in range(NW):
            par = blk % 2
            if blk == 0:
                emit_pload(0)
                emit_pload(1)
            wxl = [f"D:r3:{blk - 2}"] if blk >= 2 else list(FENCE2) + ["V:pall"]
            sch.dma("sy", lambda e, blk=blk, par=par: e.dma_start(
                out=x1r2[par][:], in_=v_x1[:, blk, :]), waits=wxl, mark=f"L:xb:{blk}", phase=ph)
            if blk == NW - 1:
                sch.dma("sy", lambda e, blk=blk, par=par: e.dma_start(
                    out=x1r2[par][:], in_=v_x1[:, blk, :]), mark=f"L:xb:{blk}", phase=ph)
            if blk + 2 < NW:
                emit_pload(blk + 2)
            if blk == 0:
                emit_quant2(0)
                emit_quant2(1)

            wd0 = [f"D:ih:{blk}"]
            sch.push(f"D:ih:{blk}", "L:iwd")
            if blk >= 2:
                sch.push(f"D:ih:{blk}", f"S:de:{blk - 2}")
            first = True
            for half2 in range(2):
                for ic in range(ICN):
                    sch.op("pe", lambda e, half2=half2, ic=ic, par=par: e.matmul(
                        pb[2 * par + half2][:, 0:512], ih[par][:, ic, :],
                        iwd_sb[:, ic, half2 * 512:(half2 + 1) * 512],
                        start=(ic == 0), stop=(ic == ICN - 1)),
                        waits=(wd0 if first else []),
                        mark=(f"T:d:{blk}" if (half2 == 1 and ic == ICN - 1) else None),
                        phase=ph)
                    first = False
            if blk + 2 < NW:
                emit_quant2(blk + 2)
            while len(pend_o) > 1:
                pfn, pwaits, pmk = pend_o.pop(0)
                sch.dma("sc", pfn, waits=pwaits, mark=pmk, phase=ph)
            wde = [f"T:d:{blk}"]
            if blk >= 2:
                wde.append(f"D:r3:{blk - 2}")
            sch.op("sc", lambda e, par=par: e.copy(out=dsb[par][:, 0:512],
                                                   in_=pb[2 * par][:, 0:512]),
                   waits=wde, phase=ph)
            sch.op("sc", lambda e, par=par: e.copy(out=dsb[par][:, 512:1024],
                                                   in_=pb[2 * par + 1][:, 0:512]),
                   mark=f"S:de:{blk}", phase=ph)
            wsc = [f"S:de:{blk}"]
            if blk == 0:
                wsc.append("D:sdbc")
                if DEBUG_TAPS or DEBUG_TAPS2:
                    wsc.append("TAP:dsb")
            sch.op("dv", lambda e, par=par: e.tensor_tensor(
                out=dsb[par][:], in0=dsb[par][:], in1=sdbc[:], op=ALU.mult),
                waits=wsc, phase=ph)
            wr3 = [f"L:xb:{blk}"]
            if blk < NW - 1:
                wr3.append(f"L:p:{blk + 1}")
            sch.op("dv", lambda e, par=par: e.scalar_tensor_tensor(
                out=r3b[:], in0=dsb[par][:], scalar=1.0, in1=x1r2[par][:],
                op0=ALU.mult, op1=ALU.add, accum_out=sl(par, 5)),
                waits=wr3, mark=f"D:r3:{blk}", phase=ph)
            if (DEBUG_TAPS or DEBUG_TAPS2) and blk == 0:
                sch.dma("sy", lambda e: e.dma_start(out=d_tih, in_=ih[0][:]),
                        waits=["D:ih:0"], mark="TAP:ih", phase=ph)
                sch.dma("sy", lambda e: e.dma_start(out=d_tdsb, in_=dsb[0][:]),
                        waits=["S:de:0"], mark="TAP:dsb", phase=ph)
                sch.dma("sy", lambda e: e.dma_start(out=d_tdsc, in_=dsb[0][:]),
                        waits=["D:r3:0"], mark="TAP:dsc", phase=ph)
                sch.dma("sy", lambda e: e.dma_start(out=d_tr3, in_=r3b[:]),
                        waits=["D:r3:0"], mark="TAP:r3", phase=ph)
                sch.dma("sy", lambda e: e.dma_start(out=d_tsd, in_=sdbc[:]),
                        waits=["D:sdbc"], mark="TAP:sd", phase=ph)
            wsq3 = [f"D:r3:{blk}"]
            if (DEBUG_TAPS or DEBUG_TAPS2) and blk == 0:
                wsq3.append("TAP:dsc")
            sch.op("sc", lambda e, par=par: e.activation(
                out=dsb[par][:], in_=r3b[:], func=AF.Square, accum_out=sl(par, 6)),
                waits=wsq3, mark=f"S:sq3:{blk}", phase=ph)
            sch.op("dv", lambda e, par=par: e.tensor_scalar(
                out=sl(par, 7), in0=sl(par, 5), scalar1=1.0 / 1024.0, scalar2=None,
                op0=ALU.mult), waits=[f"S:sq3:{blk}"], mark=f"D:mn3:{blk}", phase=ph)
            sch.op("dv", lambda e, par=par: e.tensor_scalar(
                out=sl(par, 8), in0=sl(par, 7), scalar1=sl(par, 7), scalar2=None,
                op0=ALU.mult), waits=[f"D:mn3:{blk}"], mark=f"D:m23:{blk}", phase=ph)
            sch.op("dv", lambda e, par=par: e.tensor_scalar(
                out=sl(par, 8), in0=sl(par, 6), scalar1=1.0 / 1024.0, scalar2=sl(par, 8),
                op0=ALU.mult, op1=ALU.subtract),
                waits=[f"D:m23:{blk}"], mark=f"D:vr3:{blk}", phase=ph)
            sch.op("sc", lambda e, par=par: e.activation(
                out=sl(par, 9), in_=sl(par, 8), func=AF.Sqrt, bias=epsv[:]),
                waits=[f"D:vr3:{blk}"], mark=f"S:st3:{blk}", phase=ph)
            sch.op("dv", lambda e, par=par: e.reciprocal(out=sl(par, 9), in_=sl(par, 9)),
                   waits=[f"S:st3:{blk}"], mark=f"D:rc3:{blk}", phase=ph)
            wob = [f"D:rc3:{blk}"] + ([f"V:o:{blk - 2}"] if blk >= 2 else [])
            sch.op("dv", lambda e, par=par: e.tensor_scalar(
                out=outb[par][:], in0=r3b[:], scalar1=sl(par, 7), scalar2=sl(par, 9),
                op0=ALU.subtract, op1=ALU.mult), waits=wob, mark=f"D:ob:{blk}", phase=ph)
            pend_o.append((
                lambda e, blk=blk, par=par: e.dma_start(out=v_out[:, blk, :], in_=outb[par][:]),
                [f"D:ob:{blk}"], f"V:o:{blk}"))

        while pend_o:
            pfn, pwaits, pmk = pend_o.pop(0)
            sch.dma("sc", pfn, waits=pwaits, mark=pmk, phase=ph)

        if DEBUG_DUMPS:
            d_dbg1 = nc.dram_tensor("dbg_x1", [S, H], F32, kind="ExternalOutput").ap()
            d_dbg2 = nc.dram_tensor("dbg_p", [II, S], F32, kind="ExternalOutput").ap()
            d_dbg3 = nc.dram_tensor("dbg_iwg", [H, II], BF16, kind="ExternalOutput").ap()
            d_dbg4 = nc.dram_tensor("dbg_iwd", [II, H], BF16, kind="ExternalOutput").ap()
            d_dbg5 = nc.dram_tensor("dbg_sml", [128, 24], F32, kind="ExternalOutput").ap()
            d_dbg6 = nc.dram_tensor("dbg_srow", [1, 16], F32, kind="ExternalOutput").ap()
            sch.dma("sy", lambda e: e.dma_start(out=d_dbg1, in_=d_x1),
                    waits=[f"V:o:{NW - 1}"], mark="DBG1", phase=ph)
            sch.dma("sy", lambda e: e.dma_start(out=d_dbg2, in_=d_p), mark="DBG2", phase=ph)
            sch.dma("sy", lambda e: e.dma_start(out=d_dbg3, in_=d_iwg), mark="DBG3", phase=ph)
            sch.dma("sy", lambda e: e.dma_start(out=d_dbg4, in_=d_iwd), mark="DBG4", phase=ph)
            sch.dma("sy", lambda e: e.dma_start(out=d_dbg5, in_=sml[:]), mark="DBG5", phase=ph)
            sch.dma("sy", lambda e: e.dma_start(out=d_dbg6, in_=srow[:]), mark="DBG6", phase=ph)
            d_dbg7 = nc.dram_tensor("dbg_clip", [1, 2], F32, kind="ExternalOutput").ap()
            d_dbg8 = nc.dram_tensor("dbg_bc2", [128, 4], F32, kind="ExternalOutput").ap()
            sch.dma("sy", lambda e: e.dma_start(out=d_dbg7, in_=clip_sb[:]), mark="DBG7", phase=ph)
            sch.dma("sy", lambda e: e.dma_start(out=d_dbg8, in_=bc2[:]), mark="DBG8", phase=ph)
            sch.wait_only("sy", ["DBG8", "DBG7", "DBG6", "DBG5", "DBG4", "DBG3", "DBG2", "DBG1"], phase=ph)

        sch.wait_only("sy", [f"V:o:{NW - 1}"], phase=ph)

        sch.emit(block, sems, 2)
        b2.close()

    return nc


_NC_CACHE = []


def kernel(L0_attn_x, L0_attn_Wq, L0_attn_bq, L0_attn_Wk, L0_attn_bk, L0_attn_Wv,
           L0_attn_bv, L0_attn_gamma, L0_attn_beta, L0_ffn_clip_x, L0_ffn_clip_down,
           L0_ffn_w_gate, L0_ffn_w_up, L0_ffn_w_down):
    x = np.ascontiguousarray(np.asarray(L0_attn_x, dtype=np.float32))
    Wq = np.asarray(L0_attn_Wq, np.float32)
    Wk = np.asarray(L0_attn_Wk, np.float32)
    Wv = np.asarray(L0_attn_Wv, np.float32)
    wg = np.ascontiguousarray(np.asarray(L0_ffn_w_gate, np.float32))
    wu = np.ascontiguousarray(np.asarray(L0_ffn_w_up, np.float32))
    wd = np.ascontiguousarray(np.asarray(L0_ffn_w_down, np.float32))

    common = {
        "wqT": np.ascontiguousarray(Wq.T),
        "wkT": np.ascontiguousarray(Wk.T),
        "wvT": np.ascontiguousarray(Wv.T),
        "bq_pc": np.ascontiguousarray(np.asarray(L0_attn_bq, np.float32).reshape(HC, 128).T),
        "bk_pc": np.ascontiguousarray(np.asarray(L0_attn_bk, np.float32).reshape(HC, 128).T),
        "bv_row": np.ascontiguousarray(np.asarray(L0_attn_bv, np.float32).reshape(1, H)),
        "gam_row": np.ascontiguousarray(np.asarray(L0_attn_gamma, np.float32).reshape(1, H)),
        "bet_row": np.ascontiguousarray(np.asarray(L0_attn_beta, np.float32).reshape(1, H)),
        "clip2": np.ascontiguousarray(
            np.concatenate([np.asarray(L0_ffn_clip_x, np.float32).reshape(1),
                            np.asarray(L0_ffn_clip_down, np.float32).reshape(1)]).reshape(1, 2)),
        "w_gate": wg, "w_gateT": np.ascontiguousarray(wg.T),
        "w_up": wu, "w_upT": np.ascontiguousarray(wu.T),
        "w_down": wd, "w_downT": np.ascontiguousarray(wd.T),
    }

    if not _NC_CACHE:
        _NC_CACHE.append(build_program())
    nc = _NC_CACHE[0]

    in_maps = []
    for c in range(8):
        m = dict(common)
        m["x_tm"] = np.ascontiguousarray(x[c])
        m["x_fm"] = np.ascontiguousarray(x[c].T)
        in_maps.append(m)

    res = run_bass_kernel_spmd(nc, in_maps, core_ids=list(range(8)))
    out = np.stack([np.asarray(r["out"], np.float32) for r in res.results], axis=0)
    return out
